# version 04
# baseline (speedup 1.0000x reference)
import numpy as np
import ml_dtypes
from contextlib import ExitStack

import concourse.bass as bass
import concourse.mybir as mybir
from concourse.bass_utils import run_bass_kernel_spmd

F32 = mybir.dt.float32
BF16 = mybir.dt.bfloat16
AF = mybir.ActivationFunctionType
ALU = mybir.AluOpType
AX = mybir.AxisListType

MAGIC = float(np.float32(3 * 2**22))
N_CORES = 8
T = 1024
NTT = 8
D = 4096
NDC = 32
F = 16384
NFC = 128
NFSB = 8
FC_PER_FSB = 16
NDG = 16
S_ATTN = 1.0 / float(np.sqrt(np.float32(128.0)))
EPS = 1e-5
NW = float(F * D)
ESUB = 8
N_ESUB = 128 // ESUB
SCAN_CH = 2048
N_SCAN = (NFC * 128 * D) // (128 * SCAN_CH)


class Sems:
    def __init__(self):
        self.cnt = {}
        self.events = {}
        self.handles = {}
        self.dry = True
        self.trace = None
        self.cur = None

    def sem(self, name):
        self.cnt[name] = 0
        return name

    def inc(self, name, instr, k=1):
        self.cnt[name] += k
        if self.dry and self.trace is not None:
            self.trace[self.cur].append(("inc", name, k))
        if not self.dry:
            instr.then_inc(self.handles[name], k)

    def mark(self, evt, name):
        if self.dry:
            assert evt not in self.events, evt
            self.events[evt] = (name, self.cnt[name])
        else:
            assert self.events[evt] == (name, self.cnt[name]), (
                evt, self.events[evt], (name, self.cnt[name]))

    def inc_mark(self, name, instr, evt, k=1):
        self.inc(name, instr, k)
        self.mark(evt, name)

    def alias(self, evt, name):
        self.mark(evt, name)

    def wait(self, eng, evt):
        if self.dry:
            if self.trace is not None:
                self.trace[self.cur].append(("waitevt", evt))
            return
        name, v = self.events[evt]
        if v > 0:
            eng.wait_ge(self.handles[name], v)

    def wait_val(self, eng, name, v):
        if self.dry and self.trace is not None:
            self.trace[self.cur].append(("wait", name, v))
        if not self.dry:
            eng.wait_ge(self.handles[name], v)


class _DryInstr:
    def then_inc(self, *a, **kw):
        return self


class DryEng:
    def __getattr__(self, k):
        def f(*a, **kw):
            return _DryInstr()
        return f


def build_nc(stop_after=None, dbg=False):
    nc = bass.Bass("TRN2", target_bir_lowering=False)
    if dbg:
        dbg_qkv = nc.dram_tensor("dbg_qkv", [128, NTT * 384], F32,
                                 kind="ExternalOutput").ap()
        dbg_misc = nc.dram_tensor("dbg_misc", [128, 64], F32,
                                  kind="ExternalOutput").ap()
        dbg_w3 = nc.dram_tensor("dbg_w3", [128, 6144], F32,
                                kind="ExternalOutput").ap()
        dbg_ln1 = nc.dram_tensor("dbg_ln1", [128, 2048], F32,
                                 kind="ExternalOutput").ap()
        dbg_fm = nc.dram_tensor("dbg_fm", [128, 2048], F32,
                                kind="ExternalOutput").ap()
        dbg_cols = nc.dram_tensor("dbg_cols", [128, 64], F32,
                                  kind="ExternalOutput").ap()
        dbg_g = nc.dram_tensor("dbg_g", [3, 128, T], F32,
                               kind="ExternalOutput").ap()
        dbg_m = nc.dram_tensor("dbg_m", [128, 40], F32,
                               kind="ExternalOutput").ap()
        dbg_acc = nc.dram_tensor("dbg_acc", [4, 128, T], F32,
                                 kind="ExternalOutput").ap()

    x_d = nc.dram_tensor("x", [NTT, 128, D], F32, kind="ExternalInput").ap()
    wq_d = nc.dram_tensor("wq", [32, D, 128], F32, kind="ExternalInput").ap()
    wk_d = nc.dram_tensor("wk", [NDC, 128, 128], F32, kind="ExternalInput").ap()
    wv_d = nc.dram_tensor("wv", [NDC, 128, 128], F32, kind="ExternalInput").ap()
    bq_d = nc.dram_tensor("bq", [128, 32], F32, kind="ExternalInput").ap()
    bk_d = nc.dram_tensor("bk", [128, 1], F32, kind="ExternalInput").ap()
    bv_d = nc.dram_tensor("bv", [128, 1], F32, kind="ExternalInput").ap()
    wup_d = nc.dram_tensor("wup", [NFC, 128, D], F32, kind="ExternalInput").ap()
    wdn_d = nc.dram_tensor("wdn", [NFSB, NDG, 128, D], F32, kind="ExternalInput").ap()
    g_d = nc.dram_tensor("g_scr", [NFC, 128, T], F32, kind="Internal").ap()
    x2_d = nc.dram_tensor("x2_scr", [NTT, 128, D], F32, kind="Internal").ap()
    out_d = nc.dram_tensor("out", [NTT, 128, D], F32, kind="ExternalOutput").ap()

    sems = Sems()
    for s in ("sx", "swq", "saux", "supa", "supc", "sdnc", "sdnd", "sgst",
              "sgld", "sx2s", "sx2l", "sout", "sv", "ss", "st", "sg"):
        sems.sem(s)

    with ExitStack() as top:
        E = top.enter_context

        cols = E(nc.sbuf_tensor("cols", [128, 64], F32))
        amax_x = E(nc.sbuf_tensor("amax_x", [128, NTT], F32))
        sxq_c = E(nc.sbuf_tensor("sxq_c", [128, NTT], F32))
        m1_c = E(nc.sbuf_tensor("m1_c", [128, NTT], F32))
        m2_c = E(nc.sbuf_tensor("m2_c", [128, NTT], F32))
        ones_f = E(nc.sbuf_tensor("ones_f", [1, 128], F32))
        ones_col = E(nc.sbuf_tensor("ones_col", [128, 1], F32))
        sc_row = E(nc.sbuf_tensor("sc_row", [1, 4], F32))
        sup_inv = E(nc.sbuf_tensor("sup_inv", [128, 1], F32))
        sdn_inv = E(nc.sbuf_tensor("sdn_inv", [128, 1], F32))
        sup_b = E(nc.sbuf_tensor("sup_b", [128, 1], F32))
        sdn_b = E(nc.sbuf_tensor("sdn_b", [128, 1], F32))
        id_bf = E(nc.sbuf_tensor("id_bf", [128, 128], BF16))
        id_f32 = E(nc.sbuf_tensor("id_f32", [128, 128], F32))
        m1_bc = E(nc.sbuf_tensor("m1_bc", [128, T], F32))
        m1_row = E(nc.sbuf_tensor("m1_row", [1, T], F32))
        brow3 = E(nc.sbuf_tensor("brow3", [1, 384], F32))
        bcols = E(nc.sbuf_tensor("bcols", [128, 4], F32))
        ddrow = E(nc.sbuf_tensor("ddrow", [1, 512], F32))
        nsup2 = E(nc.sbuf_tensor("nsup2", [128, 1], F32))
        nsdn2 = E(nc.sbuf_tensor("nsdn2", [128, 1], F32))

        W = 49664
        arena = E(nc.sbuf_tensor("arena", [128, W], F32))

        def A(off, n):
            return arena[:, off:off + n]

        def AB(off, n):
            return arena[:, off:off + n].bitcast(BF16)

        xt = [A(0, D), A(D, D)]
        ln1t = AB(8192, 2048)
        xqt = AB(10240, 2048)
        ln1fm = AB(12288, 2048).rearrange("p (c t) -> p c t", t=128)
        ln2h = A(14336, 2048)
        ews = A(16384, ESUB * 128)
        up_cols = A(17408, 1024)
        dn_cols = A(22528, 1024)
        ddwkA = A(24576, 3072)
        ddwkC = A(18432, 3072)
        w3 = AB(18432, 6144).rearrange("p (c n) -> p c n", n=384)
        wqsum = A(24576, NDC * 128).rearrange("p (c m) -> p c m", m=128)
        scst = [A(24576, SCAN_CH), A(24576 + SCAN_CH, SCAN_CH)]
        qkvT = A(28672, NTT * 384).rearrange("p (a n) -> p a n", n=384)
        bias_bc = A(31744, 384)
        attno = A(32128, 256)
        XQ_OFF = W - 16384
        xq_fm = AB(XQ_OFF, 16384).rearrange("p (c t) -> p c t", t=T)

        upst = [A(0, D), A(D, D)]
        ternu = [AB(8192, 2048), AB(10240, 2048)]
        reluws = A(12288, T)
        grng = [A(13312 + i * T, T) for i in range(4)]
        mrun = A(17408, T)
        scstd = [A(18432, SCAN_CH), A(18432 + SCAN_CH, SCAN_CH)]

        acc = A(0, NDC * T).rearrange("p (c t) -> p c t", t=T)
        hq = AB(32768, 8192).rearrange("p (c t) -> p c t", t=T)
        gld = [A(40960, T), A(40960 + T, T)]
        dnst = [A(43008, 2048), A(43008 + 2048, 2048)]
        ternd = [AB(47104, 1024), AB(48128, 1024)]

        x2t = [A(32768, D), A(32768 + D, D)]
        outt = [A(40960, D), A(40960 + D, D)]

        ps = E(nc.psum_tensor([128, 4096], F32))
        ps_tr = [ps[:, 0:256].bitcast(BF16).rearrange("p (c t) -> p c t", t=128),
                 ps[:, 512:768].bitcast(BF16).rearrange("p (c t) -> p c t", t=128)]
        ps_qkv = [ps[:, 1024:1408], ps[:, 1536:1920]]
        ps_misc = [ps[:, 2048:2560], ps[:, 2560:3072]]
        ps_up = [ps[:, 0:1024], ps[:, 1024:2048]]
        ps_m1 = [ps[:, 2048:2560], ps[:, 2560:3072]]
        ps_dn = [ps[:, 0:1024], ps[:, 1024:2048], ps[:, 2048:3072], ps[:, 3072:4096]]
        ps_tl = [ps[:, 0:512].rearrange("p (c t) -> p c t", t=128),
                 ps[:, 512:1024].rearrange("p (c t) -> p c t", t=128)]

        def dd_level(v, cur_hi, cur_lo, out_hi, out_lo, tmp1, tmp2, w):
            w2 = w // 2
            re = cur_hi.rearrange("p (n two) -> p n two", two=2)
            a, b = re[:, :, 0:1], re[:, :, 1:2]
            oh = out_hi.unsqueeze(2)
            t1 = tmp1.unsqueeze(2)
            t2 = tmp2.unsqueeze(2)
            v.tensor_tensor(out=oh, in0=a, in1=b, op=ALU.add)
            v.drain()
            v.tensor_tensor(out=t1, in0=oh, in1=a, op=ALU.subtract)
            v.drain()
            v.tensor_tensor(out=t2, in0=oh, in1=t1, op=ALU.subtract)
            v.drain()
            v.tensor_tensor(out=t2, in0=a, in1=t2, op=ALU.subtract)
            v.tensor_tensor(out=t1, in0=b, in1=t1, op=ALU.subtract)
            v.drain()
            v.tensor_tensor(out=t1, in0=t1, in1=t2, op=ALU.add)
            if cur_lo is None:
                v.drain()
                v.tensor_copy(out=out_lo, in_=tmp1)
            else:
                rl = cur_lo.rearrange("p (n two) -> p n two", two=2)
                v.tensor_tensor(out=out_lo.unsqueeze(2), in0=rl[:, :, 0:1],
                                in1=rl[:, :, 1:2], op=ALU.add)
                v.drain()
                v.tensor_tensor(out=out_lo, in0=out_lo, in1=tmp1, op=ALU.add)
            v.drain()

        def dd_tree_cols(v, src_cols, wk):
            hiA, loA = wk[:, 0:512], wk[:, 512:1024]
            hiB, loB = wk[:, 1024:1536], wk[:, 1536:2048]
            tmp1, tmp2 = wk[:, 2048:2560], wk[:, 2560:3072]
            w = 1024
            cur_hi, cur_lo = src_cols, None
            flip = 0
            while w > 1:
                w2 = w // 2
                oh, ol = (hiA, loA) if flip == 0 else (hiB, loB)
                dd_level(v, cur_hi, cur_lo, oh[:, 0:w2], ol[:, 0:w2],
                         tmp1[:, 0:w2], tmp2[:, 0:w2], w)
                cur_hi, cur_lo = oh[:, 0:w2], ol[:, 0:w2]
                w = w2
                flip ^= 1
            return cur_hi, cur_lo

        def dd_tree_row(v, out_scalar):
            cur_hi, cur_lo = ddrow[0:1, 0:128], ddrow[0:1, 128:256]
            w = 128
            flip = 0
            while w > 1:
                w2 = w // 2
                if flip == 0:
                    oh, ol = ddrow[0:1, 256:256 + w2], ddrow[0:1, 320:320 + w2]
                else:
                    oh, ol = ddrow[0:1, 0:w2], ddrow[0:1, 128:128 + w2]
                dd_level(v, cur_hi, cur_lo, oh, ol,
                         ddrow[0:1, 384:384 + w2], ddrow[0:1, 448:448 + w2], w)
                cur_hi, cur_lo = oh, ol
                w = w2
                flip ^= 1
            v.tensor_tensor(out=out_scalar, in0=cur_hi, in1=cur_lo, op=ALU.add)
            v.drain()

        def prog_sync(sy):
            for i in range(N_SCAN):
                if i < 2:
                    sems.wait(sy, "v:w3")
                else:
                    sems.wait(sy, f"s:scup:{i-2}")
                dd = sy.dma_start(
                    out=scst[i % 2],
                    in_=wup_d[i // 2][:, (i % 2) * SCAN_CH:(i % 2) * SCAN_CH + SCAN_CH])
                sems.inc("supa", dd, 16)
            if stop_after == 'A':
                return
            sems.wait_val(sy, "sx2s", 16 * NTT)
            sems.wait(sy, "v:phaseA")
            sems.wait(sy, "s:phaseA")
            sems.wait(sy, "t:phaseA")
            sems.wait(sy, "g:phaseA")
            for fc in range(NFC):
                if fc >= 2:
                    sems.wait(sy, f"v:uC:{fc-2}")
                dd = sy.dma_start(out=upst[fc % 2], in_=wup_d[fc])
                sems.inc("supc", dd, 16)
            if stop_after == 'C':
                return
            sems.wait(sy, "v:phaseC")
            sems.wait(sy, "s:phaseC")
            sems.wait(sy, "t:phaseC")
            for j in range(NFSB * NDG):
                fsb, dg = j // NDG, j % NDG
                for half in range(2):
                    jj = j * 2 + half
                    if jj >= 2:
                        sems.wait(sy, f"v:dC:{jj-2}")
                    dd = sy.dma_start(
                        out=dnst[jj % 2],
                        in_=wdn_d[fsb, dg][:, half * 2048:half * 2048 + 2048])
                    sems.inc("sdnd", dd, 16)

        def prog_gpsimd(g):
            g.memset(id_bf[:], 0.0)
            g.affine_select(out=id_bf[:], in_=id_bf[:], compare_op=ALU.not_equal,
                            fill=1.0, base=0, pattern=[[-1, 128]], channel_multiplier=1)
            g.memset(id_f32[:], 0.0)
            i2 = g.affine_select(out=id_f32[:], in_=id_f32[:], compare_op=ALU.not_equal,
                                 fill=1.0, base=0, pattern=[[-1, 128]],
                                 channel_multiplier=1)
            sems.inc_mark("sg", i2, "g:ident")
            for tt in range(2):
                dd = g.dma_start(out=xt[tt], in_=x_d[tt])
                sems.inc("sx", dd, 16)
            for h in range(32):
                src = wq_d[h].rearrange("(c p) m -> p c m", p=128)
                if h == 0:
                    dd = g.dma_start(out=wqsum, in_=src)
                else:
                    sems.wait_val(g, "swq", 16 * h)
                    dd = g.dma_start(out=wqsum, in_=src, accum_op=ALU.add)
                sems.inc("swq", dd, 16)
            dd = g.dma_start(out=w3[:, :, 128:256],
                             in_=wk_d.rearrange("c p m -> p c m"))
            sems.inc("saux", dd, 16)
            dd = g.dma_start(out=w3[:, :, 256:384],
                             in_=wv_d.rearrange("c p m -> p c m"))
            sems.inc("saux", dd, 16)
            dd = g.dma_start(out=cols[:, 32:64], in_=bq_d)
            sems.inc("saux", dd, 16)
            dd = g.dma_start(out=bcols[:, 1:2], in_=bk_d)
            sems.inc("saux", dd, 16)
            dd = g.dma_start(out=bcols[:, 2:3], in_=bv_d)
            sems.inc_mark("saux", dd, "g:aux", k=16)
            sci = 0
            for tt in range(NTT):
                if tt >= 2:
                    dd = g.dma_start(out=xt[tt % 2], in_=x_d[tt])
                    sems.inc("sx", dd, 16)
                for i in range(32 * tt, 32 * tt + 32):
                    sems.wait_val(g, "supa", 16 * (i + 1))
                    ins = g.tensor_scalar(out=scst[i % 2], in0=scst[i % 2],
                                          scalar1=-1.0, scalar2=1.0,
                                          op0=ALU.max, op1=ALU.min)
                    sems.inc_mark("sg", ins, f"g:scup:{i}")
                    sci += 1
                sems.wait(g, f"v:xq:{tt}")
                dd = g.dma_start(out=x2_d[tt], in_=xt[tt % 2])
                sems.inc("sx2s", dd, 16)
                if stop_after == 'A':
                    dd = g.dma_start(out=out_d[tt], in_=xt[tt % 2])
                    sems.inc("sout", dd, 16)
            sems.alias("g:phaseA", "sx2s")
            if stop_after == 'A':
                if dbg:
                    sems.wait(g, "v:phaseA")
                    dd = g.dma_start(out=dbg_qkv,
                                     in_=qkvT.rearrange("p a n -> p (a n)"))
                    sems.inc("sout", dd, 16)
                    dd = g.dma_start(out=dbg_misc[:, 0:8], in_=amax_x[:])
                    sems.inc("sout", dd, 16)
                    dd = g.dma_start(out=dbg_misc[:, 8:16], in_=sxq_c[:])
                    sems.inc("sout", dd, 16)
                    dd = g.dma_start(out=dbg_misc[:, 16:19], in_=bcols[:, 0:3])
                    sems.inc("sout", dd, 16)
                    dd = g.dma_start(out=dbg_misc[:, 24:27],
                                     in_=bias_bc[:, 0:3])
                    sems.inc("sout", dd, 16)
                    dd = g.dma_start(out=dbg_w3,
                                     in_=w3.rearrange("p c n -> p (c n)")
                                     .bitcast(F32))
                    sems.inc("sout", dd, 16)
                    dd = g.dma_start(out=dbg_ln1, in_=ln1t.bitcast(F32))
                    sems.inc("sout", dd, 16)
                    dd = g.dma_start(out=dbg_fm,
                                     in_=ln1fm.rearrange("p c t -> p (c t)")
                                     .bitcast(F32))
                    sems.inc("sout", dd, 16)
                    dd = g.dma_start(out=dbg_cols, in_=cols[:])
                    sems.inc("sout", dd, 16)
                    sems.wait_val(g, "sout", 16 * (NTT + 9))
                else:
                    sems.wait_val(g, "sout", 16 * NTT)
                return
            for fc in range(NFC):
                sems.wait(g, f"v:gsq:{fc}")
                dd = g.dma_start(out=g_d[fc], in_=grng[fc % 4])
                sems.inc("sgst", dd, 16)
                for i in (2 * fc, 2 * fc + 1):
                    sems.wait_val(g, "sdnc", 16 * (i + 1))
                    ins = g.tensor_scalar(out=scstd[i % 2], in0=scstd[i % 2],
                                          scalar1=-1.0, scalar2=1.0,
                                          op0=ALU.max, op1=ALU.min)
                    sems.inc_mark("sg", ins, f"g:scdn:{i}")
            sems.alias("g:phaseC", "sgst")
            if stop_after == 'C':
                if dbg:
                    sems.wait(g, "v:phaseC")
                    for gi, fc in enumerate((0, 64, 127)):
                        dd = g.dma_start(out=dbg_g[gi], in_=g_d[fc])
                        sems.inc("sout", dd, 16)
                    dd = g.dma_start(out=dbg_m[:, 0:8], in_=m1_c[:])
                    sems.inc("sout", dd, 16)
                    dd = g.dma_start(out=dbg_m[:, 8:16], in_=m2_c[:])
                    sems.inc("sout", dd, 16)
                    dd = g.dma_start(out=dbg_m[:, 16:24], in_=amax_x[:])
                    sems.inc("sout", dd, 16)
                    dd = g.dma_start(allow_slow_non_contiguous=True, out=dbg_m[:, 24:25], in_=sup_b[:])
                    sems.inc("sout", dd, 16)
                    dd = g.dma_start(allow_slow_non_contiguous=True, out=dbg_m[:, 25:26], in_=sdn_b[:])
                    sems.inc("sout", dd, 16)
                    dd = g.dma_start(allow_slow_non_contiguous=True, out=dbg_m[:, 26:27], in_=sup_inv[:])
                    sems.inc("sout", dd, 16)
                    dd = g.dma_start(allow_slow_non_contiguous=True, out=dbg_m[:, 27:28], in_=sdn_inv[:])
                    sems.inc("sout", dd, 16)
                    dd = g.dma_start(out=dbg_m[:, 28:36], in_=m1_bc[:, 0:8])
                    sems.inc("sout", dd, 16)
                    sems.wait_val(g, "sout", 16 * 11)
                return
            sems.wait(g, "v:phaseC")
            sems.wait(g, "s:phaseC")
            sems.wait(g, "t:phaseC")
            for idx in range(NFC):
                if idx >= 2:
                    sems.wait(g, f"v:hq:{idx-2}")
                dd = g.dma_start(out=gld[idx % 2], in_=g_d[idx])
                sems.inc("sgld", dd, 16)
            if stop_after == 'D':
                if dbg:
                    sems.wait(g, "v:D:done")
                    for ai, dc in enumerate((0, 1, 16, 31)):
                        dd = g.dma_start(out=dbg_acc[ai], in_=acc[:, dc, :])
                        sems.inc("sout", dd, 16)
                    sems.wait_val(g, "sout", 16 * 4)
                return
            sems.wait(g, "t:D:done")
            sems.wait(g, "v:D:done")
            for tt in range(NTT):
                if tt >= 2:
                    sems.wait(g, f"v:tail:{tt-2}")
                    dd = g.dma_start(out=out_d[tt - 2], in_=outt[tt % 2])
                    sems.inc("sout", dd, 16)
                dd = g.dma_start(out=x2t[tt % 2], in_=x2_d[tt])
                sems.inc("sx2l", dd, 16)
            for tt in (NTT - 2, NTT - 1):
                sems.wait(g, f"v:tail:{tt}")
                dd = g.dma_start(out=out_d[tt], in_=outt[tt % 2])
                sems.inc("sout", dd, 16)
            sems.wait_val(g, "sout", 16 * NTT)

        def prog_vector(v):
            v.memset(ones_f[:], 1.0)
            v.memset(ones_col[:], 1.0)
            sems.wait_val(v, "swq", 16 * 32)
            ins = v.tensor_copy(out=w3[:, :, 0:128], in_=wqsum)
            sems.inc_mark("sv", ins, "v:w3")
            sems.wait(v, "g:aux")
            ins = v.tensor_reduce(out=bcols[:, 0:1].unsqueeze(2),
                                  in_=cols[:, 32:64].unsqueeze(1), axis=AX.X,
                                  op=ALU.add)
            sems.inc_mark("sv", ins, "v:bcols")
            sems.wait(v, "t:btr")
            ins = v.tensor_copy(out=brow3[0:1, :], in_=ps_misc[0][0:1, 0:384])
            sems.inc_mark("sv", ins, "v:brow")
            sems.wait(v, "t:biasbc")
            ins = v.tensor_copy(out=bias_bc, in_=ps_qkv[0])
            sems.inc_mark("sv", ins, "v:biasbc")

            for tt in range(NTT):
                xs = xt[tt % 2]
                sems.wait_val(v, "sx", 16 * (tt + 1))
                c_mean = cols[:, 8 * tt + 0:8 * tt + 1]
                c_tmp = cols[:, 8 * tt + 1:8 * tt + 2]
                c_var = cols[:, 8 * tt + 2:8 * tt + 3]
                c_rstd = cols[:, 8 * tt + 3:8 * tt + 4]
                c_m2 = cols[:, 8 * tt + 4:8 * tt + 5]

                def stats(var_evt):
                    v.tensor_reduce(out=c_mean, in_=xs, axis=AX.X, op=ALU.add)
                    v.scalar_tensor_tensor(out=ln2h, in0=xs[:, 0:2048], scalar=0.0,
                                           in1=xs[:, 0:2048], op0=ALU.add,
                                           op1=ALU.mult, accum_out=c_tmp)
                    v.scalar_tensor_tensor(out=ln2h, in0=xs[:, 2048:4096], scalar=0.0,
                                           in1=xs[:, 2048:4096], op0=ALU.add,
                                           op1=ALU.mult, accum_out=c_m2)
                    v.drain()
                    v.tensor_tensor(out=c_tmp, in0=c_tmp, in1=c_m2, op=ALU.add)
                    v.tensor_scalar(out=c_mean, in0=c_mean, scalar1=1.0 / D,
                                    scalar2=None, op0=ALU.mult, op1=ALU.bypass)
                    v.drain()
                    v.scalar_tensor_tensor(out=c_m2, in0=c_mean, scalar=0.0,
                                           in1=c_mean, op0=ALU.add, op1=ALU.mult)
                    v.tensor_scalar(out=c_var, in0=c_tmp, scalar1=1.0 / D,
                                    scalar2=None, op0=ALU.mult, op1=ALU.bypass)
                    v.drain()
                    ins = v.tensor_scalar(out=c_var, in0=c_var, scalar1=c_m2,
                                          scalar2=EPS, op0=ALU.subtract, op1=ALU.add)
                    sems.inc_mark("sv", ins, var_evt)

                stats(f"v:var1:{tt}")
                sems.wait(v, f"s:rstd1:{tt}")
                ins = v.tensor_scalar(out=ln1t, in0=xs, scalar1=c_mean,
                                      scalar2=c_rstd, op0=ALU.subtract, op1=ALU.mult)
                sems.inc_mark("sv", ins, f"v:ln1:{tt}")
                for gp in range(8):
                    sems.wait(v, f"t:tr1:{tt}:{gp}")
                    ins = v.tensor_copy(out=ln1fm[:, 4 * gp:4 * gp + 4, :],
                                        in_=ps_tr[gp % 2])
                    sems.inc_mark("sv", ins, f"v:fm1:{tt}:{gp}")
                sems.wait(v, f"t:qkv:{tt}")
                ins = v.tensor_tensor(out=qkvT[:, tt, :], in0=ps_qkv[tt % 2],
                                      in1=bias_bc, op=ALU.add)
                sems.inc_mark("sv", ins, f"v:qkv:{tt}")
                qs = qkvT[:, tt, 0:128]
                ks = qkvT[:, tt, 128:256]
                vs = qkvT[:, tt, 256:384]
                c_aq = cols[:, 8 * tt + 5:8 * tt + 6]
                c_ak = cols[:, 8 * tt + 6:8 * tt + 7]
                c_cs = cols[:, 8 * tt + 7:8 * tt + 8]
                v.tensor_reduce(out=c_aq, in_=qs, axis=AX.X, op=ALU.max,
                                apply_absolute_value=True)
                v.tensor_reduce(out=c_ak, in_=ks, axis=AX.X, op=ALU.max,
                                apply_absolute_value=True)
                v.drain()
                v.scalar_tensor_tensor(out=c_cs, in0=c_aq, scalar=0.0, in1=c_ak,
                                       op0=ALU.add, op1=ALU.mult)
                v.drain()
                ins = v.tensor_scalar(out=c_cs, in0=c_cs, scalar1=-S_ATTN,
                                      scalar2=None, op0=ALU.mult, op1=ALU.bypass)
                sems.inc_mark("sv", ins, f"v:attnc:{tt}")
                o_num = attno[:, 0:128]
                o_den = attno[:, 128:256]
                e3 = ews.rearrange("p (a b) -> p a b", b=128)
                for su in range(N_ESUB):
                    dlo = su * ESUB
                    ins = v.tensor_tensor(
                        out=e3,
                        in0=qs[:, dlo:dlo + ESUB].unsqueeze(2)
                            .broadcast_to([128, ESUB, 128]),
                        in1=ks.unsqueeze(1).broadcast_to([128, ESUB, 128]),
                        op=ALU.mult)
                    sems.inc_mark("sv", ins, f"v:S:{tt}:{su}")
                    sems.wait(v, f"s:exp:{tt}:{su}")
                    v.tensor_reduce(out=o_den[:, dlo:dlo + ESUB].unsqueeze(2),
                                    in_=e3, axis=AX.X, op=ALU.add)
                    v.tensor_tensor(out=e3, in0=e3,
                                    in1=vs.unsqueeze(1).broadcast_to([128, ESUB, 128]),
                                    op=ALU.mult)
                    ins = v.tensor_reduce(out=o_num[:, dlo:dlo + ESUB].unsqueeze(2),
                                          in_=e3, axis=AX.X, op=ALU.add)
                    sems.inc_mark("sv", ins, f"v:esub:{tt}:{su}")
                v.reciprocal(out=o_den, in_=o_den)
                v.tensor_tensor(out=o_num, in0=o_num, in1=o_den, op=ALU.mult)
                x3 = xs.rearrange("p (a b) -> p a b", b=32)
                ins = v.tensor_tensor(out=x3, in0=x3,
                                      in1=o_num.unsqueeze(2).broadcast_to([128, 128, 32]),
                                      op=ALU.add)
                sems.inc_mark("sv", ins, f"v:x2:{tt}")
                stats(f"v:var2:{tt}")
                sems.wait(v, f"s:rstd2:{tt}")
                for hh in range(2):
                    sl = slice(2048 * hh, 2048 * hh + 2048)
                    v.tensor_scalar(out=ln2h, in0=xs[:, sl], scalar1=c_mean,
                                    scalar2=c_rstd, op0=ALU.subtract, op1=ALU.mult)
                    v.tensor_reduce(out=c_tmp if hh == 0 else c_m2, in_=ln2h,
                                    axis=AX.X, op=ALU.max, apply_absolute_value=True)
                v.drain()
                v.tensor_tensor(out=amax_x[:, tt:tt + 1], in0=c_tmp, in1=c_m2,
                                op=ALU.max)
                v.drain()
                v.tensor_scalar(out=c_tmp, in0=amax_x[:, tt:tt + 1], scalar1=EPS,
                                scalar2=None, op0=ALU.max, op1=ALU.bypass)
                v.drain()
                v.reciprocal(out=sxq_c[:, tt:tt + 1], in_=c_tmp)
                v.drain()
                v.tensor_scalar(out=sxq_c[:, tt:tt + 1], in0=sxq_c[:, tt:tt + 1],
                                scalar1=127.0, scalar2=None, op0=ALU.mult,
                                op1=ALU.bypass)
                v.drain()
                for hh in range(2):
                    sl = slice(2048 * hh, 2048 * hh + 2048)
                    v.tensor_scalar(out=ln2h, in0=xs[:, sl], scalar1=c_mean,
                                    scalar2=c_rstd, op0=ALU.subtract, op1=ALU.mult)
                    v.tensor_scalar(out=ln2h, in0=ln2h, scalar1=sxq_c[:, tt:tt + 1],
                                    scalar2=MAGIC, op0=ALU.mult, op1=ALU.add)
                    ins = v.tensor_scalar(out=xqt[:, sl], in0=ln2h, scalar1=MAGIC,
                                          scalar2=None, op0=ALU.subtract,
                                          op1=ALU.bypass)
                sems.inc_mark("sv", ins, f"v:xq:{tt}")
                for gp in range(8):
                    sems.wait(v, f"t:tr2:{tt}:{gp}")
                    ins = v.tensor_copy(
                        out=xq_fm[:, 4 * gp:4 * gp + 4, 128 * tt:128 * tt + 128],
                        in_=ps_tr[gp % 2])
                    sems.inc_mark("sv", ins, f"v:fm2:{tt}:{gp}")
            sems.wait(v, "s:scupall")
            dhi, dlo = dd_tree_cols(v, up_cols, ddwkA)
            ins = v.tensor_copy(out=ddwkA[:, 3071:3072], in_=dhi)
            sems.inc_mark("sv", ins, "v:ddup1")
            sems.wait(v, "t:ddup")
            ins = v.tensor_copy(out=ddrow[0:1, 0:128], in_=ps_misc[0][0:1, 0:128])
            ins = v.tensor_copy(out=ddrow[0:1, 128:256], in_=ps_misc[0][0:1, 128:256])
            v.drain()
            dd_tree_row(v, sc_row[0:1, 0:1])
            v.tensor_scalar(out=sc_row[0:1, 0:1], in0=sc_row[0:1, 0:1],
                            scalar1=1.0 / NW, scalar2=EPS, op0=ALU.mult, op1=ALU.add)
            v.drain()
            ins = v.reciprocal(out=sc_row[0:1, 1:2], in_=sc_row[0:1, 0:1])
            v.drain()
            ins = v.tensor_copy(out=sc_row[0:1, 1:2], in_=sc_row[0:1, 1:2])
            sems.inc_mark("sv", ins, "v:suprow")
            sems.wait(v, "t:supbc")
            ins = v.tensor_copy(out=sup_b[:], in_=ps_misc[1][:, 0:1])
            v.drain()
            v.tensor_scalar(out=sup_inv[:], in0=sup_b[:], scalar1=0.5, scalar2=None,
                            op0=ALU.mult, op1=ALU.bypass)
            v.tensor_scalar(out=nsup2[:], in0=sup_b[:], scalar1=-0.5, scalar2=None,
                            op0=ALU.mult, op1=ALU.bypass)
            v.drain()
            ins = v.tensor_copy(out=bcols[:, 3:4], in_=ones_col[:])
            sems.inc_mark("sv", ins, "v:phaseA")
            if stop_after == 'A':
                return

            v.memset(mrun, -3.0e38)
            for fc in range(NFC):
                sems.wait_val(v, "supc", 16 * (fc + 1))
                if fc >= 2:
                    sems.wait(v, f"t:up:{fc-2}")
                ins = v.tensor_scalar(out=ternu[fc % 2], in0=upst[fc % 2],
                                      scalar1=nsup2[:, 0:1], scalar2=None,
                                      op0=ALU.is_le, op1=ALU.bypass)
                sems.inc_mark("sv", ins, f"v:uA:{fc}")
                ins = v.scalar_tensor_tensor(out=ternu[fc % 2], in0=upst[fc % 2],
                                             scalar=sup_inv[:, 0:1], in1=ternu[fc % 2],
                                             op0=ALU.is_ge, op1=ALU.subtract)
                sems.inc_mark("sv", ins, f"v:uC:{fc}")
                sems.wait(v, f"t:up:{fc}")
                if fc >= 4:
                    sems.wait_val(v, "sgst", 16 * (fc - 3))
                v.tensor_scalar(out=reluws, in0=ps_up[fc % 2], scalar1=0.0,
                                scalar2=None, op0=ALU.max, op1=ALU.bypass)
                ins = v.tensor_tensor(out=mrun, in0=mrun, in1=ps_up[fc % 2],
                                      op=ALU.max)
                sems.inc_mark("sv", ins, f"v:upev:{fc}")
                ins = v.scalar_tensor_tensor(out=grng[fc % 4], in0=reluws,
                                             scalar=0.0, in1=reluws,
                                             op0=ALU.add, op1=ALU.mult)
                sems.inc_mark("sv", ins, f"v:gsq:{fc}")
            sems.wait(v, "s:scdnall")
            dhi, dlo = dd_tree_cols(v, dn_cols, ddwkC)
            ins = v.tensor_copy(out=ddwkC[:, 3071:3072], in_=dhi)
            sems.inc_mark("sv", ins, "v:dddn1")
            sems.wait(v, "t:dddn")
            ins = v.tensor_copy(out=ddrow[0:1, 0:128], in_=ps_misc[0][0:1, 0:128])
            ins = v.tensor_copy(out=ddrow[0:1, 128:256], in_=ps_misc[0][0:1, 128:256])
            v.drain()
            dd_tree_row(v, sc_row[0:1, 2:3])
            v.tensor_scalar(out=sc_row[0:1, 2:3], in0=sc_row[0:1, 2:3],
                            scalar1=1.0 / NW, scalar2=EPS, op0=ALU.mult, op1=ALU.add)
            v.drain()
            ins = v.reciprocal(out=sc_row[0:1, 3:4], in_=sc_row[0:1, 2:3])
            v.drain()
            ins = v.tensor_copy(out=sc_row[0:1, 3:4], in_=sc_row[0:1, 3:4])
            sems.inc_mark("sv", ins, "v:sdnrow")
            sems.wait(v, "t:sdnbc")
            ins = v.tensor_copy(out=sdn_b[:], in_=ps_misc[1][:, 0:1])
            v.drain()
            v.tensor_scalar(out=sdn_inv[:], in0=sdn_b[:], scalar1=0.5, scalar2=None,
                            op0=ALU.mult, op1=ALU.bypass)
            v.tensor_scalar(out=nsdn2[:], in0=sdn_b[:], scalar1=-0.5, scalar2=None,
                            op0=ALU.mult, op1=ALU.bypass)
            v.drain()
            ins = v.tensor_copy(out=bcols[:, 3:4], in_=ones_col[:])
            sems.inc_mark("sv", ins, "v:sdnb")
            for tt in range(NTT):
                sems.wait(v, f"t:trm:{tt}")
                c_g = cols[:, 8 * tt + 0:8 * tt + 1]
                c_c1 = cols[:, 8 * tt + 1:8 * tt + 2]
                c_ah = cols[:, 8 * tt + 2:8 * tt + 3]
                c_sh = cols[:, 8 * tt + 3:8 * tt + 4]
                v.tensor_reduce(out=c_g, in_=ps_misc[tt % 2][:, 0:128], axis=AX.X,
                                op=ALU.max)
                v.drain()
                v.tensor_scalar(out=c_g, in0=c_g, scalar1=0.0, scalar2=None,
                                op0=ALU.max, op1=ALU.bypass)
                v.tensor_scalar(out=c_c1, in0=amax_x[:, tt:tt + 1],
                                scalar1=sup_b[:, 0:1], scalar2=1.0 / 127.0,
                                op0=ALU.mult, op1=ALU.mult)
                v.drain()
                v.scalar_tensor_tensor(out=c_g, in0=c_g, scalar=0.0, in1=c_g,
                                       op0=ALU.add, op1=ALU.mult)
                v.scalar_tensor_tensor(out=c_c1, in0=c_c1, scalar=0.0, in1=c_c1,
                                       op0=ALU.add, op1=ALU.mult)
                v.drain()
                v.scalar_tensor_tensor(out=c_ah, in0=c_g, scalar=0.0, in1=c_c1,
                                       op0=ALU.add, op1=ALU.mult)
                v.drain()
                v.tensor_scalar(out=c_ah, in0=c_ah, scalar1=EPS, scalar2=None,
                                op0=ALU.max, op1=ALU.bypass)
                v.drain()
                v.reciprocal(out=c_sh, in_=c_ah)
                v.drain()
                v.tensor_scalar(out=c_sh, in0=c_sh, scalar1=127.0, scalar2=None,
                                op0=ALU.mult, op1=ALU.bypass)
                v.drain()
                ins = v.scalar_tensor_tensor(out=m1_c[:, tt:tt + 1], in0=c_c1,
                                             scalar=0.0, in1=c_sh,
                                             op0=ALU.add, op1=ALU.mult)
                sems.inc_mark("sv", ins, f"v:m1c:{tt}")
                v.tensor_scalar(out=m2_c[:, tt:tt + 1], in0=c_ah,
                                scalar1=1.0 / 127.0, scalar2=sdn_b[:, 0:1],
                                op0=ALU.mult, op1=ALU.mult)
                sems.wait(v, f"t:trm1:{tt}")
                ins = v.tensor_copy(out=m1_row[0:1, 128 * tt:128 * tt + 128],
                                    in_=ps_m1[tt % 2][0:1, 0:128])
                sems.inc_mark("sv", ins, f"v:m1r:{tt}")
            sems.wait(v, "t:m1bc")
            ins = v.tensor_copy(out=m1_bc[:, 0:512], in_=ps_up[0][:, 0:512])
            ins = v.tensor_copy(out=m1_bc[:, 512:1024], in_=ps_up[1][:, 0:512])
            sems.inc_mark("sv", ins, "v:phaseC")
            if stop_after == 'C':
                return

            for fsb in range(NFSB):
                for jj in range(FC_PER_FSB):
                    idx = fsb * FC_PER_FSB + jj
                    sems.wait_val(v, "sgld", 16 * (idx + 1))
                    if fsb > 0 and jj == 0:
                        sems.wait(v, f"t:dn:{(fsb * NDG - 1) * 2 + 1}")
                    ins = v.tensor_tensor(out=gld[idx % 2], in0=gld[idx % 2],
                                          in1=m1_bc[:], op=ALU.mult)
                    ins = v.tensor_scalar(out=hq[:, jj, :], in0=gld[idx % 2],
                                          scalar1=MAGIC, scalar2=MAGIC,
                                          op0=ALU.add, op1=ALU.subtract)
                    sems.inc_mark("sv", ins, f"v:hq:{idx}")
                for dg in range(NDG):
                    j = fsb * NDG + dg
                    for half in range(2):
                        jj2 = j * 2 + half
                        sems.wait_val(v, "sdnd", 16 * (jj2 + 1))
                        if jj2 >= 2:
                            sems.wait(v, f"t:dn:{jj2-2}")
                        ins = v.tensor_scalar(out=ternd[jj2 % 2], in0=dnst[jj2 % 2],
                                              scalar1=nsdn2[:, 0:1], scalar2=None,
                                              op0=ALU.is_le, op1=ALU.bypass)
                        sems.inc_mark("sv", ins, f"v:dA:{jj2}")
                        ins = v.scalar_tensor_tensor(out=ternd[jj2 % 2],
                                             in0=dnst[jj2 % 2],
                                             scalar=sdn_inv[:, 0:1],
                                             in1=ternd[jj2 % 2],
                                             op0=ALU.is_ge, op1=ALU.subtract)
                        sems.inc_mark("sv", ins, f"v:dC:{jj2}")
                    sems.wait(v, f"t:dn:{j*2+1}")
                    for dcl in range(2):
                        dst = acc[:, 2 * dg + dcl, :]
                        pp = ps_dn[(dg % 2) * 2 + dcl]
                        if fsb == 0:
                            ins = v.tensor_copy(out=dst, in_=pp)
                        else:
                            ins = v.tensor_tensor(out=dst, in0=dst, in1=pp,
                                                  op=ALU.add)
                    sems.inc_mark("sv", ins, f"v:dev:{j}")
            sems.mark("v:D:done", "sv")
            if stop_after == 'D':
                return
            for tt in range(NTT):
                sems.wait_val(v, "sx2l", 16 * (tt + 1))
                if tt >= 2:
                    sems.wait_val(v, "sout", 16 * (tt - 1))
                for gp in range(8):
                    sems.wait(v, f"t:tlt:{tt}:{gp}")
                    ins = v.scalar_tensor_tensor(
                        out=outt[tt % 2][:, 512 * gp:512 * gp + 512],
                        in0=ps_tl[gp % 2].rearrange("p c t -> p (c t)"),
                        scalar=m2_c[:, tt:tt + 1],
                        in1=x2t[tt % 2][:, 512 * gp:512 * gp + 512],
                        op0=ALU.mult, op1=ALU.add)
                    sems.inc_mark("sv", ins, f"v:tailg:{tt}:{gp}")
                sems.mark(f"v:tail:{tt}", "sv")

        def prog_scalar(s):
            scup = 0
            for tt in range(NTT):
                c_var = cols[:, 8 * tt + 2:8 * tt + 3]
                c_rstd = cols[:, 8 * tt + 3:8 * tt + 4]
                c_cs = cols[:, 8 * tt + 7:8 * tt + 8]
                sems.wait(s, f"v:var1:{tt}")
                s.activation(out=c_rstd, in_=c_var, func=AF.Ln)
                s.drain()
                ins = s.activation(out=c_rstd, in_=c_rstd, func=AF.Exp, scale=-0.5)
                sems.inc_mark("ss", ins, f"s:rstd1:{tt}")
                for su in range(N_ESUB):
                    sems.wait(s, f"v:S:{tt}:{su}")
                    ins = s.activation(out=ews, in_=ews, func=AF.Exp,
                                       scale=S_ATTN, bias=c_cs)
                    sems.inc_mark("ss", ins, f"s:exp:{tt}:{su}")
                sems.wait(s, f"v:var2:{tt}")
                s.activation(out=c_rstd, in_=c_var, func=AF.Ln)
                s.drain()
                ins = s.activation(out=c_rstd, in_=c_rstd, func=AF.Exp, scale=-0.5)
                sems.inc_mark("ss", ins, f"s:rstd2:{tt}")
                for i in range(32 * tt, 32 * tt + 32):
                    sems.wait(s, f"g:scup:{i}")
                    for q in range(4):
                        ins = s.activation(out=scst[i % 2][:, 512 * q:512 * q + 512],
                                           in_=scst[i % 2][:, 512 * q:512 * q + 512],
                                           func=AF.Abs,
                                           accum_out=up_cols[:, 4 * i + q:4 * i + q + 1])
                    sems.inc_mark("ss", ins, f"s:scup:{i}")
                    scup += 1
            s.drain()
            ins = s.activation(out=bcols[:, 3:4], in_=bcols[:, 3:4], func=AF.Abs)
            sems.inc_mark("ss", ins, "s:scupall")
            sems.alias("s:phaseA", "ss")
            if stop_after == 'A':
                return
            sems.wait(s, "t:qkv:7")
            for fc in range(NFC):
                for i in (2 * fc, 2 * fc + 1):
                    if i >= 2:
                        sems.wait(s, f"s:scdn:{i-2}")
                    jch = i // 2
                    dd = s.dma_start(
                        out=scstd[i % 2],
                        in_=wdn_d[jch // NDG, jch % NDG]
                        [:, (i % 2) * SCAN_CH:(i % 2) * SCAN_CH + SCAN_CH])
                    sems.inc("sdnc", dd, 16)
                for i in (2 * fc, 2 * fc + 1):
                    sems.wait(s, f"g:scdn:{i}")
                    for q in range(4):
                        ins = s.activation(out=scstd[i % 2][:, 512 * q:512 * q + 512],
                                           in_=scstd[i % 2][:, 512 * q:512 * q + 512],
                                           func=AF.Abs,
                                           accum_out=dn_cols[:, 4 * i + q:4 * i + q + 1])
                    sems.inc_mark("ss", ins, f"s:scdn:{i}")
            s.drain()
            ins = s.activation(out=bcols[:, 3:4], in_=bcols[:, 3:4], func=AF.Abs)
            sems.inc_mark("ss", ins, "s:scdnall")
            sems.alias("s:phaseC", "ss")
            if stop_after == 'C':
                return

        def prog_tensor(t):
            sems.wait(t, "v:bcols")
            sems.wait(t, "g:ident")
            for bi in range(3):
                ins = t.transpose(out=ps_misc[0][0:1, 128 * bi:128 * bi + 128],
                                  in_=bcols[:, bi:bi + 1], identity=id_f32[:])
            sems.inc_mark("st", ins, "t:btr")
            sems.wait(t, "v:brow")
            ins = t.matmul(out=ps_qkv[0], lhsT=ones_f[:], rhs=brow3[0:1, :],
                           start=True, stop=True)
            sems.inc_mark("st", ins, "t:biasbc")
            for tt in range(NTT):
                sems.wait(t, f"v:ln1:{tt}")
                for gp in range(8):
                    if not (tt == 0 and gp <= 1):
                        if gp >= 2:
                            sems.wait(t, f"v:fm1:{tt}:{gp-2}")
                        else:
                            sems.wait(t, f"v:fm2:{tt-1}:{6+gp}")
                    for k in range(4):
                        cidx = 4 * gp + k
                        ins = t.transpose(out=ps_tr[gp % 2][:, k, :],
                                          in_=ln1t[:, 128 * cidx:128 * cidx + 128],
                                          identity=id_bf[:])
                    sems.inc_mark("st", ins, f"t:tr1:{tt}:{gp}")
                sems.wait(t, f"v:fm1:{tt}:7")
                if tt == 0:
                    sems.wait(t, "v:biasbc")
                else:
                    sems.wait(t, f"v:qkv:{tt-1}")
                for dc in range(NDC):
                    ins = t.matmul(out=ps_qkv[tt % 2], lhsT=ln1fm[:, dc, :],
                                   rhs=w3[:, dc, :], start=(dc == 0),
                                   stop=(dc == NDC - 1))
                sems.inc_mark("st", ins, f"t:qkv:{tt}")
                sems.wait(t, f"v:xq:{tt}")
                for gp in range(8):
                    if gp >= 2:
                        sems.wait(t, f"v:fm2:{tt}:{gp-2}")
                    else:
                        sems.wait(t, f"v:fm1:{tt}:{6+gp}")
                    for k in range(4):
                        cidx = 4 * gp + k
                        ins = t.transpose(out=ps_tr[gp % 2][:, k, :],
                                          in_=xqt[:, 128 * cidx:128 * cidx + 128],
                                          identity=id_bf[:])
                    sems.inc_mark("st", ins, f"t:tr2:{tt}:{gp}")
            sems.wait(t, "v:ddup1")
            ins = t.transpose(out=ps_misc[0][0:1, 0:128],
                              in_=ddwkA[:, 1024:1025], identity=id_f32[:])
            ins = t.transpose(out=ps_misc[0][0:1, 128:256],
                              in_=ddwkA[:, 1536:1537], identity=id_f32[:])
            sems.inc_mark("st", ins, "t:ddup")
            sems.wait(t, "v:suprow")
            ins = t.matmul(out=ps_misc[1][:, 0:2], lhsT=ones_f[:],
                           rhs=sc_row[0:1, 0:2], start=True, stop=True)
            sems.inc_mark("st", ins, "t:supbc")
            sems.alias("t:phaseA", "st")
            if stop_after == 'A':
                return
            for fc in range(NFC):
                sems.wait(t, f"v:uC:{fc}")
                if fc >= 2:
                    sems.wait(t, f"v:upev:{fc-2}")
                tern3 = ternu[fc % 2].rearrange("p (c m) -> p c m", m=128)
                for dc in range(NDC):
                    for th in range(2):
                        ins = t.matmul(out=ps_up[fc % 2][:, 512 * th:512 * th + 512],
                                       lhsT=tern3[:, dc, :],
                                       rhs=xq_fm[:, dc, 512 * th:512 * th + 512],
                                       start=(dc == 0), stop=(dc == NDC - 1))
                sems.inc_mark("st", ins, f"t:up:{fc}")
            sems.wait(t, "v:dddn1")
            ins = t.transpose(out=ps_misc[0][0:1, 0:128],
                              in_=ddwkC[:, 1024:1025], identity=id_f32[:])
            ins = t.transpose(out=ps_misc[0][0:1, 128:256],
                              in_=ddwkC[:, 1536:1537], identity=id_f32[:])
            sems.inc_mark("st", ins, "t:dddn")
            sems.wait(t, "v:sdnrow")
            ins = t.matmul(out=ps_misc[1][:, 0:2], lhsT=ones_f[:],
                           rhs=sc_row[0:1, 2:4], start=True, stop=True)
            sems.inc_mark("st", ins, "t:sdnbc")
            for tt in range(NTT):
                if tt == 0:
                    sems.wait(t, f"v:upev:{NFC-1}")
                    sems.wait(t, "v:sdnb")
                if tt >= 2:
                    sems.wait(t, f"v:m1c:{tt-2}")
                ins = t.transpose(out=ps_misc[tt % 2][:, 0:128],
                                  in_=mrun[:, 128 * tt:128 * tt + 128],
                                  identity=id_f32[:])
                sems.inc_mark("st", ins, f"t:trm:{tt}")
                sems.wait(t, f"v:m1c:{tt}")
                ins = t.transpose(out=ps_m1[tt % 2][0:1, 0:128],
                                  in_=m1_c[:, tt:tt + 1], identity=id_f32[:])
                sems.inc_mark("st", ins, f"t:trm1:{tt}")
            sems.wait(t, "v:m1r:7")
            ins = t.matmul(out=ps_up[0][:, 0:512], lhsT=ones_f[:],
                           rhs=m1_row[0:1, 0:512], start=True, stop=True)
            ins = t.matmul(out=ps_up[1][:, 0:512], lhsT=ones_f[:],
                           rhs=m1_row[0:1, 512:1024], start=True, stop=True)
            sems.inc_mark("st", ins, "t:m1bc")
            sems.alias("t:phaseC", "st")
            if stop_after == 'C':
                return
            for j in range(NFSB * NDG):
                fsb, dg = j // NDG, j % NDG
                if dg == 0:
                    sems.wait(t, f"v:hq:{fsb * FC_PER_FSB + FC_PER_FSB - 1}")
                if j >= 2:
                    sems.wait(t, f"v:dev:{j-2}")
                for half in range(2):
                    sems.wait(t, f"v:dC:{j*2+half}")
                    tern4 = ternd[(j * 2 + half) % 2].rearrange(
                        "p (a c m) -> p a c m", c=2, m=128)
                    for fcl in range(8):
                        fcg = half * 8 + fcl
                        for dcl in range(2):
                            for th in range(2):
                                ins = t.matmul(
                                    out=ps_dn[(dg % 2) * 2 + dcl][:, 512 * th:512 * th + 512],
                                    lhsT=tern4[:, fcl, dcl, :],
                                    rhs=hq[:, fcg, 512 * th:512 * th + 512],
                                    start=(fcg == 0), stop=(fcg == FC_PER_FSB - 1))
                    sems.inc_mark("st", ins, f"t:dn:{j*2+half}")
            sems.alias("t:D:done", "st")
            if stop_after == 'D':
                return
            for tt in range(NTT):
                for gp in range(8):
                    if tt == 0 and gp <= 1:
                        sems.wait(t, "v:D:done")
                    elif gp >= 2:
                        sems.wait(t, f"v:tailg:{tt}:{gp-2}")
                    else:
                        sems.wait(t, f"v:tailg:{tt-1}:{6+gp}")
                    for k in range(4):
                        ins = t.transpose(
                            out=ps_tl[gp % 2][:, k, :],
                            in_=acc[:, 4 * gp + k, 128 * tt:128 * tt + 128],
                            identity=id_f32[:])
                    sems.inc_mark("st", ins, f"t:tlt:{tt}:{gp}")

        for name in list(sems.cnt):
            sems.handles[name] = E(nc.semaphore(name))

        dry = DryEng()
        sems.dry = True
        import os
        if os.environ.get("KERNEL_TRACE_SCHED"):
            sems.trace = {n: [] for n in
                          ("sync", "gpsimd", "vector", "scalar", "tensor")}
        for nm, f in (("sync", prog_sync), ("gpsimd", prog_gpsimd),
                      ("vector", prog_vector), ("scalar", prog_scalar),
                      ("tensor", prog_tensor)):
            sems.cur = nm
            f(dry)
        if sems.trace is not None:
            import json
            with open("/tmp/sched_trace.json", "w") as fh:
                json.dump({"trace": sems.trace,
                           "events": {k: list(v) for k, v in sems.events.items()}},
                          fh)
        final_counts = dict(sems.cnt)
        sems.dry = False
        for k in sems.cnt:
            sems.cnt[k] = 0

        block = E(nc.Block())

        @block.sync
        def _(sy):
            prog_sync(sy)

        @block.gpsimd
        def _(g):
            prog_gpsimd(g)

        @block.vector
        def _(v):
            prog_vector(v)

        @block.scalar
        def _(s):
            prog_scalar(s)

        @block.tensor
        def _(t):
            prog_tensor(t)

        for k, vv in final_counts.items():
            assert sems.cnt[k] == vv, (k, sems.cnt[k], vv)

    return nc


def _relayout(Wq, Wk, Wv, w_up, w_down):
    wq = np.ascontiguousarray(np.asarray(Wq, dtype=np.float32).transpose(2, 0, 1))
    wk = np.ascontiguousarray(np.asarray(Wk, dtype=np.float32).reshape(D, 128)
                              .reshape(NDC, 128, 128))
    wv = np.ascontiguousarray(np.asarray(Wv, dtype=np.float32).reshape(D, 128)
                              .reshape(NDC, 128, 128))
    wup = np.asarray(w_up, dtype=np.float32)
    wup_arr = np.ascontiguousarray(
        wup.reshape(NFC, 128, NDC, 128).transpose(0, 3, 2, 1)).reshape(NFC, 128, D)
    wdn = np.asarray(w_down, dtype=np.float32)
    wdn_arr = np.ascontiguousarray(
        wdn.reshape(NDG, 2, 128, NFSB, FC_PER_FSB, 128)
        .transpose(3, 0, 5, 4, 1, 2)).reshape(NFSB, NDG, 128, D)
    return wq, wk, wv, wup_arr, wdn_arr


def kernel(L0_attn_x, L0_attn_Wq, L0_attn_bq, L0_attn_Wk, L0_attn_bk,
           L0_attn_Wv, L0_attn_bv, L0_ffn_w_up, L0_ffn_w_down):
    x = np.asarray(L0_attn_x, dtype=np.float32).reshape(N_CORES * T, D)
    wq, wk, wv, wup_arr, wdn_arr = _relayout(
        L0_attn_Wq, L0_attn_Wk, L0_attn_Wv, L0_ffn_w_up, L0_ffn_w_down)
    bq = np.ascontiguousarray(np.asarray(L0_attn_bq, dtype=np.float32))
    bk = np.ascontiguousarray(np.asarray(L0_attn_bk, dtype=np.float32))
    bv = np.ascontiguousarray(np.asarray(L0_attn_bv, dtype=np.float32))

    nc = build_nc()
    in_maps = []
    for c in range(N_CORES):
        shard = np.ascontiguousarray(x[c * T:(c + 1) * T].reshape(NTT, 128, D))
        in_maps.append({"x": shard, "wq": wq, "wk": wk, "wv": wv,
                        "bq": bq, "bk": bk, "bv": bv,
                        "wup": wup_arr, "wdn": wdn_arr})
    res = run_bass_kernel_spmd(nc, in_maps, core_ids=list(range(N_CORES)))
    outs = [np.asarray(r["out"]).reshape(T, D) for r in res.results]
    return np.concatenate(outs, axis=0).reshape(4, 2048, D)


if __name__ == "__main__":
    build_nc()
    print("build ok")


# version 06
# speedup vs baseline: 1.8500x; 1.8500x over previous
import numpy as np
import ml_dtypes
from contextlib import ExitStack

import concourse.bass as bass
import concourse.mybir as mybir
from concourse.bass_utils import run_bass_kernel_spmd

F32 = mybir.dt.float32
BF16 = mybir.dt.bfloat16
AF = mybir.ActivationFunctionType
ALU = mybir.AluOpType
AX = mybir.AxisListType

MAGIC = float(np.float32(3 * 2**22))
N_CORES = 8
T = 1024
NTT = 8
D = 4096
NDC = 32
F = 16384
NFC = 128
NFSB = 8
FC_PER_FSB = 16
NDG = 16
S_ATTN = 1.0 / float(np.sqrt(np.float32(128.0)))
EPS = 1e-5
NW = float(F * D)
ESUB = 8
N_ESUB = 128 // ESUB
SCAN_CH = 2048
N_SCAN = (NFC * 128 * D) // (128 * SCAN_CH)


class Sems:
    def __init__(self):
        self.cnt = {}
        self.events = {}
        self.handles = {}
        self.dry = True
        self.trace = None
        self.cur = None

    def sem(self, name):
        self.cnt[name] = 0
        return name

    def inc(self, name, instr, k=1):
        self.cnt[name] += k
        if self.dry and self.trace is not None:
            self.trace[self.cur].append(("inc", name, k))
        if not self.dry:
            instr.then_inc(self.handles[name], k)

    def mark(self, evt, name):
        if self.dry:
            assert evt not in self.events, evt
            self.events[evt] = (name, self.cnt[name])
        else:
            assert self.events[evt] == (name, self.cnt[name]), (
                evt, self.events[evt], (name, self.cnt[name]))

    def inc_mark(self, name, instr, evt, k=1):
        self.inc(name, instr, k)
        self.mark(evt, name)

    def alias(self, evt, name):
        self.mark(evt, name)

    def wait(self, eng, evt):
        if self.dry:
            if self.trace is not None:
                self.trace[self.cur].append(("waitevt", evt))
            return
        name, v = self.events[evt]
        if v > 0:
            eng.wait_ge(self.handles[name], v)

    def wait_val(self, eng, name, v):
        if self.dry and self.trace is not None:
            self.trace[self.cur].append(("wait", name, v))
        if not self.dry:
            eng.wait_ge(self.handles[name], v)


class _DryInstr:
    def then_inc(self, *a, **kw):
        return self


class DryEng:
    def __getattr__(self, k):
        def f(*a, **kw):
            return _DryInstr()
        return f


def build_nc(stop_after=None, dbg=False):
    nc = bass.Bass("TRN2", target_bir_lowering=False)
    if dbg:
        dbg_qkv = nc.dram_tensor("dbg_qkv", [128, NTT * 384], F32,
                                 kind="ExternalOutput").ap()
        dbg_misc = nc.dram_tensor("dbg_misc", [128, 64], F32,
                                  kind="ExternalOutput").ap()
        dbg_w3 = nc.dram_tensor("dbg_w3", [128, 6144], F32,
                                kind="ExternalOutput").ap()
        dbg_ln1 = nc.dram_tensor("dbg_ln1", [128, 2048], F32,
                                 kind="ExternalOutput").ap()
        dbg_fm = nc.dram_tensor("dbg_fm", [128, 2048], F32,
                                kind="ExternalOutput").ap()
        dbg_cols = nc.dram_tensor("dbg_cols", [128, 64], F32,
                                  kind="ExternalOutput").ap()
        dbg_g = nc.dram_tensor("dbg_g", [3, 128, T], F32,
                               kind="ExternalOutput").ap()
        dbg_m = nc.dram_tensor("dbg_m", [128, 40], F32,
                               kind="ExternalOutput").ap()
        dbg_acc = nc.dram_tensor("dbg_acc", [4, 128, T], F32,
                                 kind="ExternalOutput").ap()

    x_d = nc.dram_tensor("x", [NTT, 128, D], F32, kind="ExternalInput").ap()
    wq_d = nc.dram_tensor("wq", [32, D, 128], F32, kind="ExternalInput").ap()
    wk_d = nc.dram_tensor("wk", [NDC, 128, 128], F32, kind="ExternalInput").ap()
    wv_d = nc.dram_tensor("wv", [NDC, 128, 128], F32, kind="ExternalInput").ap()
    bq_d = nc.dram_tensor("bq", [128, 32], F32, kind="ExternalInput").ap()
    bk_d = nc.dram_tensor("bk", [128, 1], F32, kind="ExternalInput").ap()
    bv_d = nc.dram_tensor("bv", [128, 1], F32, kind="ExternalInput").ap()
    wup_d = nc.dram_tensor("wup", [NFC, 128, D], F32, kind="ExternalInput").ap()
    wdn_d = nc.dram_tensor("wdn", [NFSB, NDG, 128, D], F32, kind="ExternalInput").ap()
    g_d = nc.dram_tensor("g_scr", [NFC, 128, T], F32, kind="Internal").ap()
    x2_d = nc.dram_tensor("x2_scr", [NTT, 128, D], F32, kind="Internal").ap()
    out_d = nc.dram_tensor("out", [NTT, 128, D], F32, kind="ExternalOutput").ap()

    sems = Sems()
    for s in ("sx", "swq", "saux", "supa", "supc", "sdnc", "sdnd", "sgst",
              "sgld", "sx2s", "sx2l", "sout", "sv", "ss", "st", "sg"):
        sems.sem(s)

    with ExitStack() as top:
        E = top.enter_context

        cols = E(nc.sbuf_tensor("cols", [128, 64], F32))
        amax_x = E(nc.sbuf_tensor("amax_x", [128, NTT], F32))
        sxq_c = E(nc.sbuf_tensor("sxq_c", [128, NTT], F32))
        m1_c = E(nc.sbuf_tensor("m1_c", [128, NTT], F32))
        m2_c = E(nc.sbuf_tensor("m2_c", [128, NTT], F32))
        ones_f = E(nc.sbuf_tensor("ones_f", [1, 128], F32))
        ones_col = E(nc.sbuf_tensor("ones_col", [128, 1], F32))
        sc_row = E(nc.sbuf_tensor("sc_row", [1, 4], F32))
        sup_inv = E(nc.sbuf_tensor("sup_inv", [128, 1], F32))
        sdn_inv = E(nc.sbuf_tensor("sdn_inv", [128, 1], F32))
        sup_b = E(nc.sbuf_tensor("sup_b", [128, 1], F32))
        sdn_b = E(nc.sbuf_tensor("sdn_b", [128, 1], F32))
        id_bf = E(nc.sbuf_tensor("id_bf", [128, 128], BF16))
        id_f32 = E(nc.sbuf_tensor("id_f32", [128, 128], F32))
        m1_bc = E(nc.sbuf_tensor("m1_bc", [128, T], F32))
        m1_row = E(nc.sbuf_tensor("m1_row", [1, T], F32))
        brow3 = E(nc.sbuf_tensor("brow3", [1, 384], F32))
        bcols = E(nc.sbuf_tensor("bcols", [128, 4], F32))
        ddrow = E(nc.sbuf_tensor("ddrow", [1, 512], F32))
        nsup2 = E(nc.sbuf_tensor("nsup2", [128, 1], F32))
        nsdn2 = E(nc.sbuf_tensor("nsdn2", [128, 1], F32))

        W = 49664
        arena = E(nc.sbuf_tensor("arena", [128, W], F32))

        def A(off, n):
            return arena[:, off:off + n]

        def AB(off, n):
            return arena[:, off:off + n].bitcast(BF16)

        xt = [A(0, D), A(D, D)]
        ln1t = AB(8192, 2048)
        xqt = AB(10240, 2048)
        ln1fm = AB(12288, 2048).rearrange("p (c t) -> p c t", t=128)
        ln2h = A(14336, 2048)
        ews = A(16384, ESUB * 128)
        up_cols = A(17408, 1024)
        dn_cols = A(22528, 1024)
        ddwkA = A(24576, 3072)
        ddwkC = A(18432, 3072)
        isleC = A(23552, D)
        w3 = AB(18432, 6144).rearrange("p (c n) -> p c n", n=384)
        wqsum = A(24576, NDC * 128).rearrange("p (c m) -> p c m", m=128)
        scst = [A(24576, SCAN_CH), A(24576 + SCAN_CH, SCAN_CH)]
        qkvT = A(28672, NTT * 384).rearrange("p (a n) -> p a n", n=384)
        bias_bc = A(31744, 384)
        attno = A(32128, 256)
        XQ_OFF = W - 16384
        xq_fm = AB(XQ_OFF, 16384).rearrange("p (c t) -> p c t", t=T)

        upst = [A(0, D), A(D, D)]
        ternu = [AB(8192, 2048), AB(10240, 2048)]
        reluws = A(12288, T)
        grng = [A(13312 + i * T, T) for i in range(4)]
        mrun = A(17408, T)
        scstd = [A(18432, SCAN_CH), A(18432 + SCAN_CH, SCAN_CH)]

        acc = A(0, NDC * T).rearrange("p (c t) -> p c t", t=T)
        hq = AB(32768, 8192).rearrange("p (c t) -> p c t", t=T)
        gld = [A(40960, T), A(40960 + T, T)]
        dnst = [A(43008, 2048), A(43008 + 2048, 2048)]
        ternd = [AB(47104, 1024), AB(48128, 1024)]
        isleD = A(49152, 512)

        x2t = [A(32768, D), A(32768 + D, D)]
        outt = [A(40960, D), A(40960 + D, D)]

        ps = E(nc.psum_tensor([128, 4096], F32))
        ps_tr = [ps[:, 0:256].bitcast(BF16).rearrange("p (c t) -> p c t", t=128),
                 ps[:, 512:768].bitcast(BF16).rearrange("p (c t) -> p c t", t=128)]
        ps_qkv = [ps[:, 1024:1408], ps[:, 1536:1920]]
        ps_misc = [ps[:, 2048:2560], ps[:, 2560:3072]]
        ps_up = [ps[:, 0:1024], ps[:, 1024:2048]]
        ps_m1 = [ps[:, 2048:2560], ps[:, 2560:3072]]
        ps_dn = [ps[:, 0:1024], ps[:, 1024:2048], ps[:, 2048:3072], ps[:, 3072:4096]]
        ps_tl = [ps[:, 0:512].rearrange("p (c t) -> p c t", t=128),
                 ps[:, 512:1024].rearrange("p (c t) -> p c t", t=128)]

        def dd_level(v, cur_hi, cur_lo, out_hi, out_lo, tmp1, tmp2, w):
            w2 = w // 2
            re = cur_hi.rearrange("p (n two) -> p n two", two=2)
            a, b = re[:, :, 0:1], re[:, :, 1:2]
            oh = out_hi.unsqueeze(2)
            t1 = tmp1.unsqueeze(2)
            t2 = tmp2.unsqueeze(2)
            v.tensor_tensor(out=oh, in0=a, in1=b, op=ALU.add)
            v.drain()
            v.tensor_tensor(out=t1, in0=oh, in1=a, op=ALU.subtract)
            v.drain()
            v.tensor_tensor(out=t2, in0=oh, in1=t1, op=ALU.subtract)
            v.drain()
            v.tensor_tensor(out=t2, in0=a, in1=t2, op=ALU.subtract)
            v.tensor_tensor(out=t1, in0=b, in1=t1, op=ALU.subtract)
            v.drain()
            v.tensor_tensor(out=t1, in0=t1, in1=t2, op=ALU.add)
            if cur_lo is None:
                v.drain()
                v.tensor_copy(out=out_lo, in_=tmp1)
            else:
                rl = cur_lo.rearrange("p (n two) -> p n two", two=2)
                v.tensor_tensor(out=out_lo.unsqueeze(2), in0=rl[:, :, 0:1],
                                in1=rl[:, :, 1:2], op=ALU.add)
                v.drain()
                v.tensor_tensor(out=out_lo, in0=out_lo, in1=tmp1, op=ALU.add)
            v.drain()

        def dd_tree_cols(v, src_cols, wk):
            hiA, loA = wk[:, 0:512], wk[:, 512:1024]
            hiB, loB = wk[:, 1024:1536], wk[:, 1536:2048]
            tmp1, tmp2 = wk[:, 2048:2560], wk[:, 2560:3072]
            w = 1024
            cur_hi, cur_lo = src_cols, None
            flip = 0
            while w > 1:
                w2 = w // 2
                oh, ol = (hiA, loA) if flip == 0 else (hiB, loB)
                dd_level(v, cur_hi, cur_lo, oh[:, 0:w2], ol[:, 0:w2],
                         tmp1[:, 0:w2], tmp2[:, 0:w2], w)
                cur_hi, cur_lo = oh[:, 0:w2], ol[:, 0:w2]
                w = w2
                flip ^= 1
            return cur_hi, cur_lo

        def dd_tree_row(v, out_scalar):
            cur_hi, cur_lo = ddrow[0:1, 0:128], ddrow[0:1, 128:256]
            w = 128
            flip = 0
            while w > 1:
                w2 = w // 2
                if flip == 0:
                    oh, ol = ddrow[0:1, 256:256 + w2], ddrow[0:1, 320:320 + w2]
                else:
                    oh, ol = ddrow[0:1, 0:w2], ddrow[0:1, 128:128 + w2]
                dd_level(v, cur_hi, cur_lo, oh, ol,
                         ddrow[0:1, 384:384 + w2], ddrow[0:1, 448:448 + w2], w)
                cur_hi, cur_lo = oh, ol
                w = w2
                flip ^= 1
            v.tensor_tensor(out=out_scalar, in0=cur_hi, in1=cur_lo, op=ALU.add)
            v.drain()

        def prog_sync(sy):
            for i in range(N_SCAN):
                if i < 2:
                    sems.wait(sy, "v:w3")
                else:
                    sems.wait(sy, f"s:scup:{i-2}")
                dd = sy.dma_start(
                    out=scst[i % 2],
                    in_=wup_d[i // 2][:, (i % 2) * SCAN_CH:(i % 2) * SCAN_CH + SCAN_CH])
                sems.inc("supa", dd, 16)
            if stop_after == 'A':
                return
            sems.wait_val(sy, "sx2s", 16 * NTT)
            sems.wait(sy, "v:phaseA")
            sems.wait(sy, "s:phaseA")
            sems.wait(sy, "t:phaseA")
            sems.wait(sy, "g:phaseA")
            for fc in range(NFC):
                if fc >= 2:
                    sems.wait(sy, f"v:uC:{fc-2}")
                dd = sy.dma_start(out=upst[fc % 2], in_=wup_d[fc])
                sems.inc("supc", dd, 16)
            if stop_after == 'C':
                return
            sems.wait(sy, "v:phaseC")
            sems.wait(sy, "s:phaseC")
            sems.wait(sy, "t:phaseC")
            for j in range(NFSB * NDG):
                fsb, dg = j // NDG, j % NDG
                for half in range(2):
                    jj = j * 2 + half
                    if jj >= 2:
                        sems.wait(sy, f"v:dC:{jj-2}")
                    dd = sy.dma_start(
                        out=dnst[jj % 2],
                        in_=wdn_d[fsb, dg][:, half * 2048:half * 2048 + 2048])
                    sems.inc("sdnd", dd, 16)

        def prog_gpsimd(g):
            g.memset(id_bf[:], 0.0)
            g.affine_select(out=id_bf[:], in_=id_bf[:], compare_op=ALU.not_equal,
                            fill=1.0, base=0, pattern=[[-1, 128]], channel_multiplier=1)
            g.memset(id_f32[:], 0.0)
            i2 = g.affine_select(out=id_f32[:], in_=id_f32[:], compare_op=ALU.not_equal,
                                 fill=1.0, base=0, pattern=[[-1, 128]],
                                 channel_multiplier=1)
            sems.inc_mark("sg", i2, "g:ident")
            for tt in range(2):
                dd = g.dma_start(out=xt[tt], in_=x_d[tt])
                sems.inc("sx", dd, 16)
            for h in range(32):
                src = wq_d[h].rearrange("(c p) m -> p c m", p=128)
                if h == 0:
                    dd = g.dma_start(out=wqsum, in_=src)
                else:
                    sems.wait_val(g, "swq", 16 * h)
                    dd = g.dma_start(out=wqsum, in_=src, accum_op=ALU.add)
                sems.inc("swq", dd, 16)
            dd = g.dma_start(out=w3[:, :, 128:256],
                             in_=wk_d.rearrange("c p m -> p c m"))
            sems.inc("saux", dd, 16)
            dd = g.dma_start(out=w3[:, :, 256:384],
                             in_=wv_d.rearrange("c p m -> p c m"))
            sems.inc("saux", dd, 16)
            dd = g.dma_start(out=cols[:, 32:64], in_=bq_d)
            sems.inc("saux", dd, 16)
            dd = g.dma_start(out=bcols[:, 1:2], in_=bk_d)
            sems.inc("saux", dd, 16)
            dd = g.dma_start(out=bcols[:, 2:3], in_=bv_d)
            sems.inc_mark("saux", dd, "g:aux", k=16)
            for tt in range(NTT):
                if tt >= 2:
                    dd = g.dma_start(out=xt[tt % 2], in_=x_d[tt])
                    sems.inc("sx", dd, 16)
                sems.wait(g, f"v:xq:{tt}")
                dd = g.dma_start(out=x2_d[tt], in_=xt[tt % 2])
                sems.inc("sx2s", dd, 16)
                if stop_after == 'A':
                    dd = g.dma_start(out=out_d[tt], in_=xt[tt % 2])
                    sems.inc("sout", dd, 16)
            sems.alias("g:phaseA", "sx2s")
            if stop_after == 'A':
                if dbg:
                    sems.wait(g, "v:phaseA")
                    dd = g.dma_start(out=dbg_qkv,
                                     in_=qkvT.rearrange("p a n -> p (a n)"))
                    sems.inc("sout", dd, 16)
                    dd = g.dma_start(out=dbg_misc[:, 0:8], in_=amax_x[:])
                    sems.inc("sout", dd, 16)
                    dd = g.dma_start(out=dbg_misc[:, 8:16], in_=sxq_c[:])
                    sems.inc("sout", dd, 16)
                    dd = g.dma_start(out=dbg_misc[:, 16:19], in_=bcols[:, 0:3])
                    sems.inc("sout", dd, 16)
                    dd = g.dma_start(out=dbg_misc[:, 24:27],
                                     in_=bias_bc[:, 0:3])
                    sems.inc("sout", dd, 16)
                    dd = g.dma_start(out=dbg_w3,
                                     in_=w3.rearrange("p c n -> p (c n)")
                                     .bitcast(F32))
                    sems.inc("sout", dd, 16)
                    dd = g.dma_start(out=dbg_ln1, in_=ln1t.bitcast(F32))
                    sems.inc("sout", dd, 16)
                    dd = g.dma_start(out=dbg_fm,
                                     in_=ln1fm.rearrange("p c t -> p (c t)")
                                     .bitcast(F32))
                    sems.inc("sout", dd, 16)
                    dd = g.dma_start(out=dbg_cols, in_=cols[:])
                    sems.inc("sout", dd, 16)
                    sems.wait_val(g, "sout", 16 * (NTT + 9))
                else:
                    sems.wait_val(g, "sout", 16 * NTT)
                return
            for fc in range(NFC):
                sems.wait(g, f"s:gsq:{fc}")
                dd = g.dma_start(out=g_d[fc], in_=grng[fc % 4])
                sems.inc("sgst", dd, 16)
            sems.alias("g:phaseC", "sgst")
            if stop_after == 'C':
                if dbg:
                    sems.wait(g, "v:phaseC")
                    for gi, fc in enumerate((0, 64, 127)):
                        dd = g.dma_start(out=dbg_g[gi], in_=g_d[fc])
                        sems.inc("sout", dd, 16)
                    dd = g.dma_start(out=dbg_m[:, 0:8], in_=m1_c[:])
                    sems.inc("sout", dd, 16)
                    dd = g.dma_start(out=dbg_m[:, 8:16], in_=m2_c[:])
                    sems.inc("sout", dd, 16)
                    dd = g.dma_start(out=dbg_m[:, 16:24], in_=amax_x[:])
                    sems.inc("sout", dd, 16)
                    dd = g.dma_start(allow_slow_non_contiguous=True, out=dbg_m[:, 24:25], in_=sup_b[:])
                    sems.inc("sout", dd, 16)
                    dd = g.dma_start(allow_slow_non_contiguous=True, out=dbg_m[:, 25:26], in_=sdn_b[:])
                    sems.inc("sout", dd, 16)
                    dd = g.dma_start(allow_slow_non_contiguous=True, out=dbg_m[:, 26:27], in_=sup_inv[:])
                    sems.inc("sout", dd, 16)
                    dd = g.dma_start(allow_slow_non_contiguous=True, out=dbg_m[:, 27:28], in_=sdn_inv[:])
                    sems.inc("sout", dd, 16)
                    dd = g.dma_start(out=dbg_m[:, 28:36], in_=m1_bc[:, 0:8])
                    sems.inc("sout", dd, 16)
                    sems.wait_val(g, "sout", 16 * 11)
                return
            sems.wait(g, "v:phaseC")
            sems.wait(g, "s:phaseC")
            sems.wait(g, "t:phaseC")
            for idx in range(NFC):
                if idx >= 2:
                    sems.wait(g, f"v:hq:{idx-2}")
                dd = g.dma_start(out=gld[idx % 2], in_=g_d[idx])
                sems.inc("sgld", dd, 16)
            if stop_after == 'D':
                if dbg:
                    sems.wait(g, "v:D:done")
                    for ai, dc in enumerate((0, 1, 16, 31)):
                        dd = g.dma_start(out=dbg_acc[ai], in_=acc[:, dc, :])
                        sems.inc("sout", dd, 16)
                    sems.wait_val(g, "sout", 16 * 4)
                return
            sems.wait(g, "t:D:done")
            sems.wait(g, "v:D:done")
            for tt in range(NTT):
                if tt >= 2:
                    sems.wait(g, f"v:tail:{tt-2}")
                    dd = g.dma_start(out=out_d[tt - 2], in_=outt[tt % 2])
                    sems.inc("sout", dd, 16)
                dd = g.dma_start(out=x2t[tt % 2], in_=x2_d[tt])
                sems.inc("sx2l", dd, 16)
            for tt in (NTT - 2, NTT - 1):
                sems.wait(g, f"v:tail:{tt}")
                dd = g.dma_start(out=out_d[tt], in_=outt[tt % 2])
                sems.inc("sout", dd, 16)
            sems.wait_val(g, "sout", 16 * NTT)

        def prog_vector(v):
            v.memset(ones_f[:], 1.0)
            v.memset(ones_col[:], 1.0)
            sems.wait_val(v, "swq", 16 * 32)
            ins = v.tensor_copy(out=w3[:, :, 0:128], in_=wqsum)
            sems.inc_mark("sv", ins, "v:w3")
            sems.wait(v, "g:aux")
            ins = v.tensor_reduce(out=bcols[:, 0:1].unsqueeze(2),
                                  in_=cols[:, 32:64].unsqueeze(1), axis=AX.X,
                                  op=ALU.add)
            sems.inc_mark("sv", ins, "v:bcols")
            sems.wait(v, "t:btr")
            ins = v.tensor_copy(out=brow3[0:1, :], in_=ps_misc[0][0:1, 0:384])
            sems.inc_mark("sv", ins, "v:brow")
            sems.wait(v, "t:biasbc")
            ins = v.tensor_copy(out=bias_bc, in_=ps_qkv[0])
            sems.inc_mark("sv", ins, "v:biasbc")

            for tt in range(NTT):
                xs = xt[tt % 2]
                sems.wait_val(v, "sx", 16 * (tt + 1))
                c_mean = cols[:, 8 * tt + 0:8 * tt + 1]
                c_tmp = cols[:, 8 * tt + 1:8 * tt + 2]
                c_var = cols[:, 8 * tt + 2:8 * tt + 3]
                c_rstd = cols[:, 8 * tt + 3:8 * tt + 4]
                c_m2 = cols[:, 8 * tt + 4:8 * tt + 5]

                def stats(var_evt, sq_evt):
                    sems.wait(v, sq_evt)
                    v.tensor_tensor(out=c_tmp, in0=c_tmp, in1=c_m2, op=ALU.add)
                    v.tensor_scalar(out=c_mean, in0=c_mean, scalar1=1.0 / D,
                                    scalar2=None, op0=ALU.mult, op1=ALU.bypass)
                    v.drain()
                    v.scalar_tensor_tensor(out=c_m2, in0=c_mean, scalar=0.0,
                                           in1=c_mean, op0=ALU.add, op1=ALU.mult)
                    v.tensor_scalar(out=c_var, in0=c_tmp, scalar1=1.0 / D,
                                    scalar2=None, op0=ALU.mult, op1=ALU.bypass)
                    v.drain()
                    ins = v.tensor_scalar(out=c_var, in0=c_var, scalar1=c_m2,
                                          scalar2=EPS, op0=ALU.subtract, op1=ALU.add)
                    sems.inc_mark("sv", ins, var_evt)

                stats(f"v:var1:{tt}", f"s:sq1:{tt}")
                sems.wait(v, f"s:rstd1:{tt}")
                for hh in range(2):
                    sl = slice(2048 * hh, 2048 * hh + 2048)
                    v.tensor_scalar(out=ln2h, in0=xs[:, sl], scalar1=c_mean,
                                    scalar2=c_rstd, op0=ALU.subtract, op1=ALU.mult)
                    ins = v.tensor_copy(out=ln1t[:, sl], in_=ln2h)
                sems.inc_mark("sv", ins, f"v:ln1:{tt}")
                for gp in range(8):
                    sems.wait(v, f"t:tr1:{tt}:{gp}")
                    ins = v.tensor_copy(out=ln1fm[:, 4 * gp:4 * gp + 4, :],
                                        in_=ps_tr[gp % 2])
                    sems.inc_mark("sv", ins, f"v:fm1:{tt}:{gp}")
                sems.wait(v, f"t:qkv:{tt}")
                ins = v.tensor_tensor(out=qkvT[:, tt, :], in0=ps_qkv[tt % 2],
                                      in1=bias_bc, op=ALU.add)
                sems.inc_mark("sv", ins, f"v:qkv:{tt}")
                qs = qkvT[:, tt, 0:128]
                ks = qkvT[:, tt, 128:256]
                vs = qkvT[:, tt, 256:384]
                c_aq = cols[:, 8 * tt + 5:8 * tt + 6]
                c_ak = cols[:, 8 * tt + 6:8 * tt + 7]
                c_cs = cols[:, 8 * tt + 7:8 * tt + 8]
                v.tensor_reduce(out=c_aq, in_=qs, axis=AX.X, op=ALU.max,
                                apply_absolute_value=True)
                v.tensor_reduce(out=c_ak, in_=ks, axis=AX.X, op=ALU.max,
                                apply_absolute_value=True)
                v.drain()
                v.scalar_tensor_tensor(out=c_cs, in0=c_aq, scalar=0.0, in1=c_ak,
                                       op0=ALU.add, op1=ALU.mult)
                v.drain()
                ins = v.tensor_scalar(out=c_cs, in0=c_cs, scalar1=-S_ATTN,
                                      scalar2=None, op0=ALU.mult, op1=ALU.bypass)
                sems.inc_mark("sv", ins, f"v:attnc:{tt}")
                o_num = attno[:, 0:128]
                o_den = attno[:, 128:256]
                e3 = ews.rearrange("p (a b) -> p a b", b=128)
                for su in range(N_ESUB):
                    dlo = su * ESUB
                    ins = v.tensor_tensor(
                        out=e3,
                        in0=qs[:, dlo:dlo + ESUB].unsqueeze(2)
                            .broadcast_to([128, ESUB, 128]),
                        in1=ks.unsqueeze(1).broadcast_to([128, ESUB, 128]),
                        op=ALU.mult)
                    sems.inc_mark("sv", ins, f"v:S:{tt}:{su}")
                    sems.wait(v, f"s:exp:{tt}:{su}")
                    v.tensor_reduce(out=o_den[:, dlo:dlo + ESUB].unsqueeze(2),
                                    in_=e3, axis=AX.X, op=ALU.add)
                    v.tensor_tensor(out=e3, in0=e3,
                                    in1=vs.unsqueeze(1).broadcast_to([128, ESUB, 128]),
                                    op=ALU.mult)
                    ins = v.tensor_reduce(out=o_num[:, dlo:dlo + ESUB].unsqueeze(2),
                                          in_=e3, axis=AX.X, op=ALU.add)
                    sems.inc_mark("sv", ins, f"v:esub:{tt}:{su}")
                v.reciprocal(out=o_den, in_=o_den)
                v.tensor_tensor(out=o_num, in0=o_num, in1=o_den, op=ALU.mult)
                x3 = xs.rearrange("p (a b) -> p a b", b=32)
                ins = v.tensor_tensor(out=x3, in0=x3,
                                      in1=o_num.unsqueeze(2).broadcast_to([128, 128, 32]),
                                      op=ALU.add)
                sems.inc_mark("sv", ins, f"v:x2:{tt}")
                stats(f"v:var2:{tt}", f"s:sq2:{tt}")
                sems.wait(v, f"s:rstd2:{tt}")
                for hh in range(2):
                    sl = slice(2048 * hh, 2048 * hh + 2048)
                    v.tensor_scalar(out=ln2h, in0=xs[:, sl], scalar1=c_mean,
                                    scalar2=c_rstd, op0=ALU.subtract, op1=ALU.mult)
                    v.tensor_reduce(out=c_tmp if hh == 0 else c_m2, in_=ln2h,
                                    axis=AX.X, op=ALU.max, apply_absolute_value=True)
                v.drain()
                v.tensor_tensor(out=amax_x[:, tt:tt + 1], in0=c_tmp, in1=c_m2,
                                op=ALU.max)
                v.drain()
                v.tensor_scalar(out=c_tmp, in0=amax_x[:, tt:tt + 1], scalar1=EPS,
                                scalar2=None, op0=ALU.max, op1=ALU.bypass)
                v.drain()
                v.reciprocal(out=sxq_c[:, tt:tt + 1], in_=c_tmp)
                v.drain()
                v.tensor_scalar(out=sxq_c[:, tt:tt + 1], in0=sxq_c[:, tt:tt + 1],
                                scalar1=127.0, scalar2=None, op0=ALU.mult,
                                op1=ALU.bypass)
                v.drain()
                for hh in range(2):
                    sl = slice(2048 * hh, 2048 * hh + 2048)
                    v.tensor_scalar(out=ln2h, in0=xs[:, sl], scalar1=c_mean,
                                    scalar2=c_rstd, op0=ALU.subtract, op1=ALU.mult)
                    v.tensor_scalar(out=ln2h, in0=ln2h, scalar1=sxq_c[:, tt:tt + 1],
                                    scalar2=MAGIC, op0=ALU.mult, op1=ALU.add)
                    v.tensor_scalar(out=ln2h, in0=ln2h, scalar1=MAGIC,
                                    scalar2=None, op0=ALU.subtract, op1=ALU.bypass)
                    ins = v.tensor_copy(out=xqt[:, sl], in_=ln2h)
                sems.inc_mark("sv", ins, f"v:xq:{tt}")
                for gp in range(8):
                    sems.wait(v, f"t:tr2:{tt}:{gp}")
                    ins = v.tensor_copy(
                        out=xq_fm[:, 4 * gp:4 * gp + 4, 128 * tt:128 * tt + 128],
                        in_=ps_tr[gp % 2])
                    sems.inc_mark("sv", ins, f"v:fm2:{tt}:{gp}")
                for i in range(32 * tt, 32 * tt + 32):
                    sems.wait_val(v, "supa", 16 * (i + 1))
                    if i >= 2:
                        sems.wait(v, f"s:scup:{i-2}")
                    ins = v.tensor_scalar(out=scst[i % 2], in0=scst[i % 2],
                                          scalar1=-1.0, scalar2=1.0,
                                          op0=ALU.max, op1=ALU.min)
                    sems.inc_mark("sv", ins, f"v:scupc:{i}")
            sems.wait(v, "s:scupall")
            dhi, dlo = dd_tree_cols(v, up_cols, ddwkA)
            ins = v.tensor_copy(out=ddwkA[:, 3071:3072], in_=dhi)
            sems.inc_mark("sv", ins, "v:ddup1")
            sems.wait(v, "t:ddup")
            ins = v.tensor_copy(out=ddrow[0:1, 0:128], in_=ps_misc[0][0:1, 0:128])
            ins = v.tensor_copy(out=ddrow[0:1, 128:256], in_=ps_misc[0][0:1, 128:256])
            v.drain()
            dd_tree_row(v, sc_row[0:1, 0:1])
            v.tensor_scalar(out=sc_row[0:1, 0:1], in0=sc_row[0:1, 0:1],
                            scalar1=1.0 / NW, scalar2=EPS, op0=ALU.mult, op1=ALU.add)
            v.drain()
            ins = v.reciprocal(out=sc_row[0:1, 1:2], in_=sc_row[0:1, 0:1])
            v.drain()
            ins = v.tensor_copy(out=sc_row[0:1, 1:2], in_=sc_row[0:1, 1:2])
            sems.inc_mark("sv", ins, "v:suprow")
            sems.wait(v, "t:supbc")
            ins = v.tensor_copy(out=sup_b[:], in_=ps_misc[1][:, 0:1])
            v.drain()
            v.tensor_scalar(out=sup_inv[:], in0=sup_b[:], scalar1=0.5, scalar2=None,
                            op0=ALU.mult, op1=ALU.bypass)
            v.tensor_scalar(out=nsup2[:], in0=sup_b[:], scalar1=-0.5, scalar2=None,
                            op0=ALU.mult, op1=ALU.bypass)
            v.drain()
            ins = v.tensor_copy(out=bcols[:, 3:4], in_=ones_col[:])
            sems.inc_mark("sv", ins, "v:phaseA")
            if stop_after == 'A':
                return

            v.memset(mrun, -3.0e38)
            for fc in range(NFC):
                sems.wait_val(v, "supc", 16 * (fc + 1))
                if fc >= 2:
                    sems.wait(v, f"t:up:{fc-2}")
                ins = v.tensor_scalar(out=isleC, in0=upst[fc % 2],
                                      scalar1=nsup2[:, 0:1], scalar2=None,
                                      op0=ALU.is_le, op1=ALU.bypass)
                sems.inc_mark("sv", ins, f"v:uA:{fc}")
                ins = v.scalar_tensor_tensor(out=ternu[fc % 2], in0=upst[fc % 2],
                                             scalar=sup_inv[:, 0:1], in1=isleC,
                                             op0=ALU.is_ge, op1=ALU.subtract)
                sems.inc_mark("sv", ins, f"v:uC:{fc}")
                sems.wait(v, f"s:gsqr:{fc}")
                ins = v.tensor_tensor(out=mrun, in0=mrun, in1=reluws,
                                      op=ALU.max)
                sems.inc_mark("sv", ins, f"v:upev:{fc}")
                for i in (2 * fc, 2 * fc + 1):
                    sems.wait_val(v, "sdnc", 16 * (i + 1))
                    if i >= 2:
                        sems.wait(v, f"s:scdn:{i-2}")
                    ins = v.tensor_scalar(out=scstd[i % 2], in0=scstd[i % 2],
                                          scalar1=-1.0, scalar2=1.0,
                                          op0=ALU.max, op1=ALU.min)
                    sems.inc_mark("sv", ins, f"v:scdnc:{i}")
            sems.wait(v, "s:scdnall")
            dhi, dlo = dd_tree_cols(v, dn_cols, ddwkC)
            ins = v.tensor_copy(out=ddwkC[:, 3071:3072], in_=dhi)
            sems.inc_mark("sv", ins, "v:dddn1")
            sems.wait(v, "t:dddn")
            ins = v.tensor_copy(out=ddrow[0:1, 0:128], in_=ps_misc[0][0:1, 0:128])
            ins = v.tensor_copy(out=ddrow[0:1, 128:256], in_=ps_misc[0][0:1, 128:256])
            v.drain()
            dd_tree_row(v, sc_row[0:1, 2:3])
            v.tensor_scalar(out=sc_row[0:1, 2:3], in0=sc_row[0:1, 2:3],
                            scalar1=1.0 / NW, scalar2=EPS, op0=ALU.mult, op1=ALU.add)
            v.drain()
            ins = v.reciprocal(out=sc_row[0:1, 3:4], in_=sc_row[0:1, 2:3])
            v.drain()
            ins = v.tensor_copy(out=sc_row[0:1, 3:4], in_=sc_row[0:1, 3:4])
            sems.inc_mark("sv", ins, "v:sdnrow")
            sems.wait(v, "t:sdnbc")
            ins = v.tensor_copy(out=sdn_b[:], in_=ps_misc[1][:, 0:1])
            v.drain()
            v.tensor_scalar(out=sdn_inv[:], in0=sdn_b[:], scalar1=0.5, scalar2=None,
                            op0=ALU.mult, op1=ALU.bypass)
            v.tensor_scalar(out=nsdn2[:], in0=sdn_b[:], scalar1=-0.5, scalar2=None,
                            op0=ALU.mult, op1=ALU.bypass)
            v.drain()
            ins = v.tensor_copy(out=bcols[:, 3:4], in_=ones_col[:])
            sems.inc_mark("sv", ins, "v:sdnb")
            for tt in range(NTT):
                sems.wait(v, f"t:trm:{tt}")
                c_g = cols[:, 8 * tt + 0:8 * tt + 1]
                c_c1 = cols[:, 8 * tt + 1:8 * tt + 2]
                c_ah = cols[:, 8 * tt + 2:8 * tt + 3]
                c_sh = cols[:, 8 * tt + 3:8 * tt + 4]
                v.tensor_reduce(out=c_g, in_=ps_misc[tt % 2][:, 0:128], axis=AX.X,
                                op=ALU.max)
                v.drain()
                v.tensor_scalar(out=c_g, in0=c_g, scalar1=0.0, scalar2=None,
                                op0=ALU.max, op1=ALU.bypass)
                v.tensor_scalar(out=c_c1, in0=amax_x[:, tt:tt + 1],
                                scalar1=sup_b[:, 0:1], scalar2=1.0 / 127.0,
                                op0=ALU.mult, op1=ALU.mult)
                v.drain()
                v.scalar_tensor_tensor(out=c_g, in0=c_g, scalar=0.0, in1=c_g,
                                       op0=ALU.add, op1=ALU.mult)
                v.scalar_tensor_tensor(out=c_c1, in0=c_c1, scalar=0.0, in1=c_c1,
                                       op0=ALU.add, op1=ALU.mult)
                v.drain()
                v.scalar_tensor_tensor(out=c_ah, in0=c_g, scalar=0.0, in1=c_c1,
                                       op0=ALU.add, op1=ALU.mult)
                v.drain()
                v.tensor_scalar(out=c_ah, in0=c_ah, scalar1=EPS, scalar2=None,
                                op0=ALU.max, op1=ALU.bypass)
                v.drain()
                v.reciprocal(out=c_sh, in_=c_ah)
                v.drain()
                v.tensor_scalar(out=c_sh, in0=c_sh, scalar1=127.0, scalar2=None,
                                op0=ALU.mult, op1=ALU.bypass)
                v.drain()
                ins = v.scalar_tensor_tensor(out=m1_c[:, tt:tt + 1], in0=c_c1,
                                             scalar=0.0, in1=c_sh,
                                             op0=ALU.add, op1=ALU.mult)
                sems.inc_mark("sv", ins, f"v:m1c:{tt}")
                v.tensor_scalar(out=m2_c[:, tt:tt + 1], in0=c_ah,
                                scalar1=1.0 / 127.0, scalar2=sdn_b[:, 0:1],
                                op0=ALU.mult, op1=ALU.mult)
                sems.wait(v, f"t:trm1:{tt}")
                ins = v.tensor_copy(out=m1_row[0:1, 128 * tt:128 * tt + 128],
                                    in_=ps_m1[tt % 2][0:1, 0:128])
                sems.inc_mark("sv", ins, f"v:m1r:{tt}")
            sems.wait(v, "t:m1bc")
            ins = v.tensor_copy(out=m1_bc[:, 0:512], in_=ps_up[0][:, 0:512])
            ins = v.tensor_copy(out=m1_bc[:, 512:1024], in_=ps_up[1][:, 0:512])
            sems.inc_mark("sv", ins, "v:phaseC")
            if stop_after == 'C':
                return

            for fsb in range(NFSB):
                for jj in range(FC_PER_FSB):
                    idx = fsb * FC_PER_FSB + jj
                    sems.wait_val(v, "sgld", 16 * (idx + 1))
                    if fsb > 0 and jj == 0:
                        sems.wait(v, f"t:dn:{(fsb * NDG - 1) * 2 + 1}")
                    ins = v.tensor_tensor(out=gld[idx % 2], in0=gld[idx % 2],
                                          in1=m1_bc[:], op=ALU.mult)
                    v.tensor_scalar(out=gld[idx % 2], in0=gld[idx % 2],
                                    scalar1=MAGIC, scalar2=MAGIC,
                                    op0=ALU.add, op1=ALU.subtract)
                    ins = v.tensor_copy(out=hq[:, jj, :], in_=gld[idx % 2])
                    sems.inc_mark("sv", ins, f"v:hq:{idx}")
                for dg in range(NDG):
                    j = fsb * NDG + dg
                    for half in range(2):
                        jj2 = j * 2 + half
                        sems.wait_val(v, "sdnd", 16 * (jj2 + 1))
                        if jj2 >= 2:
                            sems.wait(v, f"t:dn:{jj2-2}")
                        for qq in range(4):
                            sq = slice(512 * qq, 512 * qq + 512)
                            v.tensor_scalar(out=isleD, in0=dnst[jj2 % 2][:, sq],
                                            scalar1=nsdn2[:, 0:1], scalar2=None,
                                            op0=ALU.is_le, op1=ALU.bypass)
                            ins = v.scalar_tensor_tensor(
                                out=ternd[jj2 % 2][:, sq],
                                in0=dnst[jj2 % 2][:, sq],
                                scalar=sdn_inv[:, 0:1], in1=isleD,
                                op0=ALU.is_ge, op1=ALU.subtract)
                            if qq == 0:
                                sems.inc_mark("sv", ins, f"v:dA:{jj2}")
                        sems.inc_mark("sv", ins, f"v:dC:{jj2}")
                    sems.wait(v, f"t:dn:{j*2+1}")
                    for dcl in range(2):
                        dst = acc[:, 2 * dg + dcl, :]
                        pp = ps_dn[(dg % 2) * 2 + dcl]
                        if fsb == 0:
                            ins = v.tensor_copy(out=dst, in_=pp)
                        else:
                            ins = v.tensor_tensor(out=dst, in0=dst, in1=pp,
                                                  op=ALU.add)
                    sems.inc_mark("sv", ins, f"v:dev:{j}")
            sems.mark("v:D:done", "sv")
            if stop_after == 'D':
                return
            for tt in range(NTT):
                sems.wait_val(v, "sx2l", 16 * (tt + 1))
                if tt >= 2:
                    sems.wait_val(v, "sout", 16 * (tt - 1))
                for gp in range(8):
                    sems.wait(v, f"t:tlt:{tt}:{gp}")
                    ins = v.scalar_tensor_tensor(
                        out=outt[tt % 2][:, 512 * gp:512 * gp + 512],
                        in0=ps_tl[gp % 2].rearrange("p c t -> p (c t)"),
                        scalar=m2_c[:, tt:tt + 1],
                        in1=x2t[tt % 2][:, 512 * gp:512 * gp + 512],
                        op0=ALU.mult, op1=ALU.add)
                    sems.inc_mark("sv", ins, f"v:tailg:{tt}:{gp}")
                sems.mark(f"v:tail:{tt}", "sv")

        def prog_scalar(s):
            scup = 0
            for tt in range(NTT):
                c_var = cols[:, 8 * tt + 2:8 * tt + 3]
                c_rstd = cols[:, 8 * tt + 3:8 * tt + 4]
                c_cs = cols[:, 8 * tt + 7:8 * tt + 8]
                c_mean_s = cols[:, 8 * tt + 0:8 * tt + 1]
                c_s1 = cols[:, 8 * tt + 1:8 * tt + 2]
                c_s2 = cols[:, 8 * tt + 4:8 * tt + 5]
                xs = xt[tt % 2]
                sems.wait_val(s, "sx", 16 * (tt + 1))
                if tt >= 1:
                    sems.wait(s, f"v:xq:{tt-1}")
                s.activation(out=xs, in_=xs, func=AF.Identity, accum_out=c_mean_s)
                s.activation(out=ln2h, in_=xs[:, 0:2048], func=AF.Square,
                             accum_out=c_s1)
                s.activation(out=ln2h, in_=xs[:, 2048:4096], func=AF.Square,
                             accum_out=c_s2)
                s.drain()
                ins = s.activation(out=bcols[:, 3:4], in_=bcols[:, 3:4], func=AF.Abs)
                sems.inc_mark("ss", ins, f"s:sq1:{tt}")
                sems.wait(s, f"v:var1:{tt}")
                s.activation(out=c_rstd, in_=c_var, func=AF.Ln)
                s.drain()
                ins = s.activation(out=c_rstd, in_=c_rstd, func=AF.Exp, scale=-0.5)
                sems.inc_mark("ss", ins, f"s:rstd1:{tt}")
                for su in range(N_ESUB):
                    sems.wait(s, f"v:S:{tt}:{su}")
                    ins = s.activation(out=ews, in_=ews, func=AF.Exp,
                                       scale=S_ATTN, bias=c_cs)
                    sems.inc_mark("ss", ins, f"s:exp:{tt}:{su}")
                sems.wait(s, f"v:x2:{tt}")
                s.activation(out=xs, in_=xs, func=AF.Identity, accum_out=c_mean_s)
                s.activation(out=ln2h, in_=xs[:, 0:2048], func=AF.Square,
                             accum_out=c_s1)
                s.activation(out=ln2h, in_=xs[:, 2048:4096], func=AF.Square,
                             accum_out=c_s2)
                s.drain()
                ins = s.activation(out=bcols[:, 3:4], in_=bcols[:, 3:4], func=AF.Abs)
                sems.inc_mark("ss", ins, f"s:sq2:{tt}")
                sems.wait(s, f"v:var2:{tt}")
                s.activation(out=c_rstd, in_=c_var, func=AF.Ln)
                s.drain()
                ins = s.activation(out=c_rstd, in_=c_rstd, func=AF.Exp, scale=-0.5)
                sems.inc_mark("ss", ins, f"s:rstd2:{tt}")
                for i in range(32 * tt, 32 * tt + 32):
                    sems.wait(s, f"v:scupc:{i}")
                    for q in range(4):
                        ins = s.activation(out=scst[i % 2][:, 512 * q:512 * q + 512],
                                           in_=scst[i % 2][:, 512 * q:512 * q + 512],
                                           func=AF.Abs,
                                           accum_out=up_cols[:, 4 * i + q:4 * i + q + 1])
                    sems.inc_mark("ss", ins, f"s:scup:{i}")
                    scup += 1
            s.drain()
            ins = s.activation(out=bcols[:, 3:4], in_=bcols[:, 3:4], func=AF.Abs)
            sems.inc_mark("ss", ins, "s:scupall")
            sems.alias("s:phaseA", "ss")
            if stop_after == 'A':
                return
            sems.wait(s, "t:qkv:7")
            for fc in range(NFC):
                sems.wait(s, f"t:up:{fc}")
                if fc >= 1:
                    sems.wait(s, f"v:upev:{fc-1}")
                if fc >= 4:
                    sems.wait_val(s, "sgst", 16 * (fc - 3))
                ins = s.activation(out=reluws, in_=ps_up[fc % 2], func=AF.Relu)
                sems.inc_mark("ss", ins, f"s:gsqr:{fc}")
                ins = s.activation(out=grng[fc % 4], in_=reluws, func=AF.Square)
                sems.inc_mark("ss", ins, f"s:gsq:{fc}")
                for i in (2 * fc, 2 * fc + 1):
                    if i >= 2:
                        sems.wait(s, f"s:scdn:{i-2}")
                    jch = i // 2
                    dd = s.dma_start(
                        out=scstd[i % 2],
                        in_=wdn_d[jch // NDG, jch % NDG]
                        [:, (i % 2) * SCAN_CH:(i % 2) * SCAN_CH + SCAN_CH])
                    sems.inc("sdnc", dd, 16)
                for i in (2 * fc, 2 * fc + 1):
                    sems.wait(s, f"v:scdnc:{i}")
                    for q in range(4):
                        ins = s.activation(out=scstd[i % 2][:, 512 * q:512 * q + 512],
                                           in_=scstd[i % 2][:, 512 * q:512 * q + 512],
                                           func=AF.Abs,
                                           accum_out=dn_cols[:, 4 * i + q:4 * i + q + 1])
                    sems.inc_mark("ss", ins, f"s:scdn:{i}")
            s.drain()
            ins = s.activation(out=bcols[:, 3:4], in_=bcols[:, 3:4], func=AF.Abs)
            sems.inc_mark("ss", ins, "s:scdnall")
            sems.alias("s:phaseC", "ss")
            if stop_after == 'C':
                return

        def prog_tensor(t):
            sems.wait(t, "v:bcols")
            sems.wait(t, "g:ident")
            for bi in range(3):
                ins = t.transpose(out=ps_misc[0][0:1, 128 * bi:128 * bi + 128],
                                  in_=bcols[:, bi:bi + 1], identity=id_f32[:])
            sems.inc_mark("st", ins, "t:btr")
            sems.wait(t, "v:brow")
            ins = t.matmul(out=ps_qkv[0], lhsT=ones_f[:], rhs=brow3[0:1, :],
                           start=True, stop=True)
            sems.inc_mark("st", ins, "t:biasbc")
            for tt in range(NTT):
                sems.wait(t, f"v:ln1:{tt}")
                for gp in range(8):
                    if not (tt == 0 and gp <= 1):
                        if gp >= 2:
                            sems.wait(t, f"v:fm1:{tt}:{gp-2}")
                        else:
                            sems.wait(t, f"v:fm2:{tt-1}:{6+gp}")
                    for k in range(4):
                        cidx = 4 * gp + k
                        ins = t.transpose(out=ps_tr[gp % 2][:, k, :],
                                          in_=ln1t[:, 128 * cidx:128 * cidx + 128],
                                          identity=id_bf[:])
                    sems.inc_mark("st", ins, f"t:tr1:{tt}:{gp}")
                sems.wait(t, f"v:fm1:{tt}:7")
                if tt == 0:
                    sems.wait(t, "v:biasbc")
                else:
                    sems.wait(t, f"v:qkv:{tt-1}")
                for dc in range(NDC):
                    ins = t.matmul(out=ps_qkv[tt % 2], lhsT=ln1fm[:, dc, :],
                                   rhs=w3[:, dc, :], start=(dc == 0),
                                   stop=(dc == NDC - 1))
                sems.inc_mark("st", ins, f"t:qkv:{tt}")
                sems.wait(t, f"v:xq:{tt}")
                for gp in range(8):
                    if gp >= 2:
                        sems.wait(t, f"v:fm2:{tt}:{gp-2}")
                    else:
                        sems.wait(t, f"v:fm1:{tt}:{6+gp}")
                    for k in range(4):
                        cidx = 4 * gp + k
                        ins = t.transpose(out=ps_tr[gp % 2][:, k, :],
                                          in_=xqt[:, 128 * cidx:128 * cidx + 128],
                                          identity=id_bf[:])
                    sems.inc_mark("st", ins, f"t:tr2:{tt}:{gp}")
            sems.wait(t, "v:ddup1")
            ins = t.transpose(out=ps_misc[0][0:1, 0:128],
                              in_=ddwkA[:, 1024:1025], identity=id_f32[:])
            ins = t.transpose(out=ps_misc[0][0:1, 128:256],
                              in_=ddwkA[:, 1536:1537], identity=id_f32[:])
            sems.inc_mark("st", ins, "t:ddup")
            sems.wait(t, "v:suprow")
            ins = t.matmul(out=ps_misc[1][:, 0:2], lhsT=ones_f[:],
                           rhs=sc_row[0:1, 0:2], start=True, stop=True)
            sems.inc_mark("st", ins, "t:supbc")
            sems.alias("t:phaseA", "st")
            if stop_after == 'A':
                return
            for fc in range(NFC):
                sems.wait(t, f"v:uC:{fc}")
                if fc >= 2:
                    sems.wait(t, f"s:gsqr:{fc-2}")
                tern3 = ternu[fc % 2].rearrange("p (c m) -> p c m", m=128)
                for dc in range(NDC):
                    for th in range(2):
                        ins = t.matmul(out=ps_up[fc % 2][:, 512 * th:512 * th + 512],
                                       lhsT=tern3[:, dc, :],
                                       rhs=xq_fm[:, dc, 512 * th:512 * th + 512],
                                       start=(dc == 0), stop=(dc == NDC - 1))
                sems.inc_mark("st", ins, f"t:up:{fc}")
            sems.wait(t, "v:dddn1")
            ins = t.transpose(out=ps_misc[0][0:1, 0:128],
                              in_=ddwkC[:, 1024:1025], identity=id_f32[:])
            ins = t.transpose(out=ps_misc[0][0:1, 128:256],
                              in_=ddwkC[:, 1536:1537], identity=id_f32[:])
            sems.inc_mark("st", ins, "t:dddn")
            sems.wait(t, "v:sdnrow")
            ins = t.matmul(out=ps_misc[1][:, 0:2], lhsT=ones_f[:],
                           rhs=sc_row[0:1, 2:4], start=True, stop=True)
            sems.inc_mark("st", ins, "t:sdnbc")
            for tt in range(NTT):
                if tt == 0:
                    sems.wait(t, f"v:upev:{NFC-1}")
                    sems.wait(t, "v:sdnb")
                if tt >= 2:
                    sems.wait(t, f"v:m1c:{tt-2}")
                ins = t.transpose(out=ps_misc[tt % 2][:, 0:128],
                                  in_=mrun[:, 128 * tt:128 * tt + 128],
                                  identity=id_f32[:])
                sems.inc_mark("st", ins, f"t:trm:{tt}")
                sems.wait(t, f"v:m1c:{tt}")
                ins = t.transpose(out=ps_m1[tt % 2][0:1, 0:128],
                                  in_=m1_c[:, tt:tt + 1], identity=id_f32[:])
                sems.inc_mark("st", ins, f"t:trm1:{tt}")
            sems.wait(t, "v:m1r:7")
            ins = t.matmul(out=ps_up[0][:, 0:512], lhsT=ones_f[:],
                           rhs=m1_row[0:1, 0:512], start=True, stop=True)
            ins = t.matmul(out=ps_up[1][:, 0:512], lhsT=ones_f[:],
                           rhs=m1_row[0:1, 512:1024], start=True, stop=True)
            sems.inc_mark("st", ins, "t:m1bc")
            sems.alias("t:phaseC", "st")
            if stop_after == 'C':
                return
            for j in range(NFSB * NDG):
                fsb, dg = j // NDG, j % NDG
                if dg == 0:
                    sems.wait(t, f"v:hq:{fsb * FC_PER_FSB + FC_PER_FSB - 1}")
                if j >= 2:
                    sems.wait(t, f"v:dev:{j-2}")
                for half in range(2):
                    sems.wait(t, f"v:dC:{j*2+half}")
                    tern4 = ternd[(j * 2 + half) % 2].rearrange(
                        "p (a c m) -> p a c m", c=2, m=128)
                    for fcl in range(8):
                        fcg = half * 8 + fcl
                        for dcl in range(2):
                            for th in range(2):
                                ins = t.matmul(
                                    out=ps_dn[(dg % 2) * 2 + dcl][:, 512 * th:512 * th + 512],
                                    lhsT=tern4[:, fcl, dcl, :],
                                    rhs=hq[:, fcg, 512 * th:512 * th + 512],
                                    start=(fcg == 0), stop=(fcg == FC_PER_FSB - 1))
                    sems.inc_mark("st", ins, f"t:dn:{j*2+half}")
            sems.alias("t:D:done", "st")
            if stop_after == 'D':
                return
            for tt in range(NTT):
                for gp in range(8):
                    if tt == 0 and gp <= 1:
                        sems.wait(t, "v:D:done")
                    elif gp >= 2:
                        sems.wait(t, f"v:tailg:{tt}:{gp-2}")
                    else:
                        sems.wait(t, f"v:tailg:{tt-1}:{6+gp}")
                    for k in range(4):
                        ins = t.transpose(
                            out=ps_tl[gp % 2][:, k, :],
                            in_=acc[:, 4 * gp + k, 128 * tt:128 * tt + 128],
                            identity=id_f32[:])
                    sems.inc_mark("st", ins, f"t:tlt:{tt}:{gp}")

        for name in list(sems.cnt):
            sems.handles[name] = E(nc.semaphore(name))

        dry = DryEng()
        sems.dry = True
        import os
        if os.environ.get("KERNEL_TRACE_SCHED"):
            sems.trace = {n: [] for n in
                          ("sync", "gpsimd", "vector", "scalar", "tensor")}
        for nm, f in (("sync", prog_sync), ("gpsimd", prog_gpsimd),
                      ("vector", prog_vector), ("scalar", prog_scalar),
                      ("tensor", prog_tensor)):
            sems.cur = nm
            f(dry)
        if sems.trace is not None:
            import json
            with open("/tmp/sched_trace.json", "w") as fh:
                json.dump({"trace": sems.trace,
                           "events": {k: list(v) for k, v in sems.events.items()}},
                          fh)
        final_counts = dict(sems.cnt)
        sems.dry = False
        for k in sems.cnt:
            sems.cnt[k] = 0

        block = E(nc.Block())

        @block.sync
        def _(sy):
            prog_sync(sy)

        @block.gpsimd
        def _(g):
            prog_gpsimd(g)

        @block.vector
        def _(v):
            prog_vector(v)

        @block.scalar
        def _(s):
            prog_scalar(s)

        @block.tensor
        def _(t):
            prog_tensor(t)

        for k, vv in final_counts.items():
            assert sems.cnt[k] == vv, (k, sems.cnt[k], vv)

    return nc


def _relayout(Wq, Wk, Wv, w_up, w_down):
    wq = np.ascontiguousarray(np.asarray(Wq, dtype=np.float32).transpose(2, 0, 1))
    wk = np.ascontiguousarray(np.asarray(Wk, dtype=np.float32).reshape(D, 128)
                              .reshape(NDC, 128, 128))
    wv = np.ascontiguousarray(np.asarray(Wv, dtype=np.float32).reshape(D, 128)
                              .reshape(NDC, 128, 128))
    wup = np.asarray(w_up, dtype=np.float32)
    wup_arr = np.ascontiguousarray(
        wup.reshape(NFC, 128, NDC, 128).transpose(0, 3, 2, 1)).reshape(NFC, 128, D)
    wdn = np.asarray(w_down, dtype=np.float32)
    wdn_arr = np.ascontiguousarray(
        wdn.reshape(NDG, 2, 128, NFSB, FC_PER_FSB, 128)
        .transpose(3, 0, 5, 4, 1, 2)).reshape(NFSB, NDG, 128, D)
    return wq, wk, wv, wup_arr, wdn_arr


def kernel(L0_attn_x, L0_attn_Wq, L0_attn_bq, L0_attn_Wk, L0_attn_bk,
           L0_attn_Wv, L0_attn_bv, L0_ffn_w_up, L0_ffn_w_down):
    x = np.asarray(L0_attn_x, dtype=np.float32).reshape(N_CORES * T, D)
    wq, wk, wv, wup_arr, wdn_arr = _relayout(
        L0_attn_Wq, L0_attn_Wk, L0_attn_Wv, L0_ffn_w_up, L0_ffn_w_down)
    bq = np.ascontiguousarray(np.asarray(L0_attn_bq, dtype=np.float32))
    bk = np.ascontiguousarray(np.asarray(L0_attn_bk, dtype=np.float32))
    bv = np.ascontiguousarray(np.asarray(L0_attn_bv, dtype=np.float32))

    nc = build_nc()
    in_maps = []
    for c in range(N_CORES):
        shard = np.ascontiguousarray(x[c * T:(c + 1) * T].reshape(NTT, 128, D))
        in_maps.append({"x": shard, "wq": wq, "wk": wk, "wv": wv,
                        "bq": bq, "bk": bk, "bv": bv,
                        "wup": wup_arr, "wdn": wdn_arr})
    res = run_bass_kernel_spmd(nc, in_maps, core_ids=list(range(N_CORES)))
    outs = [np.asarray(r["out"]).reshape(T, D) for r in res.results]
    return np.concatenate(outs, axis=0).reshape(4, 2048, D)


if __name__ == "__main__":
    build_nc()
    print("build ok")


# version 08
# speedup vs baseline: 1.8677x; 1.0096x over previous
import numpy as np
import ml_dtypes
from contextlib import ExitStack

import concourse.bass as bass
import concourse.mybir as mybir
from concourse.bass_utils import run_bass_kernel_spmd

F32 = mybir.dt.float32
BF16 = mybir.dt.bfloat16
AF = mybir.ActivationFunctionType
ALU = mybir.AluOpType
AX = mybir.AxisListType

MAGIC = float(np.float32(3 * 2**22))
N_CORES = 8
T = 1024
NTT = 8
D = 4096
NDC = 32
F = 16384
NFC = 128
NFSB = 8
FC_PER_FSB = 16
NDG = 16
S_ATTN = 1.0 / float(np.sqrt(np.float32(128.0)))
EPS = 1e-5
NW = float(F * D)
ESUB = 8
N_ESUB = 128 // ESUB
SCAN_CH = 2048
N_SCAN = (NFC * 128 * D) // (128 * SCAN_CH)


class Sems:
    def __init__(self):
        self.cnt = {}
        self.events = {}
        self.handles = {}
        self.dry = True
        self.trace = None
        self.cur = None

    def sem(self, name):
        self.cnt[name] = 0
        return name

    def inc(self, name, instr, k=1):
        self.cnt[name] += k
        if self.dry and self.trace is not None:
            self.trace[self.cur].append(("inc", name, k))
        if not self.dry:
            instr.then_inc(self.handles[name], k)

    def mark(self, evt, name):
        if self.dry:
            assert evt not in self.events, evt
            self.events[evt] = (name, self.cnt[name])
        else:
            assert self.events[evt] == (name, self.cnt[name]), (
                evt, self.events[evt], (name, self.cnt[name]))

    def inc_mark(self, name, instr, evt, k=1):
        self.inc(name, instr, k)
        self.mark(evt, name)

    def alias(self, evt, name):
        self.mark(evt, name)

    def wait(self, eng, evt):
        if self.dry:
            if self.trace is not None:
                self.trace[self.cur].append(("waitevt", evt))
            return
        name, v = self.events[evt]
        if v > 0:
            eng.wait_ge(self.handles[name], v)

    def wait_val(self, eng, name, v):
        if self.dry and self.trace is not None:
            self.trace[self.cur].append(("wait", name, v))
        if not self.dry:
            eng.wait_ge(self.handles[name], v)


class _DryInstr:
    def then_inc(self, *a, **kw):
        return self


class DryEng:
    def __getattr__(self, k):
        def f(*a, **kw):
            return _DryInstr()
        return f


def build_nc(stop_after=None, dbg=False):
    nc = bass.Bass("TRN2", target_bir_lowering=False)
    if dbg:
        dbg_qkv = nc.dram_tensor("dbg_qkv", [128, NTT * 384], F32,
                                 kind="ExternalOutput").ap()
        dbg_misc = nc.dram_tensor("dbg_misc", [128, 64], F32,
                                  kind="ExternalOutput").ap()
        dbg_w3 = nc.dram_tensor("dbg_w3", [128, 6144], F32,
                                kind="ExternalOutput").ap()
        dbg_ln1 = nc.dram_tensor("dbg_ln1", [128, 2048], F32,
                                 kind="ExternalOutput").ap()
        dbg_fm = nc.dram_tensor("dbg_fm", [128, 2048], F32,
                                kind="ExternalOutput").ap()
        dbg_cols = nc.dram_tensor("dbg_cols", [128, 64], F32,
                                  kind="ExternalOutput").ap()
        dbg_g = nc.dram_tensor("dbg_g", [3, 128, T], F32,
                               kind="ExternalOutput").ap()
        dbg_m = nc.dram_tensor("dbg_m", [128, 40], F32,
                               kind="ExternalOutput").ap()
        dbg_acc = nc.dram_tensor("dbg_acc", [4, 128, T], F32,
                                 kind="ExternalOutput").ap()

    x_d = nc.dram_tensor("x", [NTT, 128, D], F32, kind="ExternalInput").ap()
    wq_d = nc.dram_tensor("wq", [32, D, 128], F32, kind="ExternalInput").ap()
    wk_d = nc.dram_tensor("wk", [NDC, 128, 128], F32, kind="ExternalInput").ap()
    wv_d = nc.dram_tensor("wv", [NDC, 128, 128], F32, kind="ExternalInput").ap()
    bq_d = nc.dram_tensor("bq", [128, 32], F32, kind="ExternalInput").ap()
    bk_d = nc.dram_tensor("bk", [128, 1], F32, kind="ExternalInput").ap()
    bv_d = nc.dram_tensor("bv", [128, 1], F32, kind="ExternalInput").ap()
    wup_d = nc.dram_tensor("wup", [NFC, 128, D], F32, kind="ExternalInput").ap()
    wdn_d = nc.dram_tensor("wdn", [NFSB, NDG, 128, D], F32, kind="ExternalInput").ap()
    g_d = nc.dram_tensor("g_scr", [NFC, 128, T], F32, kind="Internal").ap()
    x2_d = nc.dram_tensor("x2_scr", [NTT, 128, D], F32, kind="Internal").ap()
    out_d = nc.dram_tensor("out", [NTT, 128, D], F32, kind="ExternalOutput").ap()

    sems = Sems()
    for s in ("sx", "swq", "saux", "supa", "supc", "sdnc", "sdnd", "sgst",
              "sgld", "sx2s", "sx2l", "sout", "sv", "ss", "st", "sg"):
        sems.sem(s)

    with ExitStack() as top:
        E = top.enter_context

        cols = E(nc.sbuf_tensor("cols", [128, 64], F32))
        amax_x = E(nc.sbuf_tensor("amax_x", [128, NTT], F32))
        sxq_c = E(nc.sbuf_tensor("sxq_c", [128, NTT], F32))
        m1_c = E(nc.sbuf_tensor("m1_c", [128, NTT], F32))
        m2_c = E(nc.sbuf_tensor("m2_c", [128, NTT], F32))
        ones_f = E(nc.sbuf_tensor("ones_f", [1, 128], F32))
        ones_col = E(nc.sbuf_tensor("ones_col", [128, 1], F32))
        sc_row = E(nc.sbuf_tensor("sc_row", [1, 4], F32))
        sup_inv = E(nc.sbuf_tensor("sup_inv", [128, 1], F32))
        sdn_inv = E(nc.sbuf_tensor("sdn_inv", [128, 1], F32))
        sup_b = E(nc.sbuf_tensor("sup_b", [128, 1], F32))
        sdn_b = E(nc.sbuf_tensor("sdn_b", [128, 1], F32))
        id_bf = E(nc.sbuf_tensor("id_bf", [128, 128], BF16))
        id_f32 = E(nc.sbuf_tensor("id_f32", [128, 128], F32))
        m1_bc = E(nc.sbuf_tensor("m1_bc", [128, T], F32))
        m1_row = E(nc.sbuf_tensor("m1_row", [1, T], F32))
        brow3 = E(nc.sbuf_tensor("brow3", [1, 384], F32))
        bcols = E(nc.sbuf_tensor("bcols", [128, 4], F32))
        ddrow = E(nc.sbuf_tensor("ddrow", [1, 512], F32))
        nsup2 = E(nc.sbuf_tensor("nsup2", [128, 1], F32))
        nsdn2 = E(nc.sbuf_tensor("nsdn2", [128, 1], F32))
        vdum = E(nc.sbuf_tensor("vdum", [128, 1], F32))
        sdum = E(nc.sbuf_tensor("sdum", [128, 1], F32))

        W = 49664
        arena = E(nc.sbuf_tensor("arena", [128, W], F32))

        def A(off, n):
            return arena[:, off:off + n]

        def AB(off, n):
            return arena[:, off:off + n].bitcast(BF16)

        xt = [A(0, D), A(D, D)]
        ln1t = AB(8192, 2048)
        xqt = AB(10240, 2048)
        ln1fm = AB(12288, 2048).rearrange("p (c t) -> p c t", t=128)
        ln2h = A(14336, 2048)
        ews = A(16384, ESUB * 128)
        ews2 = A(12288, ESUB * 128)
        up_cols = A(17408, 1024)
        dn_cols = A(22528, 1024)
        ddwkA = A(24576, 3072)
        ddwkC = A(18432, 3072)
        isleC = A(23552, 2048)
        w3 = AB(18432, 6144).rearrange("p (c n) -> p c n", n=384)
        wqsum = A(24576, NDC * 128).rearrange("p (c m) -> p c m", m=128)
        scst = [A(24576, SCAN_CH), A(24576 + SCAN_CH, SCAN_CH)]
        qkvT = A(28672, NTT * 384).rearrange("p (a n) -> p a n", n=384)
        bias_bc = A(31744, 384)
        attno = A(32128, 256)
        XQ_OFF = W - 16384
        xq_fm = AB(XQ_OFF, 16384).rearrange("p (c t) -> p c t", t=T)

        upst = [A(2048 * i, 2048) for i in range(4)]
        ternu = [AB(8192 + 1024 * i, 1024) for i in range(4)]
        reluws = A(12288, T)
        grng = [A(13312 + i * T, T) for i in range(4)]
        mrun = A(17408, T)
        scstd = [A(18432, SCAN_CH), A(18432 + SCAN_CH, SCAN_CH)]

        acc = A(0, NDC * T).rearrange("p (c t) -> p c t", t=T)
        hq = AB(32768, 8192).rearrange("p (c t) -> p c t", t=T)
        gld = [A(40960, T), A(40960 + T, T)]
        dnst = [A(43008, 2048), A(43008 + 2048, 2048)]
        ternd = [AB(47104, 1024), AB(48128, 1024)]
        isleD = A(49152, 512)

        x2t = [A(32768, D), A(32768 + D, D)]
        outt = [A(40960, D), A(40960 + D, D)]

        ps = E(nc.psum_tensor([128, 4096], F32))
        ps_tr = [ps[:, 0:1024].bitcast(BF16).rearrange("p (c t) -> p c t", t=128),
                 ps[:, 1024:2048].bitcast(BF16).rearrange("p (c t) -> p c t", t=128)]
        ps_qkv = [ps[:, 2048:2432], ps[:, 2560:2944]]
        ps_misc = [ps[:, 3072:3584], ps[:, 3584:4096]]
        ps_up = [ps[:, 0:1024], ps[:, 1024:2048]]
        ps_m1 = [ps[:, 3072:3584], ps[:, 3584:4096]]
        ps_dn = [ps[:, 0:1024], ps[:, 1024:2048], ps[:, 2048:3072], ps[:, 3072:4096]]
        ps_tl = [ps[:, 0:512].rearrange("p (c t) -> p c t", t=128),
                 ps[:, 512:1024].rearrange("p (c t) -> p c t", t=128)]

        def vsig(v, evt):
            v.drain()
            ins = v.tensor_copy(out=vdum[:], in_=ones_col[:])
            sems.inc_mark("sv", ins, evt)

        def ssig(s, evt):
            s.drain()
            ins = s.activation(out=sdum[:], in_=sdum[:], func=AF.Abs)
            sems.inc_mark("ss", ins, evt)

        def dd_level(v, cur_hi, cur_lo, out_hi, out_lo, tmp1, tmp2, w):
            w2 = w // 2
            re = cur_hi.rearrange("p (n two) -> p n two", two=2)
            a, b = re[:, :, 0:1], re[:, :, 1:2]
            oh = out_hi.unsqueeze(2)
            t1 = tmp1.unsqueeze(2)
            t2 = tmp2.unsqueeze(2)
            v.tensor_tensor(out=oh, in0=a, in1=b, op=ALU.add)
            v.drain()
            v.tensor_tensor(out=t1, in0=oh, in1=a, op=ALU.subtract)
            v.drain()
            v.tensor_tensor(out=t2, in0=oh, in1=t1, op=ALU.subtract)
            v.drain()
            v.tensor_tensor(out=t2, in0=a, in1=t2, op=ALU.subtract)
            v.tensor_tensor(out=t1, in0=b, in1=t1, op=ALU.subtract)
            v.drain()
            v.tensor_tensor(out=t1, in0=t1, in1=t2, op=ALU.add)
            if cur_lo is None:
                v.drain()
                v.tensor_copy(out=out_lo, in_=tmp1)
            else:
                rl = cur_lo.rearrange("p (n two) -> p n two", two=2)
                v.tensor_tensor(out=out_lo.unsqueeze(2), in0=rl[:, :, 0:1],
                                in1=rl[:, :, 1:2], op=ALU.add)
                v.drain()
                v.tensor_tensor(out=out_lo, in0=out_lo, in1=tmp1, op=ALU.add)
            v.drain()

        def dd_tree_cols(v, src_cols, wk):
            hiA, loA = wk[:, 0:512], wk[:, 512:1024]
            hiB, loB = wk[:, 1024:1536], wk[:, 1536:2048]
            tmp1, tmp2 = wk[:, 2048:2560], wk[:, 2560:3072]
            w = 1024
            cur_hi, cur_lo = src_cols, None
            flip = 0
            while w > 1:
                w2 = w // 2
                oh, ol = (hiA, loA) if flip == 0 else (hiB, loB)
                dd_level(v, cur_hi, cur_lo, oh[:, 0:w2], ol[:, 0:w2],
                         tmp1[:, 0:w2], tmp2[:, 0:w2], w)
                cur_hi, cur_lo = oh[:, 0:w2], ol[:, 0:w2]
                w = w2
                flip ^= 1
            return cur_hi, cur_lo

        def dd_tree_row(v, out_scalar):
            cur_hi, cur_lo = ddrow[0:1, 0:128], ddrow[0:1, 128:256]
            w = 128
            flip = 0
            while w > 1:
                w2 = w // 2
                if flip == 0:
                    oh, ol = ddrow[0:1, 256:256 + w2], ddrow[0:1, 320:320 + w2]
                else:
                    oh, ol = ddrow[0:1, 0:w2], ddrow[0:1, 128:128 + w2]
                dd_level(v, cur_hi, cur_lo, oh, ol,
                         ddrow[0:1, 384:384 + w2], ddrow[0:1, 448:448 + w2], w)
                cur_hi, cur_lo = oh, ol
                w = w2
                flip ^= 1
            v.tensor_tensor(out=out_scalar, in0=cur_hi, in1=cur_lo, op=ALU.add)
            v.drain()

        def prog_sync(sy):
            for i in range(N_SCAN):
                if i < 2:
                    sems.wait(sy, "v:w3")
                else:
                    sems.wait(sy, f"s:scup:{i-2}")
                dd = sy.dma_start(
                    out=scst[i % 2],
                    in_=wup_d[i // 2][:, (i % 2) * SCAN_CH:(i % 2) * SCAN_CH + SCAN_CH])
                sems.inc("supa", dd, 16)
            if stop_after == 'A':
                return
            sems.wait_val(sy, "sx2s", 16 * NTT)
            sems.wait(sy, "v:phaseA")
            sems.wait(sy, "s:phaseA")
            sems.wait(sy, "t:phaseA")
            sems.wait(sy, "g:phaseA")
            for u in range(NFC * 2):
                if u >= 4:
                    sems.wait(sy, f"v:uC:{u-4}")
                dd = sy.dma_start(out=upst[u % 4],
                                  in_=wup_d[u // 2][:, (u % 2) * 2048:(u % 2) * 2048 + 2048])
                sems.inc("supc", dd, 16)
            if stop_after == 'C':
                return
            sems.wait(sy, "v:phaseC")
            sems.wait(sy, "s:phaseC")
            sems.wait(sy, "t:phaseC")
            for j in range(NFSB * NDG):
                fsb, dg = j // NDG, j % NDG
                for half in range(2):
                    jj = j * 2 + half
                    if jj >= 2:
                        sems.wait(sy, f"v:dC:{jj-2}")
                    dd = sy.dma_start(
                        out=dnst[jj % 2],
                        in_=wdn_d[fsb, dg][:, half * 2048:half * 2048 + 2048])
                    sems.inc("sdnd", dd, 16)

        def prog_gpsimd(g):
            g.memset(id_bf[:], 0.0)
            g.affine_select(out=id_bf[:], in_=id_bf[:], compare_op=ALU.not_equal,
                            fill=1.0, base=0, pattern=[[-1, 128]], channel_multiplier=1)
            g.memset(id_f32[:], 0.0)
            i2 = g.affine_select(out=id_f32[:], in_=id_f32[:], compare_op=ALU.not_equal,
                                 fill=1.0, base=0, pattern=[[-1, 128]],
                                 channel_multiplier=1)
            sems.inc_mark("sg", i2, "g:ident")
            for tt in range(2):
                dd = g.dma_start(out=xt[tt], in_=x_d[tt])
                sems.inc("sx", dd, 16)
            for h in range(32):
                src = wq_d[h].rearrange("(c p) m -> p c m", p=128)
                if h == 0:
                    dd = g.dma_start(out=wqsum, in_=src)
                else:
                    sems.wait_val(g, "swq", 16 * h)
                    dd = g.dma_start(out=wqsum, in_=src, accum_op=ALU.add)
                sems.inc("swq", dd, 16)
            dd = g.dma_start(out=w3[:, :, 128:256],
                             in_=wk_d.rearrange("c p m -> p c m"))
            sems.inc("saux", dd, 16)
            dd = g.dma_start(out=w3[:, :, 256:384],
                             in_=wv_d.rearrange("c p m -> p c m"))
            sems.inc("saux", dd, 16)
            dd = g.dma_start(out=cols[:, 32:64], in_=bq_d)
            sems.inc("saux", dd, 16)
            dd = g.dma_start(out=bcols[:, 1:2], in_=bk_d)
            sems.inc("saux", dd, 16)
            dd = g.dma_start(out=bcols[:, 2:3], in_=bv_d)
            sems.inc_mark("saux", dd, "g:aux", k=16)
            for tt in range(NTT):
                if tt >= 2:
                    dd = g.dma_start(out=xt[tt % 2], in_=x_d[tt])
                    sems.inc("sx", dd, 16)
                sems.wait(g, f"v:xq:{tt}")
                dd = g.dma_start(out=x2_d[tt], in_=xt[tt % 2])
                sems.inc("sx2s", dd, 16)
                if stop_after == 'A':
                    dd = g.dma_start(out=out_d[tt], in_=xt[tt % 2])
                    sems.inc("sout", dd, 16)
            sems.alias("g:phaseA", "sx2s")
            if stop_after == 'A':
                if dbg:
                    sems.wait(g, "v:phaseA")
                    dd = g.dma_start(out=dbg_qkv,
                                     in_=qkvT.rearrange("p a n -> p (a n)"))
                    sems.inc("sout", dd, 16)
                    dd = g.dma_start(out=dbg_misc[:, 0:8], in_=amax_x[:])
                    sems.inc("sout", dd, 16)
                    dd = g.dma_start(out=dbg_misc[:, 8:16], in_=sxq_c[:])
                    sems.inc("sout", dd, 16)
                    dd = g.dma_start(out=dbg_misc[:, 16:19], in_=bcols[:, 0:3])
                    sems.inc("sout", dd, 16)
                    dd = g.dma_start(out=dbg_misc[:, 24:27],
                                     in_=bias_bc[:, 0:3])
                    sems.inc("sout", dd, 16)
                    dd = g.dma_start(out=dbg_w3,
                                     in_=w3.rearrange("p c n -> p (c n)")
                                     .bitcast(F32))
                    sems.inc("sout", dd, 16)
                    dd = g.dma_start(out=dbg_ln1, in_=ln1t.bitcast(F32))
                    sems.inc("sout", dd, 16)
                    dd = g.dma_start(out=dbg_fm,
                                     in_=ln1fm.rearrange("p c t -> p (c t)")
                                     .bitcast(F32))
                    sems.inc("sout", dd, 16)
                    dd = g.dma_start(out=dbg_cols, in_=cols[:])
                    sems.inc("sout", dd, 16)
                    sems.wait_val(g, "sout", 16 * (NTT + 9))
                else:
                    sems.wait_val(g, "sout", 16 * NTT)
                return
            for fc in range(NFC):
                sems.wait(g, f"s:gsq:{fc}")
                dd = g.dma_start(out=g_d[fc], in_=grng[fc % 4])
                sems.inc("sgst", dd, 16)
            sems.alias("g:phaseC", "sgst")
            if stop_after == 'C':
                if dbg:
                    sems.wait(g, "v:phaseC")
                    for gi, fc in enumerate((0, 64, 127)):
                        dd = g.dma_start(out=dbg_g[gi], in_=g_d[fc])
                        sems.inc("sout", dd, 16)
                    dd = g.dma_start(out=dbg_m[:, 0:8], in_=m1_c[:])
                    sems.inc("sout", dd, 16)
                    dd = g.dma_start(out=dbg_m[:, 8:16], in_=m2_c[:])
                    sems.inc("sout", dd, 16)
                    dd = g.dma_start(out=dbg_m[:, 16:24], in_=amax_x[:])
                    sems.inc("sout", dd, 16)
                    dd = g.dma_start(allow_slow_non_contiguous=True, out=dbg_m[:, 24:25], in_=sup_b[:])
                    sems.inc("sout", dd, 16)
                    dd = g.dma_start(allow_slow_non_contiguous=True, out=dbg_m[:, 25:26], in_=sdn_b[:])
                    sems.inc("sout", dd, 16)
                    dd = g.dma_start(allow_slow_non_contiguous=True, out=dbg_m[:, 26:27], in_=sup_inv[:])
                    sems.inc("sout", dd, 16)
                    dd = g.dma_start(allow_slow_non_contiguous=True, out=dbg_m[:, 27:28], in_=sdn_inv[:])
                    sems.inc("sout", dd, 16)
                    dd = g.dma_start(out=dbg_m[:, 28:36], in_=m1_bc[:, 0:8])
                    sems.inc("sout", dd, 16)
                    sems.wait_val(g, "sout", 16 * 11)
                return
            sems.wait(g, "v:phaseC")
            sems.wait(g, "s:phaseC")
            sems.wait(g, "t:phaseC")
            for idx in range(NFC):
                if idx >= 2:
                    sems.wait(g, f"v:hq:{idx-2}")
                dd = g.dma_start(out=gld[idx % 2], in_=g_d[idx])
                sems.inc("sgld", dd, 16)
            if stop_after == 'D':
                if dbg:
                    sems.wait(g, "v:D:done")
                    for ai, dc in enumerate((0, 1, 16, 31)):
                        dd = g.dma_start(out=dbg_acc[ai], in_=acc[:, dc, :])
                        sems.inc("sout", dd, 16)
                    sems.wait_val(g, "sout", 16 * 4)
                return
            sems.wait(g, "t:D:done")
            sems.wait(g, "v:D:done")
            for tt in range(NTT):
                if tt >= 2:
                    sems.wait(g, f"v:tail:{tt-2}")
                    dd = g.dma_start(out=out_d[tt - 2], in_=outt[tt % 2])
                    sems.inc("sout", dd, 16)
                dd = g.dma_start(out=x2t[tt % 2], in_=x2_d[tt])
                sems.inc("sx2l", dd, 16)
            for tt in (NTT - 2, NTT - 1):
                sems.wait(g, f"v:tail:{tt}")
                dd = g.dma_start(out=out_d[tt], in_=outt[tt % 2])
                sems.inc("sout", dd, 16)
            sems.wait_val(g, "sout", 16 * NTT)

        def prog_vector(v):
            v.memset(ones_f[:], 1.0)
            v.memset(ones_col[:], 1.0)
            sems.wait_val(v, "swq", 16 * 32)
            v.tensor_copy(out=w3[:, :, 0:128], in_=wqsum)
            vsig(v, "v:w3")
            sems.wait(v, "g:aux")
            v.tensor_reduce(out=bcols[:, 0:1].unsqueeze(2),
                            in_=cols[:, 32:64].unsqueeze(1), axis=AX.X,
                            op=ALU.add)
            vsig(v, "v:bcols")
            sems.wait(v, "t:btr")
            v.tensor_copy(out=brow3[0:1, :], in_=ps_misc[0][0:1, 0:384])
            vsig(v, "v:brow")
            sems.wait(v, "t:biasbc")
            v.tensor_copy(out=bias_bc, in_=ps_qkv[0])
            vsig(v, "v:biasbc")

            for tt in range(NTT):
                xs = xt[tt % 2]
                sems.wait_val(v, "sx", 16 * (tt + 1))
                c_mean = cols[:, 8 * tt + 0:8 * tt + 1]
                c_tmp = cols[:, 8 * tt + 1:8 * tt + 2]
                c_var = cols[:, 8 * tt + 2:8 * tt + 3]
                c_rstd = cols[:, 8 * tt + 3:8 * tt + 4]
                c_m2 = cols[:, 8 * tt + 4:8 * tt + 5]

                def stats(var_evt, sq_evt):
                    sems.wait(v, sq_evt)
                    v.tensor_tensor(out=c_tmp, in0=c_tmp, in1=c_m2, op=ALU.add)
                    v.tensor_scalar(out=c_mean, in0=c_mean, scalar1=1.0 / D,
                                    scalar2=None, op0=ALU.mult, op1=ALU.bypass)
                    v.drain()
                    v.scalar_tensor_tensor(out=c_m2, in0=c_mean, scalar=0.0,
                                           in1=c_mean, op0=ALU.add, op1=ALU.mult)
                    v.tensor_scalar(out=c_var, in0=c_tmp, scalar1=1.0 / D,
                                    scalar2=None, op0=ALU.mult, op1=ALU.bypass)
                    v.drain()
                    v.tensor_scalar(out=c_var, in0=c_var, scalar1=c_m2,
                                    scalar2=EPS, op0=ALU.subtract, op1=ALU.add)
                    vsig(v, var_evt)

                stats(f"v:var1:{tt}", f"s:sq1:{tt}")
                sems.wait(v, f"s:rstd1:{tt}")
                for hh in range(2):
                    sl = slice(2048 * hh, 2048 * hh + 2048)
                    v.tensor_scalar(out=ln2h, in0=xs[:, sl], scalar1=c_mean,
                                    scalar2=c_rstd, op0=ALU.subtract, op1=ALU.mult)
                    v.tensor_copy(out=ln1t[:, sl], in_=ln2h)
                vsig(v, f"v:ln1:{tt}")
                for gp in range(2):
                    sems.wait(v, f"t:tr1:{tt}:{gp}")
                    v.tensor_copy(out=ln1fm[:, 16 * gp:16 * gp + 16, :],
                                  in_=ps_tr[gp])
                    vsig(v, f"v:fm1:{tt}:{gp}")
                sems.wait(v, f"t:qkv:{tt}")
                ins = v.tensor_tensor(out=qkvT[:, tt, :], in0=ps_qkv[tt % 2],
                                      in1=bias_bc, op=ALU.add)
                sems.inc_mark("sv", ins, f"v:qkv:{tt}")
                qs = qkvT[:, tt, 0:128]
                ks = qkvT[:, tt, 128:256]
                vs = qkvT[:, tt, 256:384]
                c_aq = cols[:, 8 * tt + 5:8 * tt + 6]
                c_ak = cols[:, 8 * tt + 6:8 * tt + 7]
                c_cs = cols[:, 8 * tt + 7:8 * tt + 8]
                v.tensor_reduce(out=c_aq, in_=qs, axis=AX.X, op=ALU.max,
                                apply_absolute_value=True)
                v.tensor_reduce(out=c_ak, in_=ks, axis=AX.X, op=ALU.max,
                                apply_absolute_value=True)
                v.drain()
                v.scalar_tensor_tensor(out=c_cs, in0=c_aq, scalar=0.0, in1=c_ak,
                                       op0=ALU.add, op1=ALU.mult)
                v.drain()
                v.tensor_scalar(out=c_cs, in0=c_cs, scalar1=-S_ATTN,
                                scalar2=None, op0=ALU.mult, op1=ALU.bypass)
                vsig(v, f"v:attnc:{tt}")
                o_num = attno[:, 0:128]
                o_den = attno[:, 128:256]
                ewss = [ews.rearrange("p (a b) -> p a b", b=128),
                        ews2.rearrange("p (a b) -> p a b", b=128)]

                def sbuild(su):
                    dlo = su * ESUB
                    v.tensor_tensor(
                        out=ewss[su % 2],
                        in0=qs[:, dlo:dlo + ESUB].unsqueeze(2)
                            .broadcast_to([128, ESUB, 128]),
                        in1=ks.unsqueeze(1).broadcast_to([128, ESUB, 128]),
                        op=ALU.mult)
                    vsig(v, f"v:S:{tt}:{su}")

                sbuild(0)
                for su in range(N_ESUB):
                    dlo = su * ESUB
                    e3 = ewss[su % 2]
                    if su + 1 < N_ESUB:
                        sbuild(su + 1)
                    sems.wait(v, f"s:exp:{tt}:{su}")
                    v.tensor_reduce(out=o_den[:, dlo:dlo + ESUB].unsqueeze(2),
                                    in_=e3, axis=AX.X, op=ALU.add)
                    v.tensor_tensor(out=e3, in0=e3,
                                    in1=vs.unsqueeze(1).broadcast_to([128, ESUB, 128]),
                                    op=ALU.mult)
                    ins = v.tensor_reduce(out=o_num[:, dlo:dlo + ESUB].unsqueeze(2),
                                          in_=e3, axis=AX.X, op=ALU.add)
                    sems.inc_mark("sv", ins, f"v:esub:{tt}:{su}")
                v.reciprocal(out=o_den, in_=o_den)
                v.tensor_tensor(out=o_num, in0=o_num, in1=o_den, op=ALU.mult)
                x3 = xs.rearrange("p (a b) -> p a b", b=32)
                v.tensor_tensor(out=x3, in0=x3,
                                in1=o_num.unsqueeze(2).broadcast_to([128, 128, 32]),
                                op=ALU.add)
                vsig(v, f"v:x2:{tt}")
                stats(f"v:var2:{tt}", f"s:sq2:{tt}")
                sems.wait(v, f"s:rstd2:{tt}")
                for hh in range(2):
                    sl = slice(2048 * hh, 2048 * hh + 2048)
                    v.tensor_scalar(out=ln2h, in0=xs[:, sl], scalar1=c_mean,
                                    scalar2=c_rstd, op0=ALU.subtract, op1=ALU.mult)
                    v.tensor_reduce(out=c_tmp if hh == 0 else c_m2, in_=ln2h,
                                    axis=AX.X, op=ALU.max, apply_absolute_value=True)
                v.drain()
                v.tensor_tensor(out=amax_x[:, tt:tt + 1], in0=c_tmp, in1=c_m2,
                                op=ALU.max)
                v.drain()
                v.tensor_scalar(out=c_tmp, in0=amax_x[:, tt:tt + 1], scalar1=EPS,
                                scalar2=None, op0=ALU.max, op1=ALU.bypass)
                v.drain()
                v.reciprocal(out=sxq_c[:, tt:tt + 1], in_=c_tmp)
                v.drain()
                v.tensor_scalar(out=sxq_c[:, tt:tt + 1], in0=sxq_c[:, tt:tt + 1],
                                scalar1=127.0, scalar2=None, op0=ALU.mult,
                                op1=ALU.bypass)
                v.drain()
                for hh in range(2):
                    sl = slice(2048 * hh, 2048 * hh + 2048)
                    v.tensor_scalar(out=ln2h, in0=xs[:, sl], scalar1=c_mean,
                                    scalar2=c_rstd, op0=ALU.subtract, op1=ALU.mult)
                    v.tensor_scalar(out=ln2h, in0=ln2h, scalar1=sxq_c[:, tt:tt + 1],
                                    scalar2=MAGIC, op0=ALU.mult, op1=ALU.add)
                    v.tensor_scalar(out=ln2h, in0=ln2h, scalar1=MAGIC,
                                    scalar2=None, op0=ALU.subtract, op1=ALU.bypass)
                    v.tensor_copy(out=xqt[:, sl], in_=ln2h)
                vsig(v, f"v:xq:{tt}")
                for gp in range(2):
                    sems.wait(v, f"t:tr2:{tt}:{gp}")
                    v.tensor_copy(
                        out=xq_fm[:, 16 * gp:16 * gp + 16, 128 * tt:128 * tt + 128],
                        in_=ps_tr[gp])
                    vsig(v, f"v:fm2:{tt}:{gp}")
                for i in range(32 * tt, 32 * tt + 32):
                    sems.wait_val(v, "supa", 16 * (i + 1))
                    if i >= 2:
                        sems.wait(v, f"s:scup:{i-2}")
                    ins = v.tensor_scalar(out=scst[i % 2], in0=scst[i % 2],
                                          scalar1=-1.0, scalar2=1.0,
                                          op0=ALU.max, op1=ALU.min)
                    sems.inc_mark("sv", ins, f"v:scupc:{i}")
            sems.wait(v, "s:scupall")
            dhi, dlo = dd_tree_cols(v, up_cols, ddwkA)
            ins = v.tensor_copy(out=ddwkA[:, 3071:3072], in_=dhi)
            sems.inc_mark("sv", ins, "v:ddup1")
            sems.wait(v, "t:ddup")
            ins = v.tensor_copy(out=ddrow[0:1, 0:128], in_=ps_misc[0][0:1, 0:128])
            ins = v.tensor_copy(out=ddrow[0:1, 128:256], in_=ps_misc[0][0:1, 128:256])
            v.drain()
            dd_tree_row(v, sc_row[0:1, 0:1])
            v.tensor_scalar(out=sc_row[0:1, 0:1], in0=sc_row[0:1, 0:1],
                            scalar1=1.0 / NW, scalar2=EPS, op0=ALU.mult, op1=ALU.add)
            v.drain()
            ins = v.reciprocal(out=sc_row[0:1, 1:2], in_=sc_row[0:1, 0:1])
            v.drain()
            ins = v.tensor_copy(out=sc_row[0:1, 1:2], in_=sc_row[0:1, 1:2])
            sems.inc_mark("sv", ins, "v:suprow")
            sems.wait(v, "t:supbc")
            ins = v.tensor_copy(out=sup_b[:], in_=ps_misc[1][:, 0:1])
            v.drain()
            v.tensor_scalar(out=sup_inv[:], in0=sup_b[:], scalar1=0.5, scalar2=None,
                            op0=ALU.mult, op1=ALU.bypass)
            v.tensor_scalar(out=nsup2[:], in0=sup_b[:], scalar1=-0.5, scalar2=None,
                            op0=ALU.mult, op1=ALU.bypass)
            v.drain()
            ins = v.tensor_copy(out=bcols[:, 3:4], in_=ones_col[:])
            sems.inc_mark("sv", ins, "v:phaseA")
            if stop_after == 'A':
                return

            v.memset(mrun, -3.0e38)
            for fc in range(NFC):
                for h in range(2):
                    u = 2 * fc + h
                    sems.wait_val(v, "supc", 16 * (u + 1))
                    if u >= 4:
                        sems.wait(v, f"t:uph:{u-4}")
                    v.tensor_scalar(out=isleC, in0=upst[u % 4],
                                    scalar1=nsup2[:, 0:1], scalar2=None,
                                    op0=ALU.is_le, op1=ALU.bypass)
                    v.scalar_tensor_tensor(out=ternu[u % 4], in0=upst[u % 4],
                                           scalar=sup_inv[:, 0:1], in1=isleC,
                                           op0=ALU.is_ge, op1=ALU.subtract)
                    vsig(v, f"v:uC:{u}")
                sems.wait(v, f"s:gsqr:{fc}")
                ins = v.tensor_tensor(out=mrun, in0=mrun, in1=reluws,
                                      op=ALU.max)
                sems.inc_mark("sv", ins, f"v:upev:{fc}")
                for i in (2 * fc, 2 * fc + 1):
                    sems.wait_val(v, "sdnc", 16 * (i + 1))
                    if i >= 2:
                        sems.wait(v, f"s:scdn:{i-2}")
                    ins = v.tensor_scalar(out=scstd[i % 2], in0=scstd[i % 2],
                                          scalar1=-1.0, scalar2=1.0,
                                          op0=ALU.max, op1=ALU.min)
                    sems.inc_mark("sv", ins, f"v:scdnc:{i}")
            sems.wait(v, "s:scdnall")
            dhi, dlo = dd_tree_cols(v, dn_cols, ddwkC)
            ins = v.tensor_copy(out=ddwkC[:, 3071:3072], in_=dhi)
            sems.inc_mark("sv", ins, "v:dddn1")
            sems.wait(v, "t:dddn")
            ins = v.tensor_copy(out=ddrow[0:1, 0:128], in_=ps_misc[0][0:1, 0:128])
            ins = v.tensor_copy(out=ddrow[0:1, 128:256], in_=ps_misc[0][0:1, 128:256])
            v.drain()
            dd_tree_row(v, sc_row[0:1, 2:3])
            v.tensor_scalar(out=sc_row[0:1, 2:3], in0=sc_row[0:1, 2:3],
                            scalar1=1.0 / NW, scalar2=EPS, op0=ALU.mult, op1=ALU.add)
            v.drain()
            ins = v.reciprocal(out=sc_row[0:1, 3:4], in_=sc_row[0:1, 2:3])
            v.drain()
            ins = v.tensor_copy(out=sc_row[0:1, 3:4], in_=sc_row[0:1, 3:4])
            sems.inc_mark("sv", ins, "v:sdnrow")
            sems.wait(v, "t:sdnbc")
            ins = v.tensor_copy(out=sdn_b[:], in_=ps_misc[1][:, 0:1])
            v.drain()
            v.tensor_scalar(out=sdn_inv[:], in0=sdn_b[:], scalar1=0.5, scalar2=None,
                            op0=ALU.mult, op1=ALU.bypass)
            v.tensor_scalar(out=nsdn2[:], in0=sdn_b[:], scalar1=-0.5, scalar2=None,
                            op0=ALU.mult, op1=ALU.bypass)
            v.drain()
            ins = v.tensor_copy(out=bcols[:, 3:4], in_=ones_col[:])
            sems.inc_mark("sv", ins, "v:sdnb")
            for tt in range(NTT):
                sems.wait(v, f"t:trm:{tt}")
                c_g = cols[:, 8 * tt + 0:8 * tt + 1]
                c_c1 = cols[:, 8 * tt + 1:8 * tt + 2]
                c_ah = cols[:, 8 * tt + 2:8 * tt + 3]
                c_sh = cols[:, 8 * tt + 3:8 * tt + 4]
                v.tensor_reduce(out=c_g, in_=ps_misc[tt % 2][:, 0:128], axis=AX.X,
                                op=ALU.max)
                v.drain()
                v.tensor_scalar(out=c_g, in0=c_g, scalar1=0.0, scalar2=None,
                                op0=ALU.max, op1=ALU.bypass)
                v.tensor_scalar(out=c_c1, in0=amax_x[:, tt:tt + 1],
                                scalar1=sup_b[:, 0:1], scalar2=1.0 / 127.0,
                                op0=ALU.mult, op1=ALU.mult)
                v.drain()
                v.scalar_tensor_tensor(out=c_g, in0=c_g, scalar=0.0, in1=c_g,
                                       op0=ALU.add, op1=ALU.mult)
                v.scalar_tensor_tensor(out=c_c1, in0=c_c1, scalar=0.0, in1=c_c1,
                                       op0=ALU.add, op1=ALU.mult)
                v.drain()
                v.scalar_tensor_tensor(out=c_ah, in0=c_g, scalar=0.0, in1=c_c1,
                                       op0=ALU.add, op1=ALU.mult)
                v.drain()
                v.tensor_scalar(out=c_ah, in0=c_ah, scalar1=EPS, scalar2=None,
                                op0=ALU.max, op1=ALU.bypass)
                v.drain()
                v.reciprocal(out=c_sh, in_=c_ah)
                v.drain()
                v.tensor_scalar(out=c_sh, in0=c_sh, scalar1=127.0, scalar2=None,
                                op0=ALU.mult, op1=ALU.bypass)
                v.drain()
                v.scalar_tensor_tensor(out=m1_c[:, tt:tt + 1], in0=c_c1,
                                       scalar=0.0, in1=c_sh,
                                       op0=ALU.add, op1=ALU.mult)
                vsig(v, f"v:m1c:{tt}")
                v.tensor_scalar(out=m2_c[:, tt:tt + 1], in0=c_ah,
                                scalar1=1.0 / 127.0, scalar2=sdn_b[:, 0:1],
                                op0=ALU.mult, op1=ALU.mult)
                sems.wait(v, f"t:trm1:{tt}")
                v.tensor_copy(out=m1_row[0:1, 128 * tt:128 * tt + 128],
                              in_=ps_m1[tt % 2][0:1, 0:128])
                vsig(v, f"v:m1r:{tt}")
            sems.wait(v, "t:m1bc")
            ins = v.tensor_copy(out=m1_bc[:, 0:512], in_=ps_up[0][:, 0:512])
            ins = v.tensor_copy(out=m1_bc[:, 512:1024], in_=ps_up[1][:, 0:512])
            sems.inc_mark("sv", ins, "v:phaseC")
            if stop_after == 'C':
                return

            for fsb in range(NFSB):
                for jj in range(FC_PER_FSB):
                    idx = fsb * FC_PER_FSB + jj
                    sems.wait_val(v, "sgld", 16 * (idx + 1))
                    if fsb > 0 and jj == 0:
                        sems.wait(v, f"t:dn:{(fsb * NDG - 1) * 2 + 1}")
                    ins = v.tensor_tensor(out=gld[idx % 2], in0=gld[idx % 2],
                                          in1=m1_bc[:], op=ALU.mult)
                    v.tensor_scalar(out=gld[idx % 2], in0=gld[idx % 2],
                                    scalar1=MAGIC, scalar2=MAGIC,
                                    op0=ALU.add, op1=ALU.subtract)
                    v.tensor_copy(out=hq[:, jj, :], in_=gld[idx % 2])
                    vsig(v, f"v:hq:{idx}")
                for dg in range(NDG):
                    j = fsb * NDG + dg
                    for half in range(2):
                        jj2 = j * 2 + half
                        sems.wait_val(v, "sdnd", 16 * (jj2 + 1))
                        if jj2 >= 2:
                            sems.wait(v, f"t:dn:{jj2-2}")
                        for qq in range(4):
                            sq = slice(512 * qq, 512 * qq + 512)
                            v.tensor_scalar(out=isleD, in0=dnst[jj2 % 2][:, sq],
                                            scalar1=nsdn2[:, 0:1], scalar2=None,
                                            op0=ALU.is_le, op1=ALU.bypass)
                            v.scalar_tensor_tensor(
                                out=ternd[jj2 % 2][:, sq],
                                in0=dnst[jj2 % 2][:, sq],
                                scalar=sdn_inv[:, 0:1], in1=isleD,
                                op0=ALU.is_ge, op1=ALU.subtract)
                        vsig(v, f"v:dC:{jj2}")
                    sems.wait(v, f"t:dn:{j*2+1}")
                    for dcl in range(2):
                        dst = acc[:, 2 * dg + dcl, :]
                        pp = ps_dn[(dg % 2) * 2 + dcl]
                        if fsb == 0:
                            ins = v.tensor_copy(out=dst, in_=pp)
                        else:
                            ins = v.tensor_tensor(out=dst, in0=dst, in1=pp,
                                                  op=ALU.add)
                    sems.inc_mark("sv", ins, f"v:dev:{j}")
            vsig(v, "v:D:done:sig")
            sems.mark("v:D:done", "sv")
            if stop_after == 'D':
                return
            for tt in range(NTT):
                sems.wait_val(v, "sx2l", 16 * (tt + 1))
                if tt >= 2:
                    sems.wait_val(v, "sout", 16 * (tt - 1))
                for gp in range(8):
                    sems.wait(v, f"t:tlt:{tt}:{gp}")
                    ins = v.scalar_tensor_tensor(
                        out=outt[tt % 2][:, 512 * gp:512 * gp + 512],
                        in0=ps_tl[gp % 2].rearrange("p c t -> p (c t)"),
                        scalar=m2_c[:, tt:tt + 1],
                        in1=x2t[tt % 2][:, 512 * gp:512 * gp + 512],
                        op0=ALU.mult, op1=ALU.add)
                    sems.inc_mark("sv", ins, f"v:tailg:{tt}:{gp}")
                vsig(v, f"v:tailsig:{tt}")
                sems.mark(f"v:tail:{tt}", "sv")

        def prog_scalar(s):
            scup = 0
            for tt in range(NTT):
                c_var = cols[:, 8 * tt + 2:8 * tt + 3]
                c_rstd = cols[:, 8 * tt + 3:8 * tt + 4]
                c_cs = cols[:, 8 * tt + 7:8 * tt + 8]
                c_mean_s = cols[:, 8 * tt + 0:8 * tt + 1]
                c_s1 = cols[:, 8 * tt + 1:8 * tt + 2]
                c_s2 = cols[:, 8 * tt + 4:8 * tt + 5]
                xs = xt[tt % 2]
                sems.wait_val(s, "sx", 16 * (tt + 1))
                if tt >= 1:
                    sems.wait(s, f"v:xq:{tt-1}")
                s.activation(out=xs, in_=xs, func=AF.Identity, accum_out=c_mean_s)
                s.activation(out=ln2h, in_=xs[:, 0:2048], func=AF.Square,
                             accum_out=c_s1)
                s.activation(out=ln2h, in_=xs[:, 2048:4096], func=AF.Square,
                             accum_out=c_s2)
                s.drain()
                ins = s.activation(out=bcols[:, 3:4], in_=bcols[:, 3:4], func=AF.Abs)
                sems.inc_mark("ss", ins, f"s:sq1:{tt}")
                sems.wait(s, f"v:var1:{tt}")
                s.activation(out=c_rstd, in_=c_var, func=AF.Ln)
                s.drain()
                s.activation(out=c_rstd, in_=c_rstd, func=AF.Exp, scale=-0.5)
                ssig(s, f"s:rstd1:{tt}")
                for su in range(N_ESUB):
                    sems.wait(s, f"v:S:{tt}:{su}")
                    tgt = ews if su % 2 == 0 else ews2
                    s.activation(out=tgt, in_=tgt, func=AF.Exp,
                                 scale=S_ATTN, bias=c_cs)
                    ssig(s, f"s:exp:{tt}:{su}")
                sems.wait(s, f"v:x2:{tt}")
                s.activation(out=xs, in_=xs, func=AF.Identity, accum_out=c_mean_s)
                s.activation(out=ln2h, in_=xs[:, 0:2048], func=AF.Square,
                             accum_out=c_s1)
                s.activation(out=ln2h, in_=xs[:, 2048:4096], func=AF.Square,
                             accum_out=c_s2)
                s.drain()
                ins = s.activation(out=bcols[:, 3:4], in_=bcols[:, 3:4], func=AF.Abs)
                sems.inc_mark("ss", ins, f"s:sq2:{tt}")
                sems.wait(s, f"v:var2:{tt}")
                s.activation(out=c_rstd, in_=c_var, func=AF.Ln)
                s.drain()
                s.activation(out=c_rstd, in_=c_rstd, func=AF.Exp, scale=-0.5)
                ssig(s, f"s:rstd2:{tt}")
                for i in range(32 * tt, 32 * tt + 32):
                    sems.wait(s, f"v:scupc:{i}")
                    for q in range(4):
                        ins = s.activation(out=scst[i % 2][:, 512 * q:512 * q + 512],
                                           in_=scst[i % 2][:, 512 * q:512 * q + 512],
                                           func=AF.Abs,
                                           accum_out=up_cols[:, 4 * i + q:4 * i + q + 1])
                    sems.inc_mark("ss", ins, f"s:scup:{i}")
                    scup += 1
            s.drain()
            ins = s.activation(out=bcols[:, 3:4], in_=bcols[:, 3:4], func=AF.Abs)
            sems.inc_mark("ss", ins, "s:scupall")
            sems.alias("s:phaseA", "ss")
            if stop_after == 'A':
                return
            sems.wait(s, "t:qkv:7")
            for fc in range(NFC):
                sems.wait(s, f"t:up:{fc}")
                if fc >= 1:
                    sems.wait(s, f"v:upev:{fc-1}")
                if fc >= 4:
                    sems.wait_val(s, "sgst", 16 * (fc - 3))
                s.activation(out=reluws, in_=ps_up[fc % 2], func=AF.Relu)
                ssig(s, f"s:gsqr:{fc}")
                s.activation(out=grng[fc % 4], in_=reluws, func=AF.Square)
                ssig(s, f"s:gsq:{fc}")
                for i in (2 * fc, 2 * fc + 1):
                    if i >= 2:
                        sems.wait(s, f"s:scdn:{i-2}")
                    jch = i // 2
                    dd = s.dma_start(
                        out=scstd[i % 2],
                        in_=wdn_d[jch // NDG, jch % NDG]
                        [:, (i % 2) * SCAN_CH:(i % 2) * SCAN_CH + SCAN_CH])
                    sems.inc("sdnc", dd, 16)
                for i in (2 * fc, 2 * fc + 1):
                    sems.wait(s, f"v:scdnc:{i}")
                    for q in range(4):
                        ins = s.activation(out=scstd[i % 2][:, 512 * q:512 * q + 512],
                                           in_=scstd[i % 2][:, 512 * q:512 * q + 512],
                                           func=AF.Abs,
                                           accum_out=dn_cols[:, 4 * i + q:4 * i + q + 1])
                    sems.inc_mark("ss", ins, f"s:scdn:{i}")
            s.drain()
            ins = s.activation(out=bcols[:, 3:4], in_=bcols[:, 3:4], func=AF.Abs)
            sems.inc_mark("ss", ins, "s:scdnall")
            sems.alias("s:phaseC", "ss")
            if stop_after == 'C':
                return

        def prog_tensor(t):
            sems.wait(t, "v:bcols")
            sems.wait(t, "g:ident")
            for bi in range(3):
                ins = t.transpose(out=ps_misc[0][0:1, 128 * bi:128 * bi + 128],
                                  in_=bcols[:, bi:bi + 1], identity=id_f32[:])
            sems.inc_mark("st", ins, "t:btr")
            sems.wait(t, "v:brow")
            ins = t.matmul(out=ps_qkv[0], lhsT=ones_f[:], rhs=brow3[0:1, :],
                           start=True, stop=True)
            sems.inc_mark("st", ins, "t:biasbc")
            for tt in range(NTT):
                sems.wait(t, f"v:ln1:{tt}")
                for gp in range(2):
                    if tt > 0:
                        sems.wait(t, f"v:fm2:{tt-1}:{gp}")
                    for k in range(16):
                        cidx = 16 * gp + k
                        ins = t.transpose(out=ps_tr[gp][:, k, :],
                                          in_=ln1t[:, 128 * cidx:128 * cidx + 128],
                                          identity=id_bf[:])
                    sems.inc_mark("st", ins, f"t:tr1:{tt}:{gp}")
                sems.wait(t, f"v:fm1:{tt}:1")
                if tt == 0:
                    sems.wait(t, "v:biasbc")
                else:
                    sems.wait(t, f"v:qkv:{tt-1}")
                for dc in range(NDC):
                    ins = t.matmul(out=ps_qkv[tt % 2], lhsT=ln1fm[:, dc, :],
                                   rhs=w3[:, dc, :], start=(dc == 0),
                                   stop=(dc == NDC - 1))
                sems.inc_mark("st", ins, f"t:qkv:{tt}")
                sems.wait(t, f"v:xq:{tt}")
                for gp in range(2):
                    sems.wait(t, f"v:fm1:{tt}:{gp}")
                    for k in range(16):
                        cidx = 16 * gp + k
                        ins = t.transpose(out=ps_tr[gp][:, k, :],
                                          in_=xqt[:, 128 * cidx:128 * cidx + 128],
                                          identity=id_bf[:])
                    sems.inc_mark("st", ins, f"t:tr2:{tt}:{gp}")
            sems.wait(t, "v:ddup1")
            ins = t.transpose(out=ps_misc[0][0:1, 0:128],
                              in_=ddwkA[:, 1024:1025], identity=id_f32[:])
            ins = t.transpose(out=ps_misc[0][0:1, 128:256],
                              in_=ddwkA[:, 1536:1537], identity=id_f32[:])
            sems.inc_mark("st", ins, "t:ddup")
            sems.wait(t, "v:suprow")
            ins = t.matmul(out=ps_misc[1][:, 0:2], lhsT=ones_f[:],
                           rhs=sc_row[0:1, 0:2], start=True, stop=True)
            sems.inc_mark("st", ins, "t:supbc")
            sems.alias("t:phaseA", "st")
            if stop_after == 'A':
                return
            for fc in range(NFC):
                if fc >= 2:
                    sems.wait(t, f"s:gsqr:{fc-2}")
                for h in range(2):
                    u = 2 * fc + h
                    sems.wait(t, f"v:uC:{u}")
                    tern3 = ternu[u % 4].rearrange("p (c m) -> p c m", m=128)
                    for dcl in range(16):
                        dc = 16 * h + dcl
                        for th in range(2):
                            ins = t.matmul(
                                out=ps_up[fc % 2][:, 512 * th:512 * th + 512],
                                lhsT=tern3[:, dcl, :],
                                rhs=xq_fm[:, dc, 512 * th:512 * th + 512],
                                start=(dc == 0), stop=(dc == NDC - 1))
                    sems.inc_mark("st", ins, f"t:uph:{u}")
                sems.alias(f"t:up:{fc}", "st")
            sems.wait(t, "v:dddn1")
            ins = t.transpose(out=ps_misc[0][0:1, 0:128],
                              in_=ddwkC[:, 1024:1025], identity=id_f32[:])
            ins = t.transpose(out=ps_misc[0][0:1, 128:256],
                              in_=ddwkC[:, 1536:1537], identity=id_f32[:])
            sems.inc_mark("st", ins, "t:dddn")
            sems.wait(t, "v:sdnrow")
            ins = t.matmul(out=ps_misc[1][:, 0:2], lhsT=ones_f[:],
                           rhs=sc_row[0:1, 2:4], start=True, stop=True)
            sems.inc_mark("st", ins, "t:sdnbc")
            for tt in range(NTT):
                if tt == 0:
                    sems.wait(t, f"v:upev:{NFC-1}")
                    sems.wait(t, "v:sdnb")
                if tt >= 2:
                    sems.wait(t, f"v:m1c:{tt-2}")
                ins = t.transpose(out=ps_misc[tt % 2][:, 0:128],
                                  in_=mrun[:, 128 * tt:128 * tt + 128],
                                  identity=id_f32[:])
                sems.inc_mark("st", ins, f"t:trm:{tt}")
                sems.wait(t, f"v:m1c:{tt}")
                ins = t.transpose(out=ps_m1[tt % 2][0:1, 0:128],
                                  in_=m1_c[:, tt:tt + 1], identity=id_f32[:])
                sems.inc_mark("st", ins, f"t:trm1:{tt}")
            sems.wait(t, "v:m1r:7")
            ins = t.matmul(out=ps_up[0][:, 0:512], lhsT=ones_f[:],
                           rhs=m1_row[0:1, 0:512], start=True, stop=True)
            ins = t.matmul(out=ps_up[1][:, 0:512], lhsT=ones_f[:],
                           rhs=m1_row[0:1, 512:1024], start=True, stop=True)
            sems.inc_mark("st", ins, "t:m1bc")
            sems.alias("t:phaseC", "st")
            if stop_after == 'C':
                return
            for j in range(NFSB * NDG):
                fsb, dg = j // NDG, j % NDG
                if dg == 0:
                    sems.wait(t, f"v:hq:{fsb * FC_PER_FSB + FC_PER_FSB - 1}")
                if j >= 2:
                    sems.wait(t, f"v:dev:{j-2}")
                for half in range(2):
                    sems.wait(t, f"v:dC:{j*2+half}")
                    tern4 = ternd[(j * 2 + half) % 2].rearrange(
                        "p (a c m) -> p a c m", c=2, m=128)
                    for fcl in range(8):
                        fcg = half * 8 + fcl
                        for dcl in range(2):
                            for th in range(2):
                                ins = t.matmul(
                                    out=ps_dn[(dg % 2) * 2 + dcl][:, 512 * th:512 * th + 512],
                                    lhsT=tern4[:, fcl, dcl, :],
                                    rhs=hq[:, fcg, 512 * th:512 * th + 512],
                                    start=(fcg == 0), stop=(fcg == FC_PER_FSB - 1))
                    sems.inc_mark("st", ins, f"t:dn:{j*2+half}")
            sems.alias("t:D:done", "st")
            if stop_after == 'D':
                return
            for tt in range(NTT):
                for gp in range(8):
                    if tt == 0 and gp <= 1:
                        sems.wait(t, "v:D:done")
                    elif gp >= 2:
                        sems.wait(t, f"v:tailg:{tt}:{gp-2}")
                    else:
                        sems.wait(t, f"v:tailg:{tt-1}:{6+gp}")
                    for k in range(4):
                        ins = t.transpose(
                            out=ps_tl[gp % 2][:, k, :],
                            in_=acc[:, 4 * gp + k, 128 * tt:128 * tt + 128],
                            identity=id_f32[:])
                    sems.inc_mark("st", ins, f"t:tlt:{tt}:{gp}")

        for name in list(sems.cnt):
            sems.handles[name] = E(nc.semaphore(name))

        dry = DryEng()
        sems.dry = True
        import os
        if os.environ.get("KERNEL_TRACE_SCHED"):
            sems.trace = {n: [] for n in
                          ("sync", "gpsimd", "vector", "scalar", "tensor")}
        for nm, f in (("sync", prog_sync), ("gpsimd", prog_gpsimd),
                      ("vector", prog_vector), ("scalar", prog_scalar),
                      ("tensor", prog_tensor)):
            sems.cur = nm
            f(dry)
        if sems.trace is not None:
            import json
            with open("/tmp/sched_trace.json", "w") as fh:
                json.dump({"trace": sems.trace,
                           "events": {k: list(v) for k, v in sems.events.items()}},
                          fh)
        final_counts = dict(sems.cnt)
        sems.dry = False
        for k in sems.cnt:
            sems.cnt[k] = 0

        block = E(nc.Block())

        @block.sync
        def _(sy):
            prog_sync(sy)

        @block.gpsimd
        def _(g):
            prog_gpsimd(g)

        @block.vector
        def _(v):
            prog_vector(v)

        @block.scalar
        def _(s):
            prog_scalar(s)

        @block.tensor
        def _(t):
            prog_tensor(t)

        for k, vv in final_counts.items():
            assert sems.cnt[k] == vv, (k, sems.cnt[k], vv)

    return nc


def _relayout(Wq, Wk, Wv, w_up, w_down):
    wq = np.ascontiguousarray(np.asarray(Wq, dtype=np.float32).transpose(2, 0, 1))
    wk = np.ascontiguousarray(np.asarray(Wk, dtype=np.float32).reshape(D, 128)
                              .reshape(NDC, 128, 128))
    wv = np.ascontiguousarray(np.asarray(Wv, dtype=np.float32).reshape(D, 128)
                              .reshape(NDC, 128, 128))
    wup = np.asarray(w_up, dtype=np.float32)
    wup_arr = np.ascontiguousarray(
        wup.reshape(NFC, 128, NDC, 128).transpose(0, 3, 2, 1)).reshape(NFC, 128, D)
    wdn = np.asarray(w_down, dtype=np.float32)
    wdn_arr = np.ascontiguousarray(
        wdn.reshape(NDG, 2, 128, NFSB, FC_PER_FSB, 128)
        .transpose(3, 0, 5, 4, 1, 2)).reshape(NFSB, NDG, 128, D)
    return wq, wk, wv, wup_arr, wdn_arr


def kernel(L0_attn_x, L0_attn_Wq, L0_attn_bq, L0_attn_Wk, L0_attn_bk,
           L0_attn_Wv, L0_attn_bv, L0_ffn_w_up, L0_ffn_w_down):
    x = np.asarray(L0_attn_x, dtype=np.float32).reshape(N_CORES * T, D)
    wq, wk, wv, wup_arr, wdn_arr = _relayout(
        L0_attn_Wq, L0_attn_Wk, L0_attn_Wv, L0_ffn_w_up, L0_ffn_w_down)
    bq = np.ascontiguousarray(np.asarray(L0_attn_bq, dtype=np.float32))
    bk = np.ascontiguousarray(np.asarray(L0_attn_bk, dtype=np.float32))
    bv = np.ascontiguousarray(np.asarray(L0_attn_bv, dtype=np.float32))

    nc = build_nc()
    in_maps = []
    for c in range(N_CORES):
        shard = np.ascontiguousarray(x[c * T:(c + 1) * T].reshape(NTT, 128, D))
        in_maps.append({"x": shard, "wq": wq, "wk": wk, "wv": wv,
                        "bq": bq, "bk": bk, "bv": bv,
                        "wup": wup_arr, "wdn": wdn_arr})
    res = run_bass_kernel_spmd(nc, in_maps, core_ids=list(range(N_CORES)))
    outs = [np.asarray(r["out"]).reshape(T, D) for r in res.results]
    return np.concatenate(outs, axis=0).reshape(4, 2048, D)


if __name__ == "__main__":
    build_nc()
    print("build ok")


# version 09
# speedup vs baseline: 1.9416x; 1.0395x over previous
import numpy as np
import ml_dtypes
from contextlib import ExitStack

import concourse.bass as bass
import concourse.mybir as mybir
from concourse.bass_utils import run_bass_kernel_spmd

F32 = mybir.dt.float32
BF16 = mybir.dt.bfloat16
AF = mybir.ActivationFunctionType
ALU = mybir.AluOpType
AX = mybir.AxisListType

MAGIC = float(np.float32(3 * 2**22))
N_CORES = 8
T = 1024
NTT = 8
D = 4096
NDC = 32
F = 16384
NFC = 128
NFSB = 8
FC_PER_FSB = 16
NDG = 16
S_ATTN = 1.0 / float(np.sqrt(np.float32(128.0)))
EPS = 1e-5
NW = float(F * D)
ESUB = 8
N_ESUB = 128 // ESUB
SCAN_CH = 2048
N_SCAN = (NFC * 128 * D) // (128 * SCAN_CH)


class Sems:
    def __init__(self):
        self.cnt = {}
        self.events = {}
        self.handles = {}
        self.dry = True
        self.trace = None
        self.cur = None

    def sem(self, name):
        self.cnt[name] = 0
        return name

    def inc(self, name, instr, k=1):
        self.cnt[name] += k
        if self.dry and self.trace is not None:
            self.trace[self.cur].append(("inc", name, k))
        if not self.dry:
            instr.then_inc(self.handles[name], k)

    def mark(self, evt, name):
        if self.dry:
            assert evt not in self.events, evt
            self.events[evt] = (name, self.cnt[name])
        else:
            assert self.events[evt] == (name, self.cnt[name]), (
                evt, self.events[evt], (name, self.cnt[name]))

    def inc_mark(self, name, instr, evt, k=1):
        self.inc(name, instr, k)
        self.mark(evt, name)

    def alias(self, evt, name):
        self.mark(evt, name)

    def wait(self, eng, evt):
        if self.dry:
            if self.trace is not None:
                self.trace[self.cur].append(("waitevt", evt))
            return
        name, v = self.events[evt]
        if v > 0:
            eng.wait_ge(self.handles[name], v)

    def wait_val(self, eng, name, v):
        if self.dry and self.trace is not None:
            self.trace[self.cur].append(("wait", name, v))
        if not self.dry:
            eng.wait_ge(self.handles[name], v)


class _DryInstr:
    def then_inc(self, *a, **kw):
        return self


class DryEng:
    def __getattr__(self, k):
        def f(*a, **kw):
            return _DryInstr()
        return f


def build_nc(stop_after=None, dbg=False):
    nc = bass.Bass("TRN2", target_bir_lowering=False)
    if dbg:
        dbg_qkv = nc.dram_tensor("dbg_qkv", [128, NTT * 384], F32,
                                 kind="ExternalOutput").ap()
        dbg_misc = nc.dram_tensor("dbg_misc", [128, 64], F32,
                                  kind="ExternalOutput").ap()
        dbg_w3 = nc.dram_tensor("dbg_w3", [128, 6144], F32,
                                kind="ExternalOutput").ap()
        dbg_ln1 = nc.dram_tensor("dbg_ln1", [128, 2048], F32,
                                 kind="ExternalOutput").ap()
        dbg_fm = nc.dram_tensor("dbg_fm", [128, 2048], F32,
                                kind="ExternalOutput").ap()
        dbg_cols = nc.dram_tensor("dbg_cols", [128, 64], F32,
                                  kind="ExternalOutput").ap()
        dbg_g = nc.dram_tensor("dbg_g", [3, 128, T], F32,
                               kind="ExternalOutput").ap()
        dbg_m = nc.dram_tensor("dbg_m", [128, 40], F32,
                               kind="ExternalOutput").ap()
        dbg_acc = nc.dram_tensor("dbg_acc", [4, 128, T], F32,
                                 kind="ExternalOutput").ap()

    x_d = nc.dram_tensor("x", [NTT, 128, D], F32, kind="ExternalInput").ap()
    wq_d = nc.dram_tensor("wq", [32, D, 128], F32, kind="ExternalInput").ap()
    wk_d = nc.dram_tensor("wk", [NDC, 128, 128], F32, kind="ExternalInput").ap()
    wv_d = nc.dram_tensor("wv", [NDC, 128, 128], F32, kind="ExternalInput").ap()
    bq_d = nc.dram_tensor("bq", [128, 32], F32, kind="ExternalInput").ap()
    bk_d = nc.dram_tensor("bk", [128, 1], F32, kind="ExternalInput").ap()
    bv_d = nc.dram_tensor("bv", [128, 1], F32, kind="ExternalInput").ap()
    wup_d = nc.dram_tensor("wup", [NFC, 128, D], F32, kind="ExternalInput").ap()
    wdn_d = nc.dram_tensor("wdn", [NFSB, NDG, 128, D], F32, kind="ExternalInput").ap()
    g_d = nc.dram_tensor("g_scr", [NFC, 128, T], F32, kind="Internal").ap()
    x2_d = nc.dram_tensor("x2_scr", [NTT, 128, D], F32, kind="Internal").ap()
    out_d = nc.dram_tensor("out", [NTT, 128, D], F32, kind="ExternalOutput").ap()

    sems = Sems()
    for s in ("sx", "swq", "saux", "supa", "supc", "sdnc", "sdnd", "sgst",
              "sgld", "sx2s", "sx2l", "sout", "sv", "ss", "st", "sg"):
        sems.sem(s)

    with ExitStack() as top:
        E = top.enter_context

        cols = E(nc.sbuf_tensor("cols", [128, 64], F32))
        amax_x = E(nc.sbuf_tensor("amax_x", [128, NTT], F32))
        sxq_c = E(nc.sbuf_tensor("sxq_c", [128, NTT], F32))
        m1_c = E(nc.sbuf_tensor("m1_c", [128, NTT], F32))
        m2_c = E(nc.sbuf_tensor("m2_c", [128, NTT], F32))
        ones_f = E(nc.sbuf_tensor("ones_f", [1, 128], F32))
        ones_col = E(nc.sbuf_tensor("ones_col", [128, 1], F32))
        sc_row = E(nc.sbuf_tensor("sc_row", [1, 4], F32))
        sup_inv = E(nc.sbuf_tensor("sup_inv", [128, 1], F32))
        sdn_inv = E(nc.sbuf_tensor("sdn_inv", [128, 1], F32))
        sup_b = E(nc.sbuf_tensor("sup_b", [128, 1], F32))
        sdn_b = E(nc.sbuf_tensor("sdn_b", [128, 1], F32))
        id_bf = E(nc.sbuf_tensor("id_bf", [128, 128], BF16))
        id_f32 = E(nc.sbuf_tensor("id_f32", [128, 128], F32))
        m1_bc = E(nc.sbuf_tensor("m1_bc", [128, T], F32))
        m1_row = E(nc.sbuf_tensor("m1_row", [1, T], F32))
        brow3 = E(nc.sbuf_tensor("brow3", [1, 384], F32))
        bcols = E(nc.sbuf_tensor("bcols", [128, 4], F32))
        ddrow = E(nc.sbuf_tensor("ddrow", [1, 512], F32))
        nsup2 = E(nc.sbuf_tensor("nsup2", [128, 1], F32))
        nsdn2 = E(nc.sbuf_tensor("nsdn2", [128, 1], F32))
        vdum = E(nc.sbuf_tensor("vdum", [128, 1], F32))
        sdum = E(nc.sbuf_tensor("sdum", [128, 1], F32))

        W = 49664
        arena = E(nc.sbuf_tensor("arena", [128, W], F32))

        def A(off, n):
            return arena[:, off:off + n]

        def AB(off, n):
            return arena[:, off:off + n].bitcast(BF16)

        xt = [A(0, D), A(D, D)]
        ln1t = AB(8192, 2048)
        xqt = AB(10240, 2048)
        ln1fm = AB(12288, 2048).rearrange("p (c t) -> p c t", t=128)
        ln2h = A(14336, 2048)
        ews = A(16384, ESUB * 128)
        ews2 = A(12288, ESUB * 128)
        up_cols = A(17408, 1024)
        dn_cols = A(22528, 1024)
        ddwkA = A(24576, 3072)
        ddwkC = A(18432, 3072)
        isleC = A(23552, 2048)
        w3 = AB(18432, 6144).rearrange("p (c n) -> p c n", n=384)
        wqsum = A(24576, NDC * 128).rearrange("p (c m) -> p c m", m=128)
        scst = [A(24576, SCAN_CH), A(24576 + SCAN_CH, SCAN_CH)]
        qkvT = A(28672, NTT * 384).rearrange("p (a n) -> p a n", n=384)
        bias_bc = A(31744, 384)
        attno = A(32128, 256)
        XQ_OFF = W - 16384
        xq_fm = AB(XQ_OFF, 16384).rearrange("p (c t) -> p c t", t=T)

        upst = [A(2048 * i, 2048) for i in range(4)]
        ternu = [AB(8192 + 1024 * i, 1024) for i in range(4)]
        reluws = A(12288, T)
        grng = [A(13312 + i * T, T) for i in range(4)]
        mrun = A(17408, T)
        scstd = [A(18432, SCAN_CH), A(18432 + SCAN_CH, SCAN_CH)]

        acc = A(0, NDC * T).rearrange("p (c t) -> p c t", t=T)
        hq = AB(32768, 8192).rearrange("p (c t) -> p c t", t=T)
        gld = [A(40960, T), A(40960 + T, T)]
        dnst = [A(43008 + 1024 * i, 1024) for i in range(4)]
        ternd = [AB(47104 + 512 * i, 512) for i in range(4)]
        isleD = A(49152, 512)

        x2t = [A(32768, D), A(32768 + D, D)]
        outt = [A(40960, D), A(40960 + D, D)]

        ps = E(nc.psum_tensor([128, 4096], F32))
        ps_tr = [ps[:, 0:1024].bitcast(BF16).rearrange("p (c t) -> p c t", t=128),
                 ps[:, 1024:2048].bitcast(BF16).rearrange("p (c t) -> p c t", t=128)]
        ps_qkv = [ps[:, 2048:2432], ps[:, 2560:2944]]
        ps_misc = [ps[:, 3072:3584], ps[:, 3584:4096]]
        ps_up = [ps[:, 0:1024], ps[:, 1024:2048]]
        ps_m1 = [ps[:, 3072:3584], ps[:, 3584:4096]]
        ps_dn = [ps[:, 0:1024], ps[:, 1024:2048], ps[:, 2048:3072], ps[:, 3072:4096]]
        ps_tl = [ps[:, 0:512].rearrange("p (c t) -> p c t", t=128),
                 ps[:, 512:1024].rearrange("p (c t) -> p c t", t=128)]

        def vsig(v, evt):
            v.drain()
            ins = v.tensor_copy(out=vdum[:], in_=ones_col[:])
            sems.inc_mark("sv", ins, evt)

        def ssig(s, evt):
            s.drain()
            ins = s.activation(out=sdum[:], in_=sdum[:], func=AF.Abs)
            sems.inc_mark("ss", ins, evt)

        def dd_level(v, cur_hi, cur_lo, out_hi, out_lo, tmp1, tmp2, w):
            w2 = w // 2
            re = cur_hi.rearrange("p (n two) -> p n two", two=2)
            a, b = re[:, :, 0:1], re[:, :, 1:2]
            oh = out_hi.unsqueeze(2)
            t1 = tmp1.unsqueeze(2)
            t2 = tmp2.unsqueeze(2)
            v.tensor_tensor(out=oh, in0=a, in1=b, op=ALU.add)
            v.drain()
            v.tensor_tensor(out=t1, in0=oh, in1=a, op=ALU.subtract)
            v.drain()
            v.tensor_tensor(out=t2, in0=oh, in1=t1, op=ALU.subtract)
            v.drain()
            v.tensor_tensor(out=t2, in0=a, in1=t2, op=ALU.subtract)
            v.tensor_tensor(out=t1, in0=b, in1=t1, op=ALU.subtract)
            v.drain()
            v.tensor_tensor(out=t1, in0=t1, in1=t2, op=ALU.add)
            if cur_lo is None:
                v.drain()
                v.tensor_copy(out=out_lo, in_=tmp1)
            else:
                rl = cur_lo.rearrange("p (n two) -> p n two", two=2)
                v.tensor_tensor(out=out_lo.unsqueeze(2), in0=rl[:, :, 0:1],
                                in1=rl[:, :, 1:2], op=ALU.add)
                v.drain()
                v.tensor_tensor(out=out_lo, in0=out_lo, in1=tmp1, op=ALU.add)
            v.drain()

        def dd_tree_cols(v, src_cols, wk):
            hiA, loA = wk[:, 0:512], wk[:, 512:1024]
            hiB, loB = wk[:, 1024:1536], wk[:, 1536:2048]
            tmp1, tmp2 = wk[:, 2048:2560], wk[:, 2560:3072]
            w = 1024
            cur_hi, cur_lo = src_cols, None
            flip = 0
            while w > 1:
                w2 = w // 2
                oh, ol = (hiA, loA) if flip == 0 else (hiB, loB)
                dd_level(v, cur_hi, cur_lo, oh[:, 0:w2], ol[:, 0:w2],
                         tmp1[:, 0:w2], tmp2[:, 0:w2], w)
                cur_hi, cur_lo = oh[:, 0:w2], ol[:, 0:w2]
                w = w2
                flip ^= 1
            return cur_hi, cur_lo

        def dd_tree_row(v, out_scalar):
            cur_hi, cur_lo = ddrow[0:1, 0:128], ddrow[0:1, 128:256]
            w = 128
            flip = 0
            while w > 1:
                w2 = w // 2
                if flip == 0:
                    oh, ol = ddrow[0:1, 256:256 + w2], ddrow[0:1, 320:320 + w2]
                else:
                    oh, ol = ddrow[0:1, 0:w2], ddrow[0:1, 128:128 + w2]
                dd_level(v, cur_hi, cur_lo, oh, ol,
                         ddrow[0:1, 384:384 + w2], ddrow[0:1, 448:448 + w2], w)
                cur_hi, cur_lo = oh, ol
                w = w2
                flip ^= 1
            v.tensor_tensor(out=out_scalar, in0=cur_hi, in1=cur_lo, op=ALU.add)
            v.drain()

        def prog_sync(sy):
            for i in range(N_SCAN):
                if i < 2:
                    sems.wait(sy, "v:w3")
                else:
                    sems.wait(sy, f"s:scup:{i-2}")
                dd = sy.dma_start(
                    out=scst[i % 2],
                    in_=wup_d[i // 2][:, (i % 2) * SCAN_CH:(i % 2) * SCAN_CH + SCAN_CH])
                sems.inc("supa", dd, 16)
            if stop_after == 'A':
                return
            sems.wait_val(sy, "sx2s", 16 * NTT)
            sems.wait(sy, "v:phaseA")
            sems.wait(sy, "s:phaseA")
            sems.wait(sy, "t:phaseA")
            sems.wait(sy, "g:phaseA")
            for u in range(NFC * 2):
                if u >= 4:
                    sems.wait(sy, f"v:uC:{u-4}")
                dd = sy.dma_start(out=upst[u % 4],
                                  in_=wup_d[u // 2][:, (u % 2) * 2048:(u % 2) * 2048 + 2048])
                sems.inc("supc", dd, 16)
            if stop_after == 'C':
                return
            sems.wait(sy, "v:phaseC")
            sems.wait(sy, "s:phaseC")
            sems.wait(sy, "t:phaseC")
            for j in range(NFSB * NDG):
                fsb, dg = j // NDG, j % NDG
                for qt in range(4):
                    jj = j * 4 + qt
                    if jj >= 4:
                        sems.wait(sy, f"v:dC:{jj-4}")
                    dd = sy.dma_start(
                        out=dnst[jj % 4],
                        in_=wdn_d[fsb, dg][:, qt * 1024:qt * 1024 + 1024])
                    sems.inc("sdnd", dd, 16)

        def prog_gpsimd(g):
            g.memset(id_bf[:], 0.0)
            g.affine_select(out=id_bf[:], in_=id_bf[:], compare_op=ALU.not_equal,
                            fill=1.0, base=0, pattern=[[-1, 128]], channel_multiplier=1)
            g.memset(id_f32[:], 0.0)
            i2 = g.affine_select(out=id_f32[:], in_=id_f32[:], compare_op=ALU.not_equal,
                                 fill=1.0, base=0, pattern=[[-1, 128]],
                                 channel_multiplier=1)
            sems.inc_mark("sg", i2, "g:ident")
            for tt in range(2):
                dd = g.dma_start(out=xt[tt], in_=x_d[tt])
                sems.inc("sx", dd, 16)
            for h in range(32):
                src = wq_d[h].rearrange("(c p) m -> p c m", p=128)
                if h == 0:
                    dd = g.dma_start(out=wqsum, in_=src)
                else:
                    sems.wait_val(g, "swq", 16 * h)
                    dd = g.dma_start(out=wqsum, in_=src, accum_op=ALU.add)
                sems.inc("swq", dd, 16)
            dd = g.dma_start(out=w3[:, :, 128:256],
                             in_=wk_d.rearrange("c p m -> p c m"))
            sems.inc("saux", dd, 16)
            dd = g.dma_start(out=w3[:, :, 256:384],
                             in_=wv_d.rearrange("c p m -> p c m"))
            sems.inc("saux", dd, 16)
            dd = g.dma_start(out=cols[:, 32:64], in_=bq_d)
            sems.inc("saux", dd, 16)
            dd = g.dma_start(out=bcols[:, 1:2], in_=bk_d)
            sems.inc("saux", dd, 16)
            dd = g.dma_start(out=bcols[:, 2:3], in_=bv_d)
            sems.inc_mark("saux", dd, "g:aux", k=16)
            for tt in range(NTT):
                if tt >= 2:
                    dd = g.dma_start(out=xt[tt % 2], in_=x_d[tt])
                    sems.inc("sx", dd, 16)
                sems.wait(g, f"v:xq:{tt}")
                dd = g.dma_start(out=x2_d[tt], in_=xt[tt % 2])
                sems.inc("sx2s", dd, 16)
                if stop_after == 'A':
                    dd = g.dma_start(out=out_d[tt], in_=xt[tt % 2])
                    sems.inc("sout", dd, 16)
            sems.alias("g:phaseA", "sx2s")
            if stop_after == 'A':
                if dbg:
                    sems.wait(g, "v:phaseA")
                    dd = g.dma_start(out=dbg_qkv,
                                     in_=qkvT.rearrange("p a n -> p (a n)"))
                    sems.inc("sout", dd, 16)
                    dd = g.dma_start(out=dbg_misc[:, 0:8], in_=amax_x[:])
                    sems.inc("sout", dd, 16)
                    dd = g.dma_start(out=dbg_misc[:, 8:16], in_=sxq_c[:])
                    sems.inc("sout", dd, 16)
                    dd = g.dma_start(out=dbg_misc[:, 16:19], in_=bcols[:, 0:3])
                    sems.inc("sout", dd, 16)
                    dd = g.dma_start(out=dbg_misc[:, 24:27],
                                     in_=bias_bc[:, 0:3])
                    sems.inc("sout", dd, 16)
                    dd = g.dma_start(out=dbg_w3,
                                     in_=w3.rearrange("p c n -> p (c n)")
                                     .bitcast(F32))
                    sems.inc("sout", dd, 16)
                    dd = g.dma_start(out=dbg_ln1, in_=ln1t.bitcast(F32))
                    sems.inc("sout", dd, 16)
                    dd = g.dma_start(out=dbg_fm,
                                     in_=ln1fm.rearrange("p c t -> p (c t)")
                                     .bitcast(F32))
                    sems.inc("sout", dd, 16)
                    dd = g.dma_start(out=dbg_cols, in_=cols[:])
                    sems.inc("sout", dd, 16)
                    sems.wait_val(g, "sout", 16 * (NTT + 9))
                else:
                    sems.wait_val(g, "sout", 16 * NTT)
                return
            for fc in range(NFC):
                sems.wait(g, f"s:gsq:{fc}")
                dd = g.dma_start(out=g_d[fc], in_=grng[fc % 4])
                sems.inc("sgst", dd, 16)
            sems.alias("g:phaseC", "sgst")
            if stop_after == 'C':
                if dbg:
                    sems.wait(g, "v:phaseC")
                    for gi, fc in enumerate((0, 64, 127)):
                        dd = g.dma_start(out=dbg_g[gi], in_=g_d[fc])
                        sems.inc("sout", dd, 16)
                    dd = g.dma_start(out=dbg_m[:, 0:8], in_=m1_c[:])
                    sems.inc("sout", dd, 16)
                    dd = g.dma_start(out=dbg_m[:, 8:16], in_=m2_c[:])
                    sems.inc("sout", dd, 16)
                    dd = g.dma_start(out=dbg_m[:, 16:24], in_=amax_x[:])
                    sems.inc("sout", dd, 16)
                    dd = g.dma_start(allow_slow_non_contiguous=True, out=dbg_m[:, 24:25], in_=sup_b[:])
                    sems.inc("sout", dd, 16)
                    dd = g.dma_start(allow_slow_non_contiguous=True, out=dbg_m[:, 25:26], in_=sdn_b[:])
                    sems.inc("sout", dd, 16)
                    dd = g.dma_start(allow_slow_non_contiguous=True, out=dbg_m[:, 26:27], in_=sup_inv[:])
                    sems.inc("sout", dd, 16)
                    dd = g.dma_start(allow_slow_non_contiguous=True, out=dbg_m[:, 27:28], in_=sdn_inv[:])
                    sems.inc("sout", dd, 16)
                    dd = g.dma_start(out=dbg_m[:, 28:36], in_=m1_bc[:, 0:8])
                    sems.inc("sout", dd, 16)
                    sems.wait_val(g, "sout", 16 * 11)
                return
            sems.wait(g, "v:phaseC")
            sems.wait(g, "s:phaseC")
            sems.wait(g, "t:phaseC")
            for idx in range(NFC):
                if idx >= 2:
                    sems.wait(g, f"v:hq:{idx-2}")
                dd = g.dma_start(out=gld[idx % 2], in_=g_d[idx])
                sems.inc("sgld", dd, 16)
            if stop_after == 'D':
                if dbg:
                    sems.wait(g, "v:D:done")
                    for ai, dc in enumerate((0, 1, 16, 31)):
                        dd = g.dma_start(out=dbg_acc[ai], in_=acc[:, dc, :])
                        sems.inc("sout", dd, 16)
                    sems.wait_val(g, "sout", 16 * 4)
                return
            sems.wait(g, "t:D:done")
            sems.wait(g, "v:D:done")
            for tt in range(NTT):
                if tt >= 2:
                    sems.wait(g, f"v:tail:{tt-2}")
                    dd = g.dma_start(out=out_d[tt - 2], in_=outt[tt % 2])
                    sems.inc("sout", dd, 16)
                dd = g.dma_start(out=x2t[tt % 2], in_=x2_d[tt])
                sems.inc("sx2l", dd, 16)
            for tt in (NTT - 2, NTT - 1):
                sems.wait(g, f"v:tail:{tt}")
                dd = g.dma_start(out=out_d[tt], in_=outt[tt % 2])
                sems.inc("sout", dd, 16)
            sems.wait_val(g, "sout", 16 * NTT)

        def prog_vector(v):
            v.memset(ones_f[:], 1.0)
            v.memset(ones_col[:], 1.0)
            sems.wait_val(v, "swq", 16 * 32)
            v.tensor_copy(out=w3[:, :, 0:128], in_=wqsum)
            vsig(v, "v:w3")
            sems.wait(v, "g:aux")
            v.tensor_reduce(out=bcols[:, 0:1].unsqueeze(2),
                            in_=cols[:, 32:64].unsqueeze(1), axis=AX.X,
                            op=ALU.add)
            vsig(v, "v:bcols")
            sems.wait(v, "t:btr")
            v.tensor_copy(out=brow3[0:1, :], in_=ps_misc[0][0:1, 0:384])
            vsig(v, "v:brow")
            sems.wait(v, "t:biasbc")
            v.tensor_copy(out=bias_bc, in_=ps_qkv[0])
            vsig(v, "v:biasbc")

            for tt in range(NTT):
                xs = xt[tt % 2]
                sems.wait_val(v, "sx", 16 * (tt + 1))
                c_mean = cols[:, 8 * tt + 0:8 * tt + 1]
                c_tmp = cols[:, 8 * tt + 1:8 * tt + 2]
                c_var = cols[:, 8 * tt + 2:8 * tt + 3]
                c_rstd = cols[:, 8 * tt + 3:8 * tt + 4]
                c_m2 = cols[:, 8 * tt + 4:8 * tt + 5]

                def stats(var_evt, sq_evt):
                    sems.wait(v, sq_evt)
                    v.tensor_tensor(out=c_tmp, in0=c_tmp, in1=c_m2, op=ALU.add)
                    v.tensor_scalar(out=c_mean, in0=c_mean, scalar1=1.0 / D,
                                    scalar2=None, op0=ALU.mult, op1=ALU.bypass)
                    v.drain()
                    v.scalar_tensor_tensor(out=c_m2, in0=c_mean, scalar=0.0,
                                           in1=c_mean, op0=ALU.add, op1=ALU.mult)
                    v.tensor_scalar(out=c_var, in0=c_tmp, scalar1=1.0 / D,
                                    scalar2=None, op0=ALU.mult, op1=ALU.bypass)
                    v.drain()
                    v.tensor_scalar(out=c_var, in0=c_var, scalar1=c_m2,
                                    scalar2=EPS, op0=ALU.subtract, op1=ALU.add)
                    vsig(v, var_evt)

                stats(f"v:var1:{tt}", f"s:sq1:{tt}")
                sems.wait(v, f"s:rstd1:{tt}")
                for hh in range(2):
                    sl = slice(2048 * hh, 2048 * hh + 2048)
                    v.tensor_scalar(out=ln2h, in0=xs[:, sl], scalar1=c_mean,
                                    scalar2=c_rstd, op0=ALU.subtract, op1=ALU.mult)
                    v.tensor_copy(out=ln1t[:, sl], in_=ln2h)
                vsig(v, f"v:ln1:{tt}")
                for gp in range(2):
                    sems.wait(v, f"t:tr1:{tt}:{gp}")
                    v.tensor_copy(out=ln1fm[:, 16 * gp:16 * gp + 16, :],
                                  in_=ps_tr[gp])
                    vsig(v, f"v:fm1:{tt}:{gp}")
                sems.wait(v, f"t:qkv:{tt}")
                ins = v.tensor_tensor(out=qkvT[:, tt, :], in0=ps_qkv[tt % 2],
                                      in1=bias_bc, op=ALU.add)
                sems.inc_mark("sv", ins, f"v:qkv:{tt}")
                qs = qkvT[:, tt, 0:128]
                ks = qkvT[:, tt, 128:256]
                vs = qkvT[:, tt, 256:384]
                c_aq = cols[:, 8 * tt + 5:8 * tt + 6]
                c_ak = cols[:, 8 * tt + 6:8 * tt + 7]
                c_cs = cols[:, 8 * tt + 7:8 * tt + 8]
                v.tensor_reduce(out=c_aq, in_=qs, axis=AX.X, op=ALU.max,
                                apply_absolute_value=True)
                v.tensor_reduce(out=c_ak, in_=ks, axis=AX.X, op=ALU.max,
                                apply_absolute_value=True)
                v.drain()
                v.scalar_tensor_tensor(out=c_cs, in0=c_aq, scalar=0.0, in1=c_ak,
                                       op0=ALU.add, op1=ALU.mult)
                v.drain()
                v.tensor_scalar(out=c_cs, in0=c_cs, scalar1=-S_ATTN,
                                scalar2=None, op0=ALU.mult, op1=ALU.bypass)
                vsig(v, f"v:attnc:{tt}")
                o_num = attno[:, 0:128]
                o_den = attno[:, 128:256]
                ewss = [ews.rearrange("p (a b) -> p a b", b=128),
                        ews2.rearrange("p (a b) -> p a b", b=128)]

                def sbuild(su):
                    dlo = su * ESUB
                    v.tensor_tensor(
                        out=ewss[su % 2],
                        in0=qs[:, dlo:dlo + ESUB].unsqueeze(2)
                            .broadcast_to([128, ESUB, 128]),
                        in1=ks.unsqueeze(1).broadcast_to([128, ESUB, 128]),
                        op=ALU.mult)
                    vsig(v, f"v:S:{tt}:{su}")

                sbuild(0)
                for su in range(N_ESUB):
                    dlo = su * ESUB
                    e3 = ewss[su % 2]
                    if su + 1 < N_ESUB:
                        sbuild(su + 1)
                    sems.wait(v, f"s:exp:{tt}:{su}")
                    v.tensor_reduce(out=o_den[:, dlo:dlo + ESUB].unsqueeze(2),
                                    in_=e3, axis=AX.X, op=ALU.add)
                    v.tensor_tensor(out=e3, in0=e3,
                                    in1=vs.unsqueeze(1).broadcast_to([128, ESUB, 128]),
                                    op=ALU.mult)
                    ins = v.tensor_reduce(out=o_num[:, dlo:dlo + ESUB].unsqueeze(2),
                                          in_=e3, axis=AX.X, op=ALU.add)
                    sems.inc_mark("sv", ins, f"v:esub:{tt}:{su}")
                v.reciprocal(out=o_den, in_=o_den)
                v.tensor_tensor(out=o_num, in0=o_num, in1=o_den, op=ALU.mult)
                x3 = xs.rearrange("p (a b) -> p a b", b=32)
                v.tensor_tensor(out=x3, in0=x3,
                                in1=o_num.unsqueeze(2).broadcast_to([128, 128, 32]),
                                op=ALU.add)
                vsig(v, f"v:x2:{tt}")
                stats(f"v:var2:{tt}", f"s:sq2:{tt}")
                sems.wait(v, f"s:rstd2:{tt}")
                for hh in range(2):
                    sl = slice(2048 * hh, 2048 * hh + 2048)
                    v.tensor_scalar(out=ln2h, in0=xs[:, sl], scalar1=c_mean,
                                    scalar2=c_rstd, op0=ALU.subtract, op1=ALU.mult)
                    v.tensor_reduce(out=c_tmp if hh == 0 else c_m2, in_=ln2h,
                                    axis=AX.X, op=ALU.max, apply_absolute_value=True)
                v.drain()
                v.tensor_tensor(out=amax_x[:, tt:tt + 1], in0=c_tmp, in1=c_m2,
                                op=ALU.max)
                v.drain()
                v.tensor_scalar(out=c_tmp, in0=amax_x[:, tt:tt + 1], scalar1=EPS,
                                scalar2=None, op0=ALU.max, op1=ALU.bypass)
                v.drain()
                v.reciprocal(out=sxq_c[:, tt:tt + 1], in_=c_tmp)
                v.drain()
                v.tensor_scalar(out=sxq_c[:, tt:tt + 1], in0=sxq_c[:, tt:tt + 1],
                                scalar1=127.0, scalar2=None, op0=ALU.mult,
                                op1=ALU.bypass)
                v.drain()
                for hh in range(2):
                    sl = slice(2048 * hh, 2048 * hh + 2048)
                    v.tensor_scalar(out=ln2h, in0=xs[:, sl], scalar1=c_mean,
                                    scalar2=c_rstd, op0=ALU.subtract, op1=ALU.mult)
                    v.tensor_scalar(out=ln2h, in0=ln2h, scalar1=sxq_c[:, tt:tt + 1],
                                    scalar2=MAGIC, op0=ALU.mult, op1=ALU.add)
                    v.tensor_scalar(out=ln2h, in0=ln2h, scalar1=MAGIC,
                                    scalar2=None, op0=ALU.subtract, op1=ALU.bypass)
                    v.tensor_copy(out=xqt[:, sl], in_=ln2h)
                vsig(v, f"v:xq:{tt}")
                for gp in range(2):
                    sems.wait(v, f"t:tr2:{tt}:{gp}")
                    v.tensor_copy(
                        out=xq_fm[:, 16 * gp:16 * gp + 16, 128 * tt:128 * tt + 128],
                        in_=ps_tr[gp])
                    vsig(v, f"v:fm2:{tt}:{gp}")
                for i in range(32 * tt, 32 * tt + 32):
                    sems.wait_val(v, "supa", 16 * (i + 1))
                    if i >= 2:
                        sems.wait(v, f"s:scup:{i-2}")
                    ins = v.tensor_scalar(out=scst[i % 2], in0=scst[i % 2],
                                          scalar1=-1.0, scalar2=1.0,
                                          op0=ALU.max, op1=ALU.min)
                    sems.inc_mark("sv", ins, f"v:scupc:{i}")
            sems.wait(v, "s:scupall")
            dhi, dlo = dd_tree_cols(v, up_cols, ddwkA)
            ins = v.tensor_copy(out=ddwkA[:, 3071:3072], in_=dhi)
            sems.inc_mark("sv", ins, "v:ddup1")
            sems.wait(v, "t:ddup")
            ins = v.tensor_copy(out=ddrow[0:1, 0:128], in_=ps_misc[0][0:1, 0:128])
            ins = v.tensor_copy(out=ddrow[0:1, 128:256], in_=ps_misc[0][0:1, 128:256])
            v.drain()
            dd_tree_row(v, sc_row[0:1, 0:1])
            v.tensor_scalar(out=sc_row[0:1, 0:1], in0=sc_row[0:1, 0:1],
                            scalar1=1.0 / NW, scalar2=EPS, op0=ALU.mult, op1=ALU.add)
            v.drain()
            ins = v.reciprocal(out=sc_row[0:1, 1:2], in_=sc_row[0:1, 0:1])
            v.drain()
            ins = v.tensor_copy(out=sc_row[0:1, 1:2], in_=sc_row[0:1, 1:2])
            sems.inc_mark("sv", ins, "v:suprow")
            sems.wait(v, "t:supbc")
            ins = v.tensor_copy(out=sup_b[:], in_=ps_misc[1][:, 0:1])
            v.drain()
            v.tensor_scalar(out=sup_inv[:], in0=sup_b[:], scalar1=0.5, scalar2=None,
                            op0=ALU.mult, op1=ALU.bypass)
            v.tensor_scalar(out=nsup2[:], in0=sup_b[:], scalar1=-0.5, scalar2=None,
                            op0=ALU.mult, op1=ALU.bypass)
            v.drain()
            ins = v.tensor_copy(out=bcols[:, 3:4], in_=ones_col[:])
            sems.inc_mark("sv", ins, "v:phaseA")
            if stop_after == 'A':
                return

            v.memset(mrun, -3.0e38)
            for fc in range(NFC):
                for h in range(2):
                    u = 2 * fc + h
                    sems.wait_val(v, "supc", 16 * (u + 1))
                    if u >= 4:
                        sems.wait(v, f"t:uph:{u-4}")
                    v.tensor_scalar(out=isleC, in0=upst[u % 4],
                                    scalar1=nsup2[:, 0:1], scalar2=None,
                                    op0=ALU.is_le, op1=ALU.bypass)
                    v.scalar_tensor_tensor(out=ternu[u % 4], in0=upst[u % 4],
                                           scalar=sup_inv[:, 0:1], in1=isleC,
                                           op0=ALU.is_ge, op1=ALU.subtract)
                    vsig(v, f"v:uC:{u}")
                sems.wait(v, f"s:gsqr:{fc}")
                ins = v.tensor_tensor(out=mrun, in0=mrun, in1=reluws,
                                      op=ALU.max)
                sems.inc_mark("sv", ins, f"v:upev:{fc}")
                for i in (2 * fc, 2 * fc + 1):
                    sems.wait_val(v, "sdnc", 16 * (i + 1))
                    if i >= 2:
                        sems.wait(v, f"s:scdn:{i-2}")
                    ins = v.tensor_scalar(out=scstd[i % 2], in0=scstd[i % 2],
                                          scalar1=-1.0, scalar2=1.0,
                                          op0=ALU.max, op1=ALU.min)
                    sems.inc_mark("sv", ins, f"v:scdnc:{i}")
            sems.wait(v, "s:scdnall")
            dhi, dlo = dd_tree_cols(v, dn_cols, ddwkC)
            ins = v.tensor_copy(out=ddwkC[:, 3071:3072], in_=dhi)
            sems.inc_mark("sv", ins, "v:dddn1")
            sems.wait(v, "t:dddn")
            ins = v.tensor_copy(out=ddrow[0:1, 0:128], in_=ps_misc[0][0:1, 0:128])
            ins = v.tensor_copy(out=ddrow[0:1, 128:256], in_=ps_misc[0][0:1, 128:256])
            v.drain()
            dd_tree_row(v, sc_row[0:1, 2:3])
            v.tensor_scalar(out=sc_row[0:1, 2:3], in0=sc_row[0:1, 2:3],
                            scalar1=1.0 / NW, scalar2=EPS, op0=ALU.mult, op1=ALU.add)
            v.drain()
            ins = v.reciprocal(out=sc_row[0:1, 3:4], in_=sc_row[0:1, 2:3])
            v.drain()
            ins = v.tensor_copy(out=sc_row[0:1, 3:4], in_=sc_row[0:1, 3:4])
            sems.inc_mark("sv", ins, "v:sdnrow")
            sems.wait(v, "t:sdnbc")
            ins = v.tensor_copy(out=sdn_b[:], in_=ps_misc[1][:, 0:1])
            v.drain()
            v.tensor_scalar(out=sdn_inv[:], in0=sdn_b[:], scalar1=0.5, scalar2=None,
                            op0=ALU.mult, op1=ALU.bypass)
            v.tensor_scalar(out=nsdn2[:], in0=sdn_b[:], scalar1=-0.5, scalar2=None,
                            op0=ALU.mult, op1=ALU.bypass)
            v.drain()
            ins = v.tensor_copy(out=bcols[:, 3:4], in_=ones_col[:])
            sems.inc_mark("sv", ins, "v:sdnb")
            for tt in range(NTT):
                sems.wait(v, f"t:trm:{tt}")
                c_g = cols[:, 8 * tt + 0:8 * tt + 1]
                c_c1 = cols[:, 8 * tt + 1:8 * tt + 2]
                c_ah = cols[:, 8 * tt + 2:8 * tt + 3]
                c_sh = cols[:, 8 * tt + 3:8 * tt + 4]
                v.tensor_reduce(out=c_g, in_=ps_misc[tt % 2][:, 0:128], axis=AX.X,
                                op=ALU.max)
                v.drain()
                v.tensor_scalar(out=c_g, in0=c_g, scalar1=0.0, scalar2=None,
                                op0=ALU.max, op1=ALU.bypass)
                v.tensor_scalar(out=c_c1, in0=amax_x[:, tt:tt + 1],
                                scalar1=sup_b[:, 0:1], scalar2=1.0 / 127.0,
                                op0=ALU.mult, op1=ALU.mult)
                v.drain()
                v.scalar_tensor_tensor(out=c_g, in0=c_g, scalar=0.0, in1=c_g,
                                       op0=ALU.add, op1=ALU.mult)
                v.scalar_tensor_tensor(out=c_c1, in0=c_c1, scalar=0.0, in1=c_c1,
                                       op0=ALU.add, op1=ALU.mult)
                v.drain()
                v.scalar_tensor_tensor(out=c_ah, in0=c_g, scalar=0.0, in1=c_c1,
                                       op0=ALU.add, op1=ALU.mult)
                v.drain()
                v.tensor_scalar(out=c_ah, in0=c_ah, scalar1=EPS, scalar2=None,
                                op0=ALU.max, op1=ALU.bypass)
                v.drain()
                v.reciprocal(out=c_sh, in_=c_ah)
                v.drain()
                v.tensor_scalar(out=c_sh, in0=c_sh, scalar1=127.0, scalar2=None,
                                op0=ALU.mult, op1=ALU.bypass)
                v.drain()
                v.scalar_tensor_tensor(out=m1_c[:, tt:tt + 1], in0=c_c1,
                                       scalar=0.0, in1=c_sh,
                                       op0=ALU.add, op1=ALU.mult)
                vsig(v, f"v:m1c:{tt}")
                v.tensor_scalar(out=m2_c[:, tt:tt + 1], in0=c_ah,
                                scalar1=1.0 / 127.0, scalar2=sdn_b[:, 0:1],
                                op0=ALU.mult, op1=ALU.mult)
                sems.wait(v, f"t:trm1:{tt}")
                v.tensor_copy(out=m1_row[0:1, 128 * tt:128 * tt + 128],
                              in_=ps_m1[tt % 2][0:1, 0:128])
                vsig(v, f"v:m1r:{tt}")
            sems.wait(v, "t:m1bc")
            ins = v.tensor_copy(out=m1_bc[:, 0:512], in_=ps_up[0][:, 0:512])
            ins = v.tensor_copy(out=m1_bc[:, 512:1024], in_=ps_up[1][:, 0:512])
            sems.inc_mark("sv", ins, "v:phaseC")
            if stop_after == 'C':
                return

            for fsb in range(NFSB):
                for jj in range(FC_PER_FSB):
                    idx = fsb * FC_PER_FSB + jj
                    sems.wait_val(v, "sgld", 16 * (idx + 1))
                    if fsb > 0 and jj == 0:
                        sems.wait(v, f"t:dn:{fsb * NDG - 1}")
                    ins = v.tensor_tensor(out=gld[idx % 2], in0=gld[idx % 2],
                                          in1=m1_bc[:], op=ALU.mult)
                    v.tensor_scalar(out=gld[idx % 2], in0=gld[idx % 2],
                                    scalar1=MAGIC, scalar2=MAGIC,
                                    op0=ALU.add, op1=ALU.subtract)
                    v.tensor_copy(out=hq[:, jj, :], in_=gld[idx % 2])
                    vsig(v, f"v:hq:{idx}")
                for dg in range(NDG):
                    j = fsb * NDG + dg
                    for qt in range(4):
                        jj2 = j * 4 + qt
                        sems.wait_val(v, "sdnd", 16 * (jj2 + 1))
                        if jj2 >= 4:
                            sems.wait(v, f"t:dnu:{jj2-4}")
                        for qq in range(2):
                            sq = slice(512 * qq, 512 * qq + 512)
                            v.tensor_scalar(out=isleD, in0=dnst[jj2 % 4][:, sq],
                                            scalar1=nsdn2[:, 0:1], scalar2=None,
                                            op0=ALU.is_le, op1=ALU.bypass)
                            v.scalar_tensor_tensor(
                                out=ternd[jj2 % 4][:, sq],
                                in0=dnst[jj2 % 4][:, sq],
                                scalar=sdn_inv[:, 0:1], in1=isleD,
                                op0=ALU.is_ge, op1=ALU.subtract)
                        vsig(v, f"v:dC:{jj2}")
                    sems.wait(v, f"t:dn:{j}")
                    for dcl in range(2):
                        dst = acc[:, 2 * dg + dcl, :]
                        pp = ps_dn[(dg % 2) * 2 + dcl]
                        if fsb == 0:
                            ins = v.tensor_copy(out=dst, in_=pp)
                        else:
                            ins = v.tensor_tensor(out=dst, in0=dst, in1=pp,
                                                  op=ALU.add)
                    sems.inc_mark("sv", ins, f"v:dev:{j}")
            vsig(v, "v:D:done:sig")
            sems.mark("v:D:done", "sv")
            if stop_after == 'D':
                return
            for tt in range(NTT):
                sems.wait_val(v, "sx2l", 16 * (tt + 1))
                if tt >= 2:
                    sems.wait_val(v, "sout", 16 * (tt - 1))
                for gp in range(8):
                    sems.wait(v, f"t:tlt:{tt}:{gp}")
                    ins = v.scalar_tensor_tensor(
                        out=outt[tt % 2][:, 512 * gp:512 * gp + 512],
                        in0=ps_tl[gp % 2].rearrange("p c t -> p (c t)"),
                        scalar=m2_c[:, tt:tt + 1],
                        in1=x2t[tt % 2][:, 512 * gp:512 * gp + 512],
                        op0=ALU.mult, op1=ALU.add)
                    sems.inc_mark("sv", ins, f"v:tailg:{tt}:{gp}")
                vsig(v, f"v:tailsig:{tt}")
                sems.mark(f"v:tail:{tt}", "sv")

        def prog_scalar(s):
            scup = 0
            for tt in range(NTT):
                c_var = cols[:, 8 * tt + 2:8 * tt + 3]
                c_rstd = cols[:, 8 * tt + 3:8 * tt + 4]
                c_cs = cols[:, 8 * tt + 7:8 * tt + 8]
                c_mean_s = cols[:, 8 * tt + 0:8 * tt + 1]
                c_s1 = cols[:, 8 * tt + 1:8 * tt + 2]
                c_s2 = cols[:, 8 * tt + 4:8 * tt + 5]
                xs = xt[tt % 2]
                sems.wait_val(s, "sx", 16 * (tt + 1))
                if tt >= 1:
                    sems.wait(s, f"v:xq:{tt-1}")
                s.activation(out=xs, in_=xs, func=AF.Identity, accum_out=c_mean_s)
                s.activation(out=ln2h, in_=xs[:, 0:2048], func=AF.Square,
                             accum_out=c_s1)
                s.activation(out=ln2h, in_=xs[:, 2048:4096], func=AF.Square,
                             accum_out=c_s2)
                s.drain()
                ins = s.activation(out=bcols[:, 3:4], in_=bcols[:, 3:4], func=AF.Abs)
                sems.inc_mark("ss", ins, f"s:sq1:{tt}")
                sems.wait(s, f"v:var1:{tt}")
                s.activation(out=c_rstd, in_=c_var, func=AF.Ln)
                s.drain()
                s.activation(out=c_rstd, in_=c_rstd, func=AF.Exp, scale=-0.5)
                ssig(s, f"s:rstd1:{tt}")
                for su in range(N_ESUB):
                    sems.wait(s, f"v:S:{tt}:{su}")
                    tgt = ews if su % 2 == 0 else ews2
                    s.activation(out=tgt, in_=tgt, func=AF.Exp,
                                 scale=S_ATTN, bias=c_cs)
                    ssig(s, f"s:exp:{tt}:{su}")
                sems.wait(s, f"v:x2:{tt}")
                s.activation(out=xs, in_=xs, func=AF.Identity, accum_out=c_mean_s)
                s.activation(out=ln2h, in_=xs[:, 0:2048], func=AF.Square,
                             accum_out=c_s1)
                s.activation(out=ln2h, in_=xs[:, 2048:4096], func=AF.Square,
                             accum_out=c_s2)
                s.drain()
                ins = s.activation(out=bcols[:, 3:4], in_=bcols[:, 3:4], func=AF.Abs)
                sems.inc_mark("ss", ins, f"s:sq2:{tt}")
                sems.wait(s, f"v:var2:{tt}")
                s.activation(out=c_rstd, in_=c_var, func=AF.Ln)
                s.drain()
                s.activation(out=c_rstd, in_=c_rstd, func=AF.Exp, scale=-0.5)
                ssig(s, f"s:rstd2:{tt}")
                for i in range(32 * tt, 32 * tt + 32):
                    sems.wait(s, f"v:scupc:{i}")
                    for q in range(4):
                        ins = s.activation(out=scst[i % 2][:, 512 * q:512 * q + 512],
                                           in_=scst[i % 2][:, 512 * q:512 * q + 512],
                                           func=AF.Abs,
                                           accum_out=up_cols[:, 4 * i + q:4 * i + q + 1])
                    sems.inc_mark("ss", ins, f"s:scup:{i}")
                    scup += 1
            s.drain()
            ins = s.activation(out=bcols[:, 3:4], in_=bcols[:, 3:4], func=AF.Abs)
            sems.inc_mark("ss", ins, "s:scupall")
            sems.alias("s:phaseA", "ss")
            if stop_after == 'A':
                return
            sems.wait(s, "t:qkv:7")
            for fc in range(NFC):
                sems.wait(s, f"t:up:{fc}")
                if fc >= 1:
                    sems.wait(s, f"v:upev:{fc-1}")
                if fc >= 4:
                    sems.wait_val(s, "sgst", 16 * (fc - 3))
                s.activation(out=reluws, in_=ps_up[fc % 2], func=AF.Relu)
                ssig(s, f"s:gsqr:{fc}")
                s.activation(out=grng[fc % 4], in_=reluws, func=AF.Square)
                ssig(s, f"s:gsq:{fc}")
                for i in (2 * fc, 2 * fc + 1):
                    if i >= 2:
                        sems.wait(s, f"s:scdn:{i-2}")
                    jch = i // 2
                    dd = s.dma_start(
                        out=scstd[i % 2],
                        in_=wdn_d[jch // NDG, jch % NDG]
                        [:, (i % 2) * SCAN_CH:(i % 2) * SCAN_CH + SCAN_CH])
                    sems.inc("sdnc", dd, 16)
                for i in (2 * fc, 2 * fc + 1):
                    sems.wait(s, f"v:scdnc:{i}")
                    for q in range(4):
                        ins = s.activation(out=scstd[i % 2][:, 512 * q:512 * q + 512],
                                           in_=scstd[i % 2][:, 512 * q:512 * q + 512],
                                           func=AF.Abs,
                                           accum_out=dn_cols[:, 4 * i + q:4 * i + q + 1])
                    sems.inc_mark("ss", ins, f"s:scdn:{i}")
            s.drain()
            ins = s.activation(out=bcols[:, 3:4], in_=bcols[:, 3:4], func=AF.Abs)
            sems.inc_mark("ss", ins, "s:scdnall")
            sems.alias("s:phaseC", "ss")
            if stop_after == 'C':
                return

        def prog_tensor(t):
            sems.wait(t, "v:bcols")
            sems.wait(t, "g:ident")
            for bi in range(3):
                ins = t.transpose(out=ps_misc[0][0:1, 128 * bi:128 * bi + 128],
                                  in_=bcols[:, bi:bi + 1], identity=id_f32[:])
            sems.inc_mark("st", ins, "t:btr")
            sems.wait(t, "v:brow")
            ins = t.matmul(out=ps_qkv[0], lhsT=ones_f[:], rhs=brow3[0:1, :],
                           start=True, stop=True)
            sems.inc_mark("st", ins, "t:biasbc")
            for tt in range(NTT):
                sems.wait(t, f"v:ln1:{tt}")
                for gp in range(2):
                    if tt > 0:
                        sems.wait(t, f"v:fm2:{tt-1}:{gp}")
                    for k in range(16):
                        cidx = 16 * gp + k
                        ins = t.transpose(out=ps_tr[gp][:, k, :],
                                          in_=ln1t[:, 128 * cidx:128 * cidx + 128],
                                          identity=id_bf[:])
                    sems.inc_mark("st", ins, f"t:tr1:{tt}:{gp}")
                sems.wait(t, f"v:fm1:{tt}:1")
                if tt == 0:
                    sems.wait(t, "v:biasbc")
                else:
                    sems.wait(t, f"v:qkv:{tt-1}")
                for dc in range(NDC):
                    ins = t.matmul(out=ps_qkv[tt % 2], lhsT=ln1fm[:, dc, :],
                                   rhs=w3[:, dc, :], start=(dc == 0),
                                   stop=(dc == NDC - 1))
                sems.inc_mark("st", ins, f"t:qkv:{tt}")
                sems.wait(t, f"v:xq:{tt}")
                for gp in range(2):
                    sems.wait(t, f"v:fm1:{tt}:{gp}")
                    for k in range(16):
                        cidx = 16 * gp + k
                        ins = t.transpose(out=ps_tr[gp][:, k, :],
                                          in_=xqt[:, 128 * cidx:128 * cidx + 128],
                                          identity=id_bf[:])
                    sems.inc_mark("st", ins, f"t:tr2:{tt}:{gp}")
            sems.wait(t, "v:ddup1")
            ins = t.transpose(out=ps_misc[0][0:1, 0:128],
                              in_=ddwkA[:, 1024:1025], identity=id_f32[:])
            ins = t.transpose(out=ps_misc[0][0:1, 128:256],
                              in_=ddwkA[:, 1536:1537], identity=id_f32[:])
            sems.inc_mark("st", ins, "t:ddup")
            sems.wait(t, "v:suprow")
            ins = t.matmul(out=ps_misc[1][:, 0:2], lhsT=ones_f[:],
                           rhs=sc_row[0:1, 0:2], start=True, stop=True)
            sems.inc_mark("st", ins, "t:supbc")
            sems.alias("t:phaseA", "st")
            if stop_after == 'A':
                return
            for fc in range(NFC):
                if fc >= 2:
                    sems.wait(t, f"s:gsqr:{fc-2}")
                for h in range(2):
                    u = 2 * fc + h
                    sems.wait(t, f"v:uC:{u}")
                    tern3 = ternu[u % 4].rearrange("p (c m) -> p c m", m=128)
                    for dcl in range(16):
                        dc = 16 * h + dcl
                        for th in range(2):
                            ins = t.matmul(
                                out=ps_up[fc % 2][:, 512 * th:512 * th + 512],
                                lhsT=tern3[:, dcl, :],
                                rhs=xq_fm[:, dc, 512 * th:512 * th + 512],
                                start=(dc == 0), stop=(dc == NDC - 1))
                    sems.inc_mark("st", ins, f"t:uph:{u}")
                sems.alias(f"t:up:{fc}", "st")
            sems.wait(t, "v:dddn1")
            ins = t.transpose(out=ps_misc[0][0:1, 0:128],
                              in_=ddwkC[:, 1024:1025], identity=id_f32[:])
            ins = t.transpose(out=ps_misc[0][0:1, 128:256],
                              in_=ddwkC[:, 1536:1537], identity=id_f32[:])
            sems.inc_mark("st", ins, "t:dddn")
            sems.wait(t, "v:sdnrow")
            ins = t.matmul(out=ps_misc[1][:, 0:2], lhsT=ones_f[:],
                           rhs=sc_row[0:1, 2:4], start=True, stop=True)
            sems.inc_mark("st", ins, "t:sdnbc")
            for tt in range(NTT):
                if tt == 0:
                    sems.wait(t, f"v:upev:{NFC-1}")
                    sems.wait(t, "v:sdnb")
                if tt >= 2:
                    sems.wait(t, f"v:m1c:{tt-2}")
                ins = t.transpose(out=ps_misc[tt % 2][:, 0:128],
                                  in_=mrun[:, 128 * tt:128 * tt + 128],
                                  identity=id_f32[:])
                sems.inc_mark("st", ins, f"t:trm:{tt}")
                sems.wait(t, f"v:m1c:{tt}")
                ins = t.transpose(out=ps_m1[tt % 2][0:1, 0:128],
                                  in_=m1_c[:, tt:tt + 1], identity=id_f32[:])
                sems.inc_mark("st", ins, f"t:trm1:{tt}")
            sems.wait(t, "v:m1r:7")
            ins = t.matmul(out=ps_up[0][:, 0:512], lhsT=ones_f[:],
                           rhs=m1_row[0:1, 0:512], start=True, stop=True)
            ins = t.matmul(out=ps_up[1][:, 0:512], lhsT=ones_f[:],
                           rhs=m1_row[0:1, 512:1024], start=True, stop=True)
            sems.inc_mark("st", ins, "t:m1bc")
            sems.alias("t:phaseC", "st")
            if stop_after == 'C':
                return
            for j in range(NFSB * NDG):
                fsb, dg = j // NDG, j % NDG
                if dg == 0:
                    sems.wait(t, f"v:hq:{fsb * FC_PER_FSB + FC_PER_FSB - 1}")
                if j >= 2:
                    sems.wait(t, f"v:dev:{j-2}")
                for qt in range(4):
                    jj2 = j * 4 + qt
                    sems.wait(t, f"v:dC:{jj2}")
                    tern4 = ternd[jj2 % 4].rearrange(
                        "p (a c m) -> p a c m", c=2, m=128)
                    for fcl in range(4):
                        fcg = qt * 4 + fcl
                        for dcl in range(2):
                            for th in range(2):
                                ins = t.matmul(
                                    out=ps_dn[(dg % 2) * 2 + dcl][:, 512 * th:512 * th + 512],
                                    lhsT=tern4[:, fcl, dcl, :],
                                    rhs=hq[:, fcg, 512 * th:512 * th + 512],
                                    start=(fcg == 0), stop=(fcg == FC_PER_FSB - 1))
                    sems.inc_mark("st", ins, f"t:dnu:{jj2}")
                sems.alias(f"t:dn:{j}", "st")
            sems.alias("t:D:done", "st")
            if stop_after == 'D':
                return
            for tt in range(NTT):
                for gp in range(8):
                    if tt == 0 and gp <= 1:
                        sems.wait(t, "v:D:done")
                    elif gp >= 2:
                        sems.wait(t, f"v:tailg:{tt}:{gp-2}")
                    else:
                        sems.wait(t, f"v:tailg:{tt-1}:{6+gp}")
                    for k in range(4):
                        ins = t.transpose(
                            out=ps_tl[gp % 2][:, k, :],
                            in_=acc[:, 4 * gp + k, 128 * tt:128 * tt + 128],
                            identity=id_f32[:])
                    sems.inc_mark("st", ins, f"t:tlt:{tt}:{gp}")

        for name in list(sems.cnt):
            sems.handles[name] = E(nc.semaphore(name))

        dry = DryEng()
        sems.dry = True
        import os
        if os.environ.get("KERNEL_TRACE_SCHED"):
            sems.trace = {n: [] for n in
                          ("sync", "gpsimd", "vector", "scalar", "tensor")}
        for nm, f in (("sync", prog_sync), ("gpsimd", prog_gpsimd),
                      ("vector", prog_vector), ("scalar", prog_scalar),
                      ("tensor", prog_tensor)):
            sems.cur = nm
            f(dry)
        if sems.trace is not None:
            import json
            with open("/tmp/sched_trace.json", "w") as fh:
                json.dump({"trace": sems.trace,
                           "events": {k: list(v) for k, v in sems.events.items()}},
                          fh)
        final_counts = dict(sems.cnt)
        sems.dry = False
        for k in sems.cnt:
            sems.cnt[k] = 0

        block = E(nc.Block())

        @block.sync
        def _(sy):
            prog_sync(sy)

        @block.gpsimd
        def _(g):
            prog_gpsimd(g)

        @block.vector
        def _(v):
            prog_vector(v)

        @block.scalar
        def _(s):
            prog_scalar(s)

        @block.tensor
        def _(t):
            prog_tensor(t)

        for k, vv in final_counts.items():
            assert sems.cnt[k] == vv, (k, sems.cnt[k], vv)

    return nc


def _relayout(Wq, Wk, Wv, w_up, w_down):
    wq = np.ascontiguousarray(np.asarray(Wq, dtype=np.float32).transpose(2, 0, 1))
    wk = np.ascontiguousarray(np.asarray(Wk, dtype=np.float32).reshape(D, 128)
                              .reshape(NDC, 128, 128))
    wv = np.ascontiguousarray(np.asarray(Wv, dtype=np.float32).reshape(D, 128)
                              .reshape(NDC, 128, 128))
    wup = np.asarray(w_up, dtype=np.float32)
    wup_arr = np.ascontiguousarray(
        wup.reshape(NFC, 128, NDC, 128).transpose(0, 3, 2, 1)).reshape(NFC, 128, D)
    wdn = np.asarray(w_down, dtype=np.float32)
    wdn_arr = np.ascontiguousarray(
        wdn.reshape(NDG, 2, 128, NFSB, FC_PER_FSB, 128)
        .transpose(3, 0, 5, 4, 1, 2)).reshape(NFSB, NDG, 128, D)
    return wq, wk, wv, wup_arr, wdn_arr


def kernel(L0_attn_x, L0_attn_Wq, L0_attn_bq, L0_attn_Wk, L0_attn_bk,
           L0_attn_Wv, L0_attn_bv, L0_ffn_w_up, L0_ffn_w_down):
    x = np.asarray(L0_attn_x, dtype=np.float32).reshape(N_CORES * T, D)
    wq, wk, wv, wup_arr, wdn_arr = _relayout(
        L0_attn_Wq, L0_attn_Wk, L0_attn_Wv, L0_ffn_w_up, L0_ffn_w_down)
    bq = np.ascontiguousarray(np.asarray(L0_attn_bq, dtype=np.float32))
    bk = np.ascontiguousarray(np.asarray(L0_attn_bk, dtype=np.float32))
    bv = np.ascontiguousarray(np.asarray(L0_attn_bv, dtype=np.float32))

    nc = build_nc()
    in_maps = []
    for c in range(N_CORES):
        shard = np.ascontiguousarray(x[c * T:(c + 1) * T].reshape(NTT, 128, D))
        in_maps.append({"x": shard, "wq": wq, "wk": wk, "wv": wv,
                        "bq": bq, "bk": bk, "bv": bv,
                        "wup": wup_arr, "wdn": wdn_arr})
    res = run_bass_kernel_spmd(nc, in_maps, core_ids=list(range(N_CORES)))
    outs = [np.asarray(r["out"]).reshape(T, D) for r in res.results]
    return np.concatenate(outs, axis=0).reshape(4, 2048, D)


if __name__ == "__main__":
    build_nc()
    print("build ok")
